# version 20
# baseline (speedup 1.0000x reference)
import contextlib
import numpy as np
import ml_dtypes
import concourse.bass as bass
import concourse.mybir as mybir
from concourse.bass_utils import run_bass_kernel_spmd
from concourse.bass_core_rust import IndirectOffsetOnAxis

F32 = mybir.dt.float32
BF16 = mybir.dt.bfloat16
I32 = mybir.dt.int32
AF = mybir.ActivationFunctionType
ALU = mybir.AluOpType
AX = mybir.AxisListType

B_LOC = 2
T = 2048
D = 1024
ROWS = B_LOC * T
NTILE = ROWS // 128

ENGINES = ("pe", "act", "dve", "pool", "sp")
SECTION = {"pe": "tensor", "act": "scalar", "dve": "vector", "pool": "gpsimd", "sp": "sync"}


DMA_ROT = 4


class Prog:

    def __init__(self, nc):
        self.nc = nc
        self.streams = list(ENGINES) + [f"{e}.d{r}" for e in ("pool", "sp", "act", "dve")
                                        for r in range(DMA_ROT)]
        self.cnt = {s: 0 for s in self.streams}
        self.dmacnt = {e: 0 for e in ("pool", "sp", "act", "dve")}
        self.q = {e: [] for e in ENGINES}
        self.writer = {}
        self.readers = {}
        self.waited = {}
        self.sems = {}

    def _add_wait(self, waits, eng, dep):
        pstream, pcnt = dep
        if pstream == eng == "pe":
            return
        if self.waited.get((eng, pstream), 0) >= pcnt:
            return
        waits[pstream] = max(waits.get(pstream, 0), pcnt)

    def op(self, eng, emit, reads=(), writes=(), inc=1, tag="", cc=False):
        is_dma = inc == 16
        waits = {}
        pend = getattr(self, "pending", {}).pop(eng, None)
        if pend:
            for pstream, v in pend.items():
                if pstream != eng and v > 0:
                    self._add_wait(waits, eng, (pstream, v))
        if cc:
            ccn = getattr(self, "cccnt", 0)
            self.cccnt = ccn + 1
            stream = "pool.cc"
            if ccn > 0:
                self._add_wait(waits, eng, (stream, ccn))
            for k in reads:
                d = self.writer.get(k)
                if d is not None:
                    self._add_wait(waits, eng, d)
            for k in writes:
                d = self.writer.get(k)
                if d is not None:
                    self._add_wait(waits, eng, d)
                for pstream, pcnt in self.readers.get(k, {}).items():
                    self._add_wait(waits, eng, (pstream, pcnt))
            for pstream, v in waits.items():
                self.waited[(eng, pstream)] = max(self.waited.get((eng, pstream), 0), v)
            self.cnt[stream] = self.cnt.get(stream, 0) + 1
            newcnt = self.cnt[stream]
            self.q[eng].append((waits, emit, stream, 1, tag))
            for k in writes:
                self.writer[k] = (stream, newcnt)
                self.readers[k] = {}
            for k in reads:
                self.readers.setdefault(k, {})[stream] = newcnt
            return newcnt
        if is_dma:
            qn = self.dmacnt[eng]
            self.dmacnt[eng] = qn + 1
            stream = f"{eng}.d{qn % DMA_ROT}"
            prev = (qn // DMA_ROT) * 16
            if prev > 0:
                self._add_wait(waits, eng, (stream, prev))
        else:
            stream = eng
        for k in reads:
            d = self.writer.get(k)
            if d is not None:
                self._add_wait(waits, eng, d)
        for k in writes:
            d = self.writer.get(k)
            if d is not None:
                self._add_wait(waits, eng, d)
            for pstream, pcnt in self.readers.get(k, {}).items():
                self._add_wait(waits, eng, (pstream, pcnt))
        for pstream, v in waits.items():
            self.waited[(eng, pstream)] = max(self.waited.get((eng, pstream), 0), v)
        self.cnt[stream] += inc
        newcnt = self.cnt[stream]
        self.q[eng].append((waits, emit, stream, inc, tag))
        for k in writes:
            self.writer[k] = (stream, newcnt)
            self.readers[k] = {}
        for k in reads:
            self.readers.setdefault(k, {})[stream] = newcnt
        return newcnt

    def barrier(self):
        snap = {s: c for s, c in self.cnt.items() if c > 0 and s != "pool.cc"}
        self.pending = {e: dict(snap) for e in ENGINES}

    def alloc_sems(self, stack):
        for s in self.streams + ["pool.cc"]:
            if s not in self.sems:
                self.sems[s] = stack.enter_context(
                    self.nc.semaphore("sem_" + s.replace(".", "_")))

    def flush(self, block):
        for e in ENGINES:
            if not self.q[e]:
                continue
            q = self.q[e]
            self.q[e] = []

            def f(engobj, q=q, e=e):
                for waits, emit, stream, inc, tag in q:
                    for pstream, v in waits.items():
                        engobj.wait_ge(self.sems[pstream], v)
                    ins = emit(engobj)
                    ins.then_inc(self.sems[stream], inc)
            getattr(block, SECTION[e])(f)

    def finish(self, block):
        final = dict(self.cnt)

        def mk(e):
            def f(engobj):
                for s, c in final.items():
                    if c > 0 and (s == e or s.startswith(e + ".")):
                        engobj.wait_ge(self.sems[s], c)
                if e == "sp":
                    for s, c in final.items():
                        if c > 0 and not (s == "sp" or s.startswith("sp.")):
                            engobj.wait_ge(self.sems[s], c)
            return f

        for e in ENGINES:
            getattr(block, SECTION[e])(mk(e))

    def build(self, stack):
        nc = self.nc
        self.alloc_sems(stack)
        block = stack.enter_context(nc.Block())
        self.flush(block)
        self.finish(block)



def emit_weight_cast(p, w_dram, scr_dram, name):
    p.op("pool",
         lambda e: e.dma_start(out=scr_dram[:, :], in_=w_dram[:, :]),
         reads=(), writes=(("wscr", name),), inc=16, tag=f"wcast {name}")


def emit_weight_load_T(p, w_dram, scr_dram, wt_sb, name, dout, din, tmps=None,
                       tmp_keys=None, cast=True, cast_name=None):
    ncol = din // 128
    cn = cast_name or name
    if cast:
        emit_weight_cast(p, w_dram, scr_dram, cn)
    for j in range(ncol):
        p.op("sp",
             lambda e, j=j: e.dma_start_transpose(
                 wt_sb[:, j, :],
                 scr_dram[:, j * 128:(j + 1) * 128]),
             reads=(("wscr", cn),), writes=(("WT", name, j),), inc=16, tag=f"wT {name} {j}")


def emit_ln_stats(p, x_ap, tile, sum_sb, ssq_sb, col, scr_sb, xkey):
    p.op("act",
         lambda e: e.activation(out=scr_sb[:], in_=x_ap, func=AF.Square,
                                accum_out=ssq_sb[:, col:col + 1]),
         reads=(xkey,), writes=(("lnscr",), ("ssq", id(ssq_sb), col)), tag=f"ln sq {tile}")
    p.op("act",
         lambda e: e.activation(out=scr_sb[:], in_=x_ap, func=AF.Identity,
                                accum_out=sum_sb[:, col:col + 1]),
         reads=(xkey,), writes=(("lnscr",), ("lsum", id(sum_sb), col)), tag=f"ln id {tile}")


def emit_ln_finalize2(p, sum_sb, ssq_sb, rstd_sb, bias_sb, t0_sb, t1_sb, cols, n, eps=1e-5):
    c0, c1 = cols
    rsum = sum_sb[:, c0:c1]
    rssq = ssq_sb[:, c0:c1]
    rk = tuple(("lsum", id(sum_sb), c) for c in range(c0, c1))
    qk = tuple(("ssq", id(ssq_sb), c) for c in range(c0, c1))
    t0 = t0_sb[:, c0:c1]
    t1 = t1_sb[:, c0:c1]
    rstd = rstd_sb[:, c0:c1]
    bias = bias_sb[:, c0:c1]
    K = id(rstd_sb)
    kt0 = (("lnt0", id(t0_sb), c0),)
    kt1 = (("lnt1", id(t0_sb), c0),)
    krstd = tuple(("rstd", K, c) for c in range(c0, c1))
    kbias = tuple(("lnbias", K, c) for c in range(c0, c1))
    inv = 1.0 / n
    p.op("dve", lambda e: e.tensor_scalar(t0, rsum, inv, None, ALU.mult),
         reads=rk, writes=kt0)
    p.op("dve", lambda e: e.tensor_scalar(t1, rssq, inv, None, ALU.mult),
         reads=qk, writes=kt1)
    p.op("dve", lambda e: e.tensor_tensor(rstd, t0, t0, ALU.mult),
         reads=kt0, writes=krstd)
    p.op("dve", lambda e: e.tensor_tensor(t1, t1, rstd, ALU.subtract),
         reads=kt1 + krstd, writes=kt1)
    p.op("dve", lambda e: e.tensor_scalar(t1, t1, float(eps), None, ALU.add),
         reads=kt1, writes=kt1)
    p.op("act", lambda e: e.activation(out=rstd, in_=t1, func=AF.Sqrt),
         reads=kt1, writes=krstd)
    p.op("dve", lambda e: e.reciprocal(rstd, rstd), reads=krstd, writes=krstd)
    p.op("dve", lambda e: e.scalar_tensor_tensor(bias, t0, -1.0, rstd, ALU.mult, ALU.mult),
         reads=kt0 + krstd, writes=kbias)



def emit_block2(p, nc, bufs, dram):
    X = bufs["X"]
    yfm = bufs["yfm"]
    xr = bufs["xr"]
    xk = bufs["xk"]
    kact = bufs["kact"]
    scr = bufs["scr"]
    ytm = bufs["ytm"]
    krelu = bufs["krelu"]
    rgate = bufs["rgate"]
    fblk = bufs["fblk"]
    dif = bufs["dif"]
    ident = bufs["ident"]
    psum = bufs["psum"]
    psumT = bufs["psumT"]
    stats = bufs["stats"]
    wrT, wkT, wvT = bufs["wrT"], bufs["wkT"], bufs["wvT"]
    mur, muk = bufs["mur"], bufs["muk"]

    sum_a, ssq_a = stats["sum_a"], stats["ssq_a"]
    rstd_a, bias_a = stats["rstd_a"], stats["bias_a"]
    sum_b, ssq_b = stats["sum_b"], stats["ssq_b"]
    rstd_b, bias_b = stats["rstd_b"], stats["bias_b"]
    t0s, t1s = stats["t0"], stats["t1"]

    NB = NTILE // 4
    for blk in range(NB):
        for tt in range(4):
            tile = blk * 4 + tt
            xa = X[:, tile, :]
            emit_ln_stats(p, xa, tile, sum_a, ssq_a, tile, scr, ("X", tile))
        emit_ln_finalize2(p, sum_a, ssq_a, rstd_a, bias_a, t0s, t1s,
                          (blk * 4, blk * 4 + 4), D)
        if blk % (NB // B_LOC) == 0:
            p.op("dve", lambda e: e.memset(yfm[:, :, 0:1], 0.0),
                 reads=(), writes=(("yfmc",),), tag="carry0")
        else:
            p.op("dve", lambda e: e.tensor_copy(yfm[:, :, 0:1], yfm[:, :, 512:513]),
                 reads=(("yfm", 7, 3),) + tuple(("yfm", j, 3) for j in range(8)),
                 writes=(("yfmc",),), tag="carry")
        for tt in range(4):
            tile = blk * 4 + tt
            xa = X[:, tile, :]
            yt = ytm[tt % 2]
            p.op("act",
                 lambda e, xa=xa, yt=yt, tile=tile: e.activation(
                     out=yt[:], in_=xa, func=AF.Identity,
                     scale=rstd_a[:, tile:tile + 1], bias=bias_a[:, tile:tile + 1]),
                 reads=(("X", tile), ("rstd", id(rstd_a), tile), ("lnbias", id(rstd_a), tile)),
                 writes=(("ytm", tt % 2),), tag=f"ln y {tile}")
            for j in range(8):
                p.op("pe",
                     lambda e, yt=yt, j=j: e.transpose(psumT[:], yt[:, j * 128:(j + 1) * 128], ident[:]),
                     reads=(("ytm", tt % 2), ("ident",)), writes=(("psumT",),), tag=f"T {tile} {j}")
                p.op("dve",
                     lambda e, j=j, tt=tt: e.tensor_copy(
                         yfm[:, j, 1 + tt * 128:1 + tt * 128 + 128], psumT[:]),
                     reads=(("psumT",),), writes=(("yfm", j, tt),), tag=f"Tc {tile} {j}")
        for j in range(8):
            ykeys = tuple(("yfm", j, tt) for tt in range(4)) + (("yfmc",),)
            d = dif[j % 2]
            p.op("dve",
                 lambda e, j=j, d=d: e.tensor_tensor(
                     d[:], yfm[:, j, 1:513], yfm[:, j, 0:512], ALU.subtract),
                 reads=ykeys, writes=(("dif", j % 2),), tag=f"dif {j}")
            p.op("dve",
                 lambda e, j=j, d=d: e.scalar_tensor_tensor(
                     xr[:, j, :], d[:], mur[:, j:j + 1], yfm[:, j, 0:512], ALU.mult, ALU.add),
                 reads=(("dif", j % 2), ("mu",)) + ykeys, writes=(("xr", j),), tag=f"xr {j}")
            p.op("dve",
                 lambda e, j=j, d=d: e.scalar_tensor_tensor(
                     xk[:, j, :], d[:], muk[:, j:j + 1], yfm[:, j, 0:512], ALU.mult, ALU.add),
                 reads=(("dif", j % 2), ("mu",)) + ykeys, writes=(("xk", j),), tag=f"xk {j}")
        for jj in range(8):
            bank = jj % 3
            pk = psum[bank]
            for j in range(8):
                p.op("pe",
                     lambda e, j=j, jj=jj, pk=pk: e.matmul(
                         pk[:], wkT[:, j, jj * 128:(jj + 1) * 128], xk[:, j, :],
                         start=(j == 0), stop=(j == 7)),
                     reads=(("WT", "wk", j), ("xk", j)), writes=(("psum", bank),),
                     tag=f"mmk {jj} {j}")
            kr = krelu[jj % 2]
            p.op("dve",
                 lambda e, kr=kr, pk=pk: e.tensor_scalar_max(kr[:], pk[:], 0.0),
                 reads=(("psum", bank),), writes=(("krelu", jj % 2),), tag=f"relu {jj}")
            p.op("act",
                 lambda e, kr=kr, jj=jj: e.activation(out=kact[:, jj, :], in_=kr[:], func=AF.Square),
                 reads=(("krelu", jj % 2),), writes=(("kact", jj),), tag=f"sq {jj}")
        for tt in range(4):
            tile = blk * 4 + tt
            for half in range(2):
                bank = 3 + half
                pr = psum[bank]
                for j in range(8):
                    p.op("pe",
                         lambda e, j=j, half=half, tt=tt, pr=pr: e.matmul(
                             pr[:], xr[:, j, tt * 128:(tt + 1) * 128],
                             wrT[:, j, half * 512:(half + 1) * 512],
                             start=(j == 0), stop=(j == 7)),
                         reads=(("WT", "wr", j), ("xr", j)), writes=(("psum", bank),),
                         tag=f"mmr {tile} {half} {j}")
                p.op("act",
                     lambda e, half=half, pr=pr, tt=tt: e.activation(
                         out=rgate[:, tt % 2, half * 512:(half + 1) * 512], in_=pr[:], func=AF.Sigmoid),
                     reads=(("psum", bank),), writes=(("rgate", tt % 2, half),), tag=f"sig {tile} {half}")
            for half in range(2):
                bank = 5 + half
                po = psum[bank]
                for jj in range(8):
                    p.op("pe",
                         lambda e, jj=jj, half=half, tt=tt, po=po: e.matmul(
                             po[:], kact[:, jj, tt * 128:(tt + 1) * 128],
                             wvT[:, jj, half * 512:(half + 1) * 512],
                             start=(jj == 0), stop=(jj == 7)),
                         reads=(("WT", "wv", jj), ("kact", jj)), writes=(("psum", bank),),
                         tag=f"mmv {tile} {half} {jj}")
                p.op("dve",
                     lambda e, half=half, tt=tt, po=po: e.tensor_tensor(
                         fblk[:, tt, half * 512:(half + 1) * 512], po[:],
                         rgate[:, tt % 2, half * 512:(half + 1) * 512], ALU.mult),
                     reads=(("psum", bank), ("rgate", tt % 2, half)), writes=(("fblk", tt, half),),
                     tag=f"fmul {tile} {half}")
        for tt in range(4):
            tile = blk * 4 + tt
            fa = fblk[:, tt, :]
            fk = (("fblk", tt, 0), ("fblk", tt, 1))
            p.op("act",
                 lambda e, fa=fa, tile=tile: e.activation(
                     out=scr[:], in_=fa, func=AF.Square, accum_out=ssq_b[:, tile:tile + 1]),
                 reads=fk, writes=(("lnscr",), ("ssq", id(ssq_b), tile)), tag=f"pln sq {tile}")
            p.op("act",
                 lambda e, fa=fa, tile=tile: e.activation(
                     out=scr[:], in_=fa, func=AF.Identity, accum_out=sum_b[:, tile:tile + 1]),
                 reads=fk, writes=(("lnscr",), ("lsum", id(sum_b), tile)), tag=f"pln id {tile}")
        emit_ln_finalize2(p, sum_b, ssq_b, rstd_b, bias_b, t0s, t1s,
                          (blk * 4, blk * 4 + 4), D)
        for tt in range(4):
            tile = blk * 4 + tt
            slot = tile % 2
            fa = fblk[:, tt, :]
            fk = (("fblk", tt, 0), ("fblk", tt, 1))
            lnf = bufs["lnf"]
            p.op("act",
                 lambda e, fa=fa, tile=tile, lnf=lnf, slot=slot: e.activation(
                     out=lnf[:, slot, :], in_=fa, func=AF.Identity,
                     scale=rstd_b[:, tile:tile + 1], bias=bias_b[:, tile:tile + 1]),
                 reads=fk + (("rstd", id(rstd_b), tile), ("lnbias", id(rstd_b), tile)),
                 writes=(("lnf", slot),), tag=f"pln y {tile}")
            p.op("pool",
                 lambda e, tile=tile, lnf=lnf, slot=slot: e.tensor_tensor(
                     X[:, tile, :], X[:, tile, :], lnf[:, slot, :], ALU.add),
                 reads=(("lnf", slot), ("X", tile)), writes=(("X", tile),), tag=f"res {tile}")





def emit_ln_norm(p, x_ap, xkeys, out_ap, outkeys, rstd_sb, bias_sb, tile, tag=""):
    K = id(rstd_sb)
    p.op("act",
         lambda e: e.activation(out=out_ap, in_=x_ap, func=AF.Identity,
                                scale=rstd_sb[:, tile:tile + 1],
                                bias=bias_sb[:, tile:tile + 1]),
         reads=tuple(xkeys) + (("rstd", K, tile), ("lnbias", K, tile)),
         writes=tuple(outkeys), tag=tag)


def emit_post_ln_residual2(p, X, fblk, scr, lnf, sum_b, ssq_b, rstd_b, bias_b, t0s, t1s, tile0):
    for pp in range(2):
        tile = tile0 + pp
        slot = tile % 2
        fk = (("fblk", slot, 0), ("fblk", slot, 1))
        fa = fblk[:, slot, :]
        p.op("act",
             lambda e, fa=fa, tile=tile: e.activation(
                 out=scr[:], in_=fa, func=AF.Square, accum_out=ssq_b[:, tile:tile + 1]),
             reads=fk, writes=(("lnscr",), ("ssq", id(ssq_b), tile)))
        p.op("act",
             lambda e, fa=fa, tile=tile: e.activation(
                 out=scr[:], in_=fa, func=AF.Identity, accum_out=sum_b[:, tile:tile + 1]),
             reads=fk, writes=(("lnscr",), ("lsum", id(sum_b), tile)))
    emit_ln_finalize2(p, sum_b, ssq_b, rstd_b, bias_b, t0s, t1s, (tile0, tile0 + 2), D)
    for pp in range(2):
        tile = tile0 + pp
        slot = tile % 2
        fk = (("fblk", slot, 0), ("fblk", slot, 1))
        emit_ln_norm(p, fblk[:, slot, :], fk, lnf[:, slot, :], (("lnf", slot),),
                     rstd_b, bias_b, tile)
        p.op("pool",
             lambda e, tile=tile, slot=slot: e.tensor_tensor(
                 X[:, tile, :], X[:, tile, :], lnf[:, slot, :], ALU.add),
             reads=(("lnf", slot), ("X", tile)), writes=(("X", tile),))


B1_STAGE = 5


def emit_block1(p, nc, bufs):
    stage = B1_STAGE
    X = bufs["X"]
    yfm = bufs["b1_yfm"]
    qfm = bufs["qfm"]
    kev = bufs["kev"]
    kod = bufs["kod"]
    vpad = bufs["vpad"]
    expS = bufs["expS"]
    otm = bufs["otm"]
    ofm = bufs["ofm"]
    fblk = bufs["fblk"]
    scr = bufs["scr"]
    lnf = bufs["lnf"]
    ytm = bufs["ytm"]
    ident = bufs["ident"]
    bo_bc = bufs["bo_bc"]
    maskA = bufs["maskA"]
    rcol = bufs["rcol"]
    psum = bufs["psum"]
    psumT = bufs["psumT"]
    psum_av = bufs["psum_av"]
    stats = bufs["stats"]
    wqT, wkT, wvT, woT = bufs["wqT"], bufs["wkT"], bufs["wvT"], bufs["woT"]

    sum_a, ssq_a = stats["sum_a"], stats["ssq_a"]
    rstd_a, bias_a = stats["rstd_a"], stats["bias_a"]
    sum_b, ssq_b = stats["sum_b"], stats["ssq_b"]
    rstd_b, bias_b = stats["rstd_b"], stats["bias_b"]
    t0s, t1s = stats["t0"], stats["t1"]

    def transpose_into(src_ap, src_keys, dst_ap, dst_keys, i):
        pt = psumT[i % 2]
        p.op("pe",
             lambda e: e.transpose(pt[:], src_ap, ident[:]),
             reads=tuple(src_keys) + (("ident",),), writes=(("psumT", i % 2),))
        p.op("dve",
             lambda e: e.tensor_copy(dst_ap, pt[:]),
             reads=(("psumT", i % 2),), writes=tuple(dst_keys))

    NB = NTILE // 4
    for blk in range(NB):
        batch_start = (blk % (NB // B_LOC) == 0)
        if batch_start:
            p.op("pool", lambda e: e.memset(kev[:, :, 0:128], 0.0),
                 reads=(("kfmhalo",),), writes=(("kfmhalo",),))
            p.op("pool", lambda e: e.memset(kod[:, :, 0:128], 0.0),
                 reads=(("kfmhalo",),), writes=(("kfmhalo",),))
            p.op("pool", lambda e: e.memset(vpad[:, 0, :, :], 0.0),
                 reads=(("vpadhalo",),), writes=(("vpadhalo",),))
        for tt in range(4):
            tile = blk * 4 + tt
            emit_ln_stats(p, X[:, tile, :], tile, sum_a, ssq_a, tile, scr, ("X", tile))
        emit_ln_finalize2(p, sum_a, ssq_a, rstd_a, bias_a, t0s, t1s,
                          (blk * 4, blk * 4 + 4), D)
        for tt in range(4):
            tile = blk * 4 + tt
            yt = ytm[tt % 2]
            emit_ln_norm(p, X[:, tile, :], (("X", tile),), yt[:], (("ytm", tt % 2),),
                         rstd_a, bias_a, tile, tag=f"b1 ln {tile}")
            for j in range(8):
                transpose_into(yt[:, j * 128:(j + 1) * 128], (("ytm", tt % 2),),
                               yfm[:, j, tt * 128:(tt + 1) * 128], (("b1yfm", j, tt),),
                               tt * 8 + j)
        ykeys = lambda j: tuple(("b1yfm", j, t_) for t_ in range(4))
        if stage < 2:
            continue
        for wT, wname, okey in ((wqT, "wq", "qfm"), (wkT, "wk", "kfm")):
            for jj in range(8):
                bank = jj % 2
                pk = psum[bank]
                for j in range(8):
                    p.op("pe",
                         lambda e, j=j, jj=jj, pk=pk, wT=wT: e.matmul(
                             pk[:], wT[:, j, jj * 128:(jj + 1) * 128], yfm[:, j, :],
                             start=(j == 0), stop=(j == 7)),
                         reads=(("WT", wname, j),) + ykeys(j),
                         writes=(("psum", bank),))
                if okey == "qfm":
                    p.op("dve",
                         lambda e, jj=jj, pk=pk: e.tensor_copy(qfm[:, jj, :], pk[:]),
                         reads=(("psum", bank),), writes=(("qfm", jj),))
                else:
                    p.op("dve",
                         lambda e, jj=jj, pk=pk: e.tensor_copy(kev[0:64, jj, 128:640], pk[0:64, :]),
                         reads=(("psum", bank),), writes=(("kfm", jj),))
                    p.op("act",
                         lambda e, jj=jj, pk=pk: e.activation(out=kod[64:128, jj, 128:640],
                                                              in_=pk[64:128, :], func=AF.Identity),
                         reads=(("psum", bank),), writes=(("kfm", jj),))
        for tt in range(4):
            for half in range(2):
                bank = 2 + half
                pv = psum[bank]
                for j in range(8):
                    p.op("pe",
                         lambda e, j=j, tt=tt, half=half, pv=pv: e.matmul(
                             pv[:], yfm[:, j, tt * 128:(tt + 1) * 128],
                             wvT[:, j, half * 512:(half + 1) * 512],
                             start=(j == 0), stop=(j == 7)),
                         reads=(("WT", "wv", j),) + ykeys(j), writes=(("psum", bank),))
                p.op("dve",
                     lambda e, tt=tt, half=half, pv=pv: e.tensor_copy(
                         vpad[:, tt + 1, half * 8:(half + 1) * 8, 0:64], pv[:]),
                     reads=(("psum", bank),), writes=(("vpad", tt + 1, half),))
        def vpad_keys(slot):
            if slot == 0:
                return (("vpadhalo",), ("vpadones",))
            return (("vpad", slot, 0), ("vpad", slot, 1), ("vpadones",))
        if stage < 2.25:
            continue
        for tq in range(4):
            mask = maskA
            mkey = ("maskA",)
            for hp in range(8):
                bank = 2 + hp % 2
                ppsc = psum[bank]
                es = expS[hp % 2]
                for hi in range(2):
                    ksrc = kev if hi == 0 else kod
                    for kt in range(2):
                        kslot = tq + kt
                        kread = (("kfmhalo",),) if kslot == 0 else (("kfm", hp),)
                        idx = hi * 2 + kt
                        p.op("pe",
                             lambda e, hp=hp, kslot=kslot, tq=tq, ppsc=ppsc, idx=idx, ksrc=ksrc: e.matmul(
                                 ppsc[:, idx * 128:(idx + 1) * 128],
                                 ksrc[:, hp, kslot * 128:(kslot + 1) * 128],
                                 qfm[:, hp, tq * 128:(tq + 1) * 128],
                                 start=(idx == 0), stop=(idx == 3), skip_group_check=True),
                             reads=(("qfm", hp), ("kzero",)) + kread,
                             writes=(("psum", bank),))
                if stage < 2.5:
                    continue
                p.op("act",
                     lambda e, es=es, ppsc=ppsc: e.activation(out=es[:], in_=ppsc[:],
                                                              func=AF.Exp, scale=0.125),
                     reads=(("psum", bank),), writes=(("expS", hp % 2),))
                if stage < 2.75:
                    continue
                p.op("dve",
                     lambda e, es=es, mask=mask: e.tensor_tensor(es[:], es[:], mask[:], ALU.mult),
                     reads=(("expS", hp % 2), mkey), writes=(("expS", hp % 2),))
                if stage < 4:
                    continue
                for hi in range(2):
                    h = 2 * hp + hi
                    pav = psum_av[hi]
                    for kt in range(2):
                        kslot = tq + kt
                        p.op("pe",
                             lambda e, hi=hi, kt=kt, h=h, kslot=kslot, pav=pav, es=es: e.matmul(
                                 pav[:],
                                 es[:, (hi * 2 + kt) * 128:(hi * 2 + kt + 1) * 128],
                                 vpad[:, kslot, h, :],
                                 start=(kt == 0), stop=(kt == 1)),
                             reads=(("expS", hp % 2),) + vpad_keys(kslot),
                             writes=(("psum_av", hi),))
                    p.op("dve",
                         lambda e, pav=pav, h=h: e.reciprocal(rcol[:, h:h + 1], pav[:, 64:65]),
                         reads=(("psum_av", hi),), writes=(("rcol", h),))
                    if hi == 0:
                        p.op("act",
                             lambda e, h=h, tq=tq, pav=pav: e.activation(
                                 out=otm[:, tq % 2, h * 64:(h + 1) * 64], in_=pav[:, 0:64],
                                 func=AF.Identity, scale=rcol[:, h:h + 1]),
                             reads=(("psum_av", hi), ("rcol", h)), writes=(("otm", tq % 2, h),))
                    else:
                        p.op("dve",
                             lambda e, h=h, tq=tq, pav=pav: e.tensor_scalar(
                                 otm[:, tq % 2, h * 64:(h + 1) * 64], pav[:, 0:64],
                                 rcol[:, h:h + 1], None, ALU.mult),
                             reads=(("psum_av", hi), ("rcol", h)), writes=(("otm", tq % 2, h),))
            if stage < 5:
                continue
            okeys = tuple(("otm", tq % 2, h) for h in range(16))
            for j in range(8):
                transpose_into(otm[:, tq % 2, j * 128:(j + 1) * 128], okeys,
                               ofm[:, tq % 2, j, :], (("ofm", j, tq % 2),),
                               tq * 8 + j)
            for half in range(2):
                bank = half
                po = psum[bank]
                for j in range(8):
                    p.op("pe",
                         lambda e, j=j, tq=tq, half=half, po=po: e.matmul(
                             po[:], ofm[:, tq % 2, j, :],
                             woT[:, j, half * 512:(half + 1) * 512],
                             start=(j == 0), stop=(j == 7)),
                         reads=(("WT", "wo", j), ("ofm", j, tq % 2)), writes=(("psum", bank),))
                p.op("dve",
                     lambda e, tq=tq, half=half, po=po: e.tensor_tensor(
                         fblk[:, tq % 2, half * 512:(half + 1) * 512], po[:],
                         bo_bc[:, half * 512:(half + 1) * 512], ALU.add),
                     reads=(("psum", bank), ("bo_bc",)), writes=(("fblk", tq % 2, half),))
            if tq % 2 == 1:
                emit_post_ln_residual2(p, X, fblk, scr, lnf, sum_b, ssq_b, rstd_b,
                                       bias_b, t0s, t1s, blk * 4 + tq - 1)
        if stage < 3:
            continue
        p.op("dve", lambda e: e.tensor_copy(kev[:, :, 0:128], kev[:, :, 512:640]),
             reads=tuple(("kfm", jj) for jj in range(8)), writes=(("kfmhalo",),))
        p.op("dve", lambda e: e.tensor_copy(kod[:, :, 0:128], kod[:, :, 512:640]),
             reads=tuple(("kfm", jj) for jj in range(8)), writes=(("kfmhalo",),))
        p.op("dve", lambda e: e.tensor_copy(vpad[:, 0, :, :], vpad[:, 4, :, :]),
             reads=(("vpad", 4, 0), ("vpad", 4, 1), ("vpadones",)), writes=(("vpadhalo",),))


def emit_b1_setup(p, nc, bufs, bo_d):
    maskA = bufs["maskA"]
    vpad = bufs["vpad"]
    kev, kod = bufs["kev"], bufs["kod"]
    bo_bc = bufs["bo_bc"]
    for m, key, with_prev in ((maskA, "maskA", True),):
        p.op("pool", lambda e, m=m: e.memset(m[:], 1.0), writes=((key,),))
        for rep in range(2):
            off = rep * 256
            p.op("pool", lambda e, m=m, off=off: e.affine_select(
                m[:, off + 128:off + 256], m[:, off + 128:off + 256],
                [[1, 128]], ALU.is_ge, 0.0, base=0, channel_multiplier=-1),
                reads=((key,),), writes=((key,),))
            p.op("pool", lambda e, m=m, off=off: e.affine_select(
                m[:, off + 128:off + 256], m[:, off + 128:off + 256],
                [[-1, 128]], ALU.is_ge, 0.0, base=63, channel_multiplier=1),
                reads=((key,),), writes=((key,),))
            if with_prev:
                p.op("pool", lambda e, m=m, off=off: e.affine_select(
                    m[:, off:off + 128], m[:, off:off + 128],
                    [[-1, 128]], ALU.is_ge, 0.0, base=-65, channel_multiplier=1),
                    reads=((key,),), writes=((key,),))
            else:
                p.op("pool", lambda e, m=m, off=off: e.memset(m[:, off:off + 128], 0.0),
                     reads=((key,),), writes=((key,),))
    p.op("pool", lambda e: e.memset(vpad[:, :, :, 64:65], 1.0), writes=(("vpadones",),))
    p.op("pool", lambda e: e.memset(kev[64:128, :, :], 0.0), writes=(("kzero",),))
    p.op("pool", lambda e: e.memset(kod[0:64, :, :], 0.0), writes=(("kzero",),))
    p.op("pool", lambda e: e.dma_start(out=bo_bc[:, :], in_=bo_d[None, :].broadcast_to([128, D])),
         writes=(("bo_bc",),), inc=16)


def build_b1_program():
    nc = bass.Bass("TRN2", target_bir_lowering=False)
    stack = contextlib.ExitStack()
    x_d = nc.dram_tensor("x", [NTILE, 128, D], F32, kind="ExternalInput").ap()
    wq_d = nc.dram_tensor("wq", [D, D], F32, kind="ExternalInput").ap()
    wk_d = nc.dram_tensor("wk", [D, D], F32, kind="ExternalInput").ap()
    wv_d = nc.dram_tensor("wv", [D, D], F32, kind="ExternalInput").ap()
    wo_d = nc.dram_tensor("wo", [D, D], F32, kind="ExternalInput").ap()
    bo_d = nc.dram_tensor("bo", [D], F32, kind="ExternalInput").ap()
    out_d = nc.dram_tensor("out", [NTILE, 128, D], F32, kind="ExternalOutput").ap()
    scr_w = {n: nc.dram_tensor(f"wscr_{n}", [D, D], BF16, kind="Internal").ap()
             for n in ("wq", "wk", "wv", "wo")}

    p = Prog(nc)
    sb = lambda name, shape, dt: stack.enter_context(nc.sbuf_tensor(name, shape, dt))
    ps = lambda name, shape, dt: stack.enter_context(nc.psum_tensor(name, shape, dt))

    bufs = {}
    bufs["X"] = sb("X", [128, NTILE, D], BF16)
    for n in ("wq", "wk", "wv", "wo"):
        bufs[n + "T"] = sb(n + "T", [128, 8, D], BF16)
    bufs["b1_yfm"] = sb("b1_yfm", [128, 8, 512], BF16)
    bufs["qfm"] = sb("qfm", [128, 8, 512], BF16)
    bufs["kev"] = sb("kev", [128, 8, 640], BF16)
    bufs["kod"] = sb("kod", [128, 8, 640], BF16)
    bufs["vpad"] = sb("vpad", [128, 5, 16, 65], BF16)
    bufs["expS"] = [sb(f"expS{i}", [128, 512], BF16) for i in range(2)]
    bufs["otm"] = sb("otm", [128, 2, D], BF16)
    bufs["ofm"] = sb("ofm", [128, 2, 8, 128], BF16)
    bufs["fblk"] = sb("fblk", [128, 2, D], BF16)
    bufs["scr"] = sb("scr", [128, D], BF16)
    bufs["lnf"] = sb("lnf", [128, 2, D], BF16)
    bufs["ytm"] = [sb(f"ytm{i}", [128, D], BF16) for i in range(2)]
    bufs["ident"] = sb("ident", [128, 128], BF16)
    bufs["bo_bc"] = sb("bo_bc", [128, D], BF16)
    bufs["maskA"] = sb("maskA", [128, 512], BF16)
    bufs["rcol"] = sb("rcol", [128, 16], F32)
    bufs["psum"] = [ps(f"pb{i}", [128, 512], F32) for i in range(4)]
    bufs["psumT"] = [ps(f"psT{i}", [128, 128], BF16) for i in range(2)]
    bufs["psum_av"] = [ps(f"pav{i}", [128, 65], F32) for i in range(2)]
    stats = {}
    for nmm in ("sum_a", "ssq_a", "rstd_a", "bias_a", "sum_b", "ssq_b", "rstd_b", "bias_b", "t0", "t1"):
        stats[nmm] = sb("st_" + nmm, [128, NTILE], F32)
    bufs["stats"] = stats
    p.op("pool", lambda e: e.memset(bufs["ident"][:], 1.0), writes=(("ident",),))
    p.op("pool", lambda e: e.affine_select(bufs["ident"][:], bufs["ident"][:],
                                           [[-1, 128]], ALU.is_equal, 0.0,
                                           base=0, channel_multiplier=1),
         reads=(("ident",),), writes=(("ident",),))
    emit_b1_setup(p, nc, bufs, bo_d)
    wtmps = [bufs["ytm"][0], bufs["ytm"][1]]
    wtkeys = [("ytm", 0), ("ytm", 1)]
    for n, wd in (("wq", wq_d), ("wk", wk_d), ("wv", wv_d), ("wo", wo_d)):
        emit_weight_load_T(p, wd, scr_w[n], bufs[n + "T"], n, D, D, wtmps, wtkeys)
    for t in range(NTILE):
        p.op("pool", lambda e, t=t: e.dma_start(out=bufs["X"][:, t, :], in_=x_d[t, :, :]),
             writes=(("X", t),), inc=16)

    emit_block1(p, nc, bufs)

    for t in range(NTILE):
        p.op("pool", lambda e, t=t: e.dma_start(out=out_d[t, :, :], in_=bufs["X"][:, t, :]),
             reads=(("X", t),), writes=(), inc=16)

    p.build(stack)
    return nc, stack





def emit_weight_load_nat(p, w_dram, w_sb, name, din, dout):
    for j in range(din // 128):
        p.op("pool",
             lambda e, j=j: e.dma_start(out=w_sb[:, j, :], in_=w_dram[j * 128:(j + 1) * 128, :]),
             writes=(("WN", name, j),), inc=16, tag=f"wnat {name} {j}")


def emit_block3(p, nc, bufs):
    X = bufs["X"]
    yfm = bufs["b3_yfm"]
    qfm = bufs["qfm3"]
    kfm = bufs["kfm3"]
    sqf = bufs["sqf"]
    skf = bufs["skf"]
    vpad3 = bufs["vpad3"]
    esb = bufs["esb"]
    ablk = bufs["ablk"]
    Mz = bufs["Mz"]
    Mzbf = bufs["Mzbf"]
    vdp = bufs["vdp"]
    sktm = bufs["sktm"]
    atmp = bufs["atmp"]
    es1 = bufs["es1"]
    rq_t, rd_t, rk_t = bufs["rq_t"], bufs["rd_t"], bufs["rk_t"]
    scr = bufs["scr"]
    lnf = bufs["lnf"]
    ytm = bufs["ytm"]
    ident = bufs["ident"]
    cmask = bufs["cmask"]
    gbc, ombc = bufs["gbc"], bufs["ombc"]
    bqc, bkc = bufs["bqc"], bufs["bkc"]
    bvbc = bufs["bvbc"]
    wqN, wkN, wvN = bufs["wqN"], bufs["wkN"], bufs["wvN"]
    pj = bufs["pj"]
    pdot, pmem, pMz = bufs["pdot"], bufs["pmem"], bufs["pMz"]
    psumT = bufs["psumT"]
    stats = bufs["stats"]
    sum_a, ssq_a = stats["sum_a"], stats["ssq_a"]
    rstd_a, bias_a = stats["rstd_a"], stats["bias_a"]
    sum_b, ssq_b = stats["sum_b"], stats["ssq_b"]
    rstd_b, bias_b = stats["rstd_b"], stats["bias_b"]
    t0s, t1s = stats["t0"], stats["t1"]
    SCALE = float(1.0 / np.sqrt(128.0))
    EPS = 1e-6

    def transpose_into(src_ap, src_keys, dst_ap, dst_keys, i):
        pt = psumT[i % 2]
        p.op("pe", lambda e: e.transpose(pt[:], src_ap, ident[:]),
             reads=tuple(src_keys) + (("ident",),), writes=(("psumT", i % 2),))
        p.op("dve", lambda e: e.tensor_copy(dst_ap, pt[:]),
             reads=(("psumT", i % 2),), writes=tuple(dst_keys))

    NB = NTILE // 4
    for blk in range(NB):
        seg = blk % 4
        if seg == 0:
            p.op("dve", lambda e: e.memset(Mz[:], 0.0),
                 reads=(("Mz",),), writes=(("Mz",),))
            p.op("dve", lambda e: e.memset(Mzbf[:], 0.0),
                 reads=(("Mzbf",),), writes=(("Mzbf",),))
        for tt in range(4):
            tile = blk * 4 + tt
            emit_ln_stats(p, X[:, tile, :], tile, sum_a, ssq_a, tile, scr, ("X", tile))
        emit_ln_finalize2(p, sum_a, ssq_a, rstd_a, bias_a, t0s, t1s,
                          (blk * 4, blk * 4 + 4), D)
        for tt in range(4):
            tile = blk * 4 + tt
            yt = ytm[tt % 2]
            emit_ln_norm(p, X[:, tile, :], (("X", tile),), yt[:], (("ytm", tt % 2),),
                         rstd_a, bias_a, tile, tag=f"b3 ln {tile}")
            for j in range(8):
                transpose_into(yt[:, j * 128:(j + 1) * 128], (("ytm", tt % 2),),
                               yfm[:, j, tt * 128:(tt + 1) * 128], (("b3yfm", j, tt),),
                               tt * 8 + j)
        ykeys = lambda j: tuple(("b3yfm", j, t_) for t_ in range(4))
        for wN, wname, dst, dkey, bias in ((wqN, "wq", qfm, "qfm3", bqc),
                                           (wkN, "wk", kfm, "kfm3", bkc)):
            for jj in range(8):
                bank = jj % 2
                pk = pj[bank]
                for j in range(8):
                    p.op("pe",
                         lambda e, j=j, jj=jj, pk=pk, wN=wN: e.matmul(
                             pk[:], wN[:, j, jj * 128:(jj + 1) * 128], yfm[:, j, :],
                             start=(j == 0), stop=(j == 7)),
                         reads=(("WN", wname, j),) + ykeys(j), writes=(("pj", bank),))
                p.op("act",
                     lambda e, jj=jj, pk=pk, dst=dst, bias=bias: e.activation(
                         out=dst[:, jj, :], in_=pk[:], func=AF.Identity,
                         bias=bias[:, jj:jj + 1]),
                     reads=(("pj", bank), ("bias3",)), writes=((dkey, jj),))
        for tt in range(4):
            for half in range(2):
                bank = half
                pv = pj[bank]
                for j in range(8):
                    p.op("pe",
                         lambda e, j=j, tt=tt, half=half, pv=pv: e.matmul(
                             pv[:], yfm[:, j, tt * 128:(tt + 1) * 128],
                             wvN[:, j, half * 512:(half + 1) * 512],
                             start=(j == 0), stop=(j == 7)),
                         reads=(("WN", "wv", j),) + ykeys(j), writes=(("pj", bank),))
                p.op("dve",
                     lambda e, tt=tt, half=half, pv=pv: e.tensor_tensor(
                         vpad3[:, tt, half * 4:(half + 1) * 4, 0:128], pv[:],
                         bvbc[:, half * 4:(half + 1) * 4, :], ALU.add),
                     reads=(("pj", bank), ("bias3",)), writes=(("vp3", tt, half),))
        vkeys = lambda tt: (("vp3", tt, 0), ("vp3", tt, 1), ("vp3ones",))
        for src_t, skey, dst, dkey in ((qfm, "qfm3", sqf, "sqf"), (kfm, "kfm3", skf, "skf")):
            for h in range(8):
                sl = h % 2
                e1 = es1[:, sl, :]
                p.op("dve",
                     lambda e, h=h, src_t=src_t, e1=e1: e.tensor_scalar_min(e1, src_t[:, h, :], 0.0),
                     reads=((skey, h),), writes=(("es1", sl),))
                p.op("act",
                     lambda e, e1=e1: e.activation(out=e1, in_=e1, func=AF.Exp),
                     reads=(("es1", sl),), writes=(("es1", sl),))
                p.op("dve",
                     lambda e, h=h, src_t=src_t, dst=dst, e1=e1: e.scalar_tensor_tensor(
                         dst[:, h, :], src_t[:, h, :], 0.0, e1, ALU.max, ALU.add),
                     reads=((skey, h), ("es1", sl)), writes=((dkey, h),))
        for h in range(8):
            for j in range(4):
                bank = j % 2
                psc = pj[bank]
                n = 512 - 128 * j
                p.op("pe",
                     lambda e, h=h, j=j, psc=psc, n=n: e.matmul(
                         psc[:, 0:n], kfm[:, h, j * 128:(j + 1) * 128],
                         qfm[:, h, j * 128:512], start=True, stop=True),
                     reads=(("kfm3", h), ("qfm3", h)), writes=(("pj", bank),))
                p.op("act",
                     lambda e, h=h, j=j, psc=psc, n=n: e.activation(
                         out=esb[:, j, j * 128:512], in_=psc[:, 0:n], func=AF.Exp,
                         scale=SCALE),
                     reads=(("pj", bank),), writes=(("esb", j),))
                p.op("dve",
                     lambda e, j=j, n=n: e.tensor_tensor(
                         esb[:, j, j * 128:512], esb[:, j, j * 128:512],
                         cmask[:, j, j * 128:512], ALU.mult),
                     reads=(("esb", j), ("cmask",)), writes=(("esb", j),))
            for i in range(4):
                pmn = pmem[i % 2]
                pd = pdot[i % 2]
                p.op("pe",
                     lambda e, h=h, i=i, pmn=pmn: e.matmul(
                         pmn[:, 0:129], sqf[:, h, i * 128:(i + 1) * 128], Mzbf[:, h, :],
                         start=True, stop=False, skip_group_check=True),
                     reads=(("sqf", h), ("Mzbf",)), writes=(("pmem", 0),))
                p.op("pe",
                     lambda e, h=h, i=i, pmn=pmn: e.matmul(
                         pmn[:, 129:258], skf[:, h, i * 128:(i + 1) * 128], Mzbf[:, h, :],
                         start=False, stop=True, skip_group_check=True),
                     reads=(("skf", h), ("Mzbf",)), writes=(("pmem", 0),))
                for j in range(i + 1):
                    p.op("pe",
                         lambda e, h=h, i=i, j=j, pd=pd: e.matmul(
                             pd[:], esb[:, j, i * 128:(i + 1) * 128],
                             vpad3[:, j, h, :], start=(j == 0), stop=(j == i)),
                         reads=(("esb", j),) + vkeys(j), writes=(("pdot", i % 2),))
                rkv = rk_t[:, i:i + 1]
                p.op("dve", lambda e, rkv=rkv, pmn=pmn: e.tensor_scalar(rkv, pmn[:, 257:258], -1.0, -EPS, ALU.mult, ALU.add),
                     reads=(("pmem", 0),), writes=(("rk_t", i),))
                p.op("dve", lambda e, rkv=rkv: e.reciprocal(rkv, rkv),
                     reads=(("rk_t", i),), writes=(("rk_t", i),))
                vd = vdp[i % 2]
                p.op("dve",
                     lambda e, h=h, i=i, vd=vd, rkv=rkv, pmn=pmn: e.scalar_tensor_tensor(
                         vd[:, 0:128], pmn[:, 129:257], rkv,
                         vpad3[:, i, h, 0:128], ALU.mult, ALU.add),
                     reads=(("pmem", 0), ("rk_t", i)) + vkeys(i), writes=(("vdp", i % 2),))
                transpose_into(skf[:, h, i * 128:(i + 1) * 128], (("skf", h),),
                               sktm[i % 2][:], (("sktm", i % 2),), i)
                p.op("pe",
                     lambda e, i=i, vd=vd: e.matmul(
                         pMz[:], sktm[i % 2][:], vd[:], start=(i == 0), stop=(i == 3)),
                     reads=(("sktm", i % 2), ("vdp", i % 2), ("vdpones",)),
                     writes=(("pMz",),))
                rqv = rq_t[:, i:i + 1]
                rdv = rd_t[:, i:i + 1]
                at = atmp[:, i % 2, :]
                p.op("dve", lambda e, rqv=rqv, pmn=pmn: e.tensor_scalar(rqv, pmn[:, 128:129], 1.0, EPS, ALU.mult, ALU.add),
                     reads=(("pmem", 0),), writes=(("rq_t", i),))
                p.op("dve", lambda e, rqv=rqv: e.reciprocal(rqv, rqv),
                     reads=(("rq_t", i),), writes=(("rq_t", i),))
                p.op("dve", lambda e, rdv=rdv, pd=pd: e.reciprocal(rdv, pd[:, 128:129]),
                     reads=(("pdot", i % 2),), writes=(("rd_t", i),))
                p.op("dve",
                     lambda e, h=h, rdv=rdv: e.tensor_scalar(rdv, rdv, ombc[:, h:h + 1], None, ALU.mult),
                     reads=(("rd_t", i), ("gom",)), writes=(("rd_t", i),))
                p.op("act",
                     lambda e, h=h, at=at, rqv=rqv, pmn=pmn: e.activation(
                         out=at, in_=pmn[:, 0:128], func=AF.Identity, scale=rqv),
                     reads=(("pmem", 0), ("rq_t", i), ("gom",)), writes=(("atmp", i % 2),))
                p.op("dve",
                     lambda e, h=h, at=at: e.tensor_scalar(at, at, gbc[:, h:h + 1], None, ALU.mult),
                     reads=(("atmp", i % 2), ("gom",)), writes=(("atmp", i % 2),))
                p.op("dve",
                     lambda e, h=h, i=i, at=at, rdv=rdv, pd=pd: e.scalar_tensor_tensor(
                         ablk[:, i, h * 128:(h + 1) * 128], pd[:, 0:128],
                         rdv, at, ALU.mult, ALU.add),
                     reads=(("pdot", i % 2), ("rd_t", i), ("atmp", i % 2)), writes=(("ablk", i, h),))
            p.op("dve",
                 lambda e, h=h: e.tensor_tensor(Mz[:, h, :], Mz[:, h, :], pMz[:], ALU.add),
                 reads=(("pMz",), ("Mz",)), writes=(("Mz",),))
            p.op("dve",
                 lambda e, h=h: e.tensor_copy(Mzbf[:, h, :], Mz[:, h, :]),
                 reads=(("Mz",),), writes=(("Mzbf",),))
        for tt in range(4):
            tile = blk * 4 + tt
            akeys = tuple(("ablk", tt, h) for h in range(8))
            fa = ablk[:, tt, :]
            p.op("act",
                 lambda e, fa=fa, tile=tile: e.activation(
                     out=scr[:], in_=fa, func=AF.Square, accum_out=ssq_b[:, tile:tile + 1]),
                 reads=akeys, writes=(("lnscr",), ("ssq", id(ssq_b), tile)))
            p.op("act",
                 lambda e, fa=fa, tile=tile: e.activation(
                     out=scr[:], in_=fa, func=AF.Identity, accum_out=sum_b[:, tile:tile + 1]),
                 reads=akeys, writes=(("lnscr",), ("lsum", id(sum_b), tile)))
        emit_ln_finalize2(p, sum_b, ssq_b, rstd_b, bias_b, t0s, t1s,
                          (blk * 4, blk * 4 + 4), D)
        for tt in range(4):
            tile = blk * 4 + tt
            slot = tile % 2
            akeys = tuple(("ablk", tt, h) for h in range(8))
            emit_ln_norm(p, ablk[:, tt, :], akeys, lnf[:, slot, :], (("lnf", slot),),
                         rstd_b, bias_b, tile)
            p.op("pool",
                 lambda e, tile=tile, slot=slot: e.tensor_tensor(
                     X[:, tile, :], X[:, tile, :], lnf[:, slot, :], ALU.add),
                 reads=(("lnf", slot), ("X", tile)), writes=(("X", tile),))
        hook = bufs.get("tail_hook")
        if hook is not None:
            hook(blk)


def emit_b3_setup(p, nc, bufs, bq_d, bk_d, bv_d, beta_d):
    cmask = bufs["cmask"]
    vpad3 = bufs["vpad3"]
    vdp = bufs["vdp"]
    gbc, ombc = bufs["gbc"], bufs["ombc"]
    bqc, bkc, bvbc = bufs["bqc"], bufs["bkc"], bufs["bvbc"]
    for j in range(4):
        p.op("pool", lambda e, j=j: e.memset(cmask[:, j, :], 1.0),
             reads=(("cmask",),), writes=(("cmask",),))
        p.op("pool", lambda e, j=j: e.affine_select(
            cmask[:, j, :], cmask[:, j, :], [[1, 512]], ALU.is_ge, 0.0,
            base=-128 * j, channel_multiplier=-1),
            reads=(("cmask",),), writes=(("cmask",),))
    p.op("pool", lambda e: e.memset(vpad3[:, :, :, 128:129], 1.0), writes=(("vp3ones",),))
    for i in range(2):
        p.op("pool", lambda e, i=i: e.memset(vdp[i][:, 128:129], 1.0), writes=(("vdpones",),))
    for jj in range(8):
        p.op("sp", lambda e, jj=jj: e.dma_start(out=bqc[:, jj], in_=bq_d[jj * 128:(jj + 1) * 128]),
             writes=(("bias3",),), inc=16)
        p.op("sp", lambda e, jj=jj: e.dma_start(out=bkc[:, jj], in_=bk_d[jj * 128:(jj + 1) * 128]),
             writes=(("bias3",),), inc=16)
    p.op("pool", lambda e: e.dma_start(
        out=bvbc[:, :, :], in_=bv_d.rearrange("(a c) -> a c", a=8)[None, :, :].broadcast_to([128, 8, 128])),
        writes=(("bias3",),), inc=16)
    p.op("pool", lambda e: e.dma_start(
        out=gbc[:, :], in_=beta_d[None, :].broadcast_to([128, 8])),
        writes=(("gom",),), inc=16)
    p.op("act", lambda e: e.activation(out=gbc[:], in_=gbc[:], func=AF.Sigmoid),
         reads=(("gom",),), writes=(("gom",),))
    p.op("dve", lambda e: e.tensor_scalar(ombc[:], gbc[:], -1.0, 1.0, ALU.mult, ALU.add),
         reads=(("gom",),), writes=(("gom",),))


def build_b3_program():
    nc = bass.Bass("TRN2", target_bir_lowering=False)
    stack = contextlib.ExitStack()
    x_d = nc.dram_tensor("x", [NTILE, 128, D], F32, kind="ExternalInput").ap()
    wq_d = nc.dram_tensor("wq", [D, D], F32, kind="ExternalInput").ap()
    bq_d = nc.dram_tensor("bq", [D], F32, kind="ExternalInput").ap()
    wk_d = nc.dram_tensor("wk", [D, D], F32, kind="ExternalInput").ap()
    bk_d = nc.dram_tensor("bk", [D], F32, kind="ExternalInput").ap()
    wv_d = nc.dram_tensor("wv", [D, D], F32, kind="ExternalInput").ap()
    bv_d = nc.dram_tensor("bv", [D], F32, kind="ExternalInput").ap()
    beta_d = nc.dram_tensor("beta", [8], F32, kind="ExternalInput").ap()
    out_d = nc.dram_tensor("out", [NTILE, 128, D], F32, kind="ExternalOutput").ap()

    p = Prog(nc)
    sb = lambda name, shape, dt: stack.enter_context(nc.sbuf_tensor(name, shape, dt))
    ps = lambda name, shape, dt: stack.enter_context(nc.psum_tensor(name, shape, dt))

    bufs = {}
    bufs["X"] = sb("X", [128, NTILE, D], BF16)
    bufs["wqN"] = sb("wqN", [128, 8, D], BF16)
    bufs["wkN"] = sb("wkN", [128, 8, D], BF16)
    bufs["wvN"] = sb("wvN", [128, 8, D], BF16)
    bufs["b3_yfm"] = sb("b3_yfm", [128, 8, 512], BF16)
    bufs["qfm3"] = sb("qfm3", [128, 8, 512], BF16)
    bufs["kfm3"] = sb("kfm3", [128, 8, 512], BF16)
    bufs["sqf"] = sb("sqf", [128, 8, 512], BF16)
    bufs["skf"] = sb("skf", [128, 8, 512], BF16)
    bufs["vpad3"] = sb("vpad3", [128, 4, 8, 129], BF16)
    bufs["esb"] = sb("esb", [128, 4, 512], BF16)
    bufs["ablk"] = sb("ablk", [128, 4, D], BF16)
    bufs["Mz"] = sb("Mz", [128, 8, 129], F32)
    bufs["Mzbf"] = sb("Mzbf", [128, 8, 129], BF16)
    bufs["vdp"] = [sb(f"vdp{i}", [128, 129], BF16) for i in range(2)]
    bufs["sktm"] = [sb(f"sktm{i}", [128, 128], BF16) for i in range(2)]
    bufs["atmp"] = sb("atmp", [128, 2, 128], F32)
    bufs["es1"] = sb("es1", [128, 2, 512], F32)
    bufs["rq_t"] = sb("rq_t", [128, 4], F32)
    bufs["rd_t"] = sb("rd_t", [128, 4], F32)
    bufs["rk_t"] = sb("rk_t", [128, 4], F32)
    bufs["scr"] = sb("scr", [128, D], BF16)
    bufs["lnf"] = sb("lnf", [128, 2, D], BF16)
    bufs["ytm"] = [sb(f"ytm{i}", [128, D], BF16) for i in range(2)]
    bufs["ident"] = sb("ident", [128, 128], BF16)
    bufs["cmask"] = sb("cmask", [128, 4, 512], BF16)
    bufs["gbc"] = sb("gbc", [128, 8], F32)
    bufs["ombc"] = sb("ombc", [128, 8], F32)
    bufs["bqc"] = sb("bqc", [128, 8], F32)
    bufs["bkc"] = sb("bkc", [128, 8], F32)
    bufs["bvbc"] = sb("bvbc", [128, 8, 128], BF16)
    bufs["pj"] = [ps(f"pj{i}", [128, 512], F32) for i in range(2)]
    bufs["pdot"] = [ps(f"pdot{i}", [128, 129], F32) for i in range(2)]
    bufs["pmem"] = [ps("pmem0", [128, 258], F32)] * 2
    bufs["pMz"] = ps("pMz", [128, 129], F32)
    bufs["psumT"] = [ps(f"psT{i}", [128, 128], BF16) for i in range(2)]
    stats = {}
    for nmm in ("sum_a", "ssq_a", "rstd_a", "bias_a", "sum_b", "ssq_b", "rstd_b", "bias_b", "t0", "t1"):
        stats[nmm] = sb("st_" + nmm, [128, NTILE], F32)
    bufs["stats"] = stats

    p.op("pool", lambda e: e.memset(bufs["ident"][:], 1.0), writes=(("ident",),))
    p.op("pool", lambda e: e.affine_select(bufs["ident"][:], bufs["ident"][:],
                                           [[-1, 128]], ALU.is_equal, 0.0,
                                           base=0, channel_multiplier=1),
         reads=(("ident",),), writes=(("ident",),))
    emit_b3_setup(p, nc, bufs, bq_d, bk_d, bv_d, beta_d)
    emit_weight_load_nat(p, wq_d, bufs["wqN"], "wq", D, D)
    emit_weight_load_nat(p, wk_d, bufs["wkN"], "wk", D, D)
    emit_weight_load_nat(p, wv_d, bufs["wvN"], "wv", D, D)
    for t in range(NTILE):
        p.op("pool", lambda e, t=t: e.dma_start(out=bufs["X"][:, t, :], in_=x_d[t, :, :]),
             writes=(("X", t),), inc=16)

    emit_block3(p, nc, bufs)

    for t in range(NTILE):
        p.op("pool", lambda e, t=t: e.dma_start(out=out_d[t, :, :], in_=bufs["X"][:, t, :]),
             reads=(("X", t),), writes=(), inc=16)

    p.build(stack)
    return nc, stack





def emit_block4_shared(p, nc, bufs, y4_out, f_out):
    X = bufs["X"]
    yfm = bufs["b4_yfm"]
    shfm = bufs["shfm"]
    sc1 = bufs["sc1"]
    sc2 = bufs["sc2"]
    gbuf = bufs["gbuf"]
    shbf = bufs["shbf"]
    ftile = bufs["ftile"]
    y4t = bufs["y4t"]
    scr = bufs["scr"]
    ytm = bufs["ytm"]
    ident = bufs["ident"]
    sw1T, sw2T, sw3T = bufs["sw1T"], bufs["sw2T"], bufs["sw3T"]
    sb1bc, sb2bc, sb3bc = bufs["sb1bc"], bufs["sb2bc"], bufs["sb3bc"]
    slgbc, slbbc = bufs["slgbc"], bufs["slbbc"]
    pj = bufs["pj"]
    psumT = bufs["psumT"]
    stats = bufs["stats"]
    sum_a, ssq_a = stats["sum_a"], stats["ssq_a"]
    rstd_a, bias_a = stats["rstd_a"], stats["bias_a"]
    sum_b, ssq_b = stats["sum_b"], stats["ssq_b"]
    rstd_b, bias_b = stats["rstd_b"], stats["bias_b"]
    t0s, t1s = stats["t0"], stats["t1"]

    def transpose_into(src_ap, src_keys, dst_ap, dst_keys, i):
        pt = psumT[i % 2]
        p.op("pe", lambda e: e.transpose(pt[:], src_ap, ident[:]),
             reads=tuple(src_keys) + (("ident",),), writes=(("psumT", i % 2),))
        p.op("dve", lambda e: e.tensor_copy(dst_ap, pt[:]),
             reads=(("psumT", i % 2),), writes=tuple(dst_keys))

    NB = NTILE // 4
    split = bool(bufs.get("split_y4"))
    y4bf = bufs.get("y4bf")
    SLW = sc1.shape[1]
    NPJ = len(pj)
    if split and not bufs.get("skip_y4_pass"):
        for blk in range(NB):
            for tt in range(4):
                tile = blk * 4 + tt
                emit_ln_stats(p, X[:, tile, :], tile, sum_a, ssq_a, tile, scr[:, 0, :], ("X", tile))
            emit_ln_finalize2(p, sum_a, ssq_a, rstd_a, bias_a, t0s, t1s,
                              (blk * 4, blk * 4 + 4), D)
            for tt in range(4):
                tile = blk * 4 + tt
                y4v = y4t[:, tt % 2, :]
                emit_ln_norm(p, X[:, tile, :], (("X", tile),), y4v, (("y4t", tt % 2),),
                             rstd_a, bias_a, tile, tag=f"b4 ln {tile}")
                p.op("pool", lambda e, tile=tile, y4v=y4v: e.dma_start(out=y4_out[tile, :, :], in_=y4v),
                     reads=(("y4t", tt % 2),), writes=(("y4w", tile),), inc=16)
        hook = bufs.get("rs_hook")
        if hook is not None:
            hook()
    for blk in range(NB):
        if not split:
            for tt in range(4):
                tile = blk * 4 + tt
                emit_ln_stats(p, X[:, tile, :], tile, sum_a, ssq_a, tile, scr[:, 0, :], ("X", tile))
            emit_ln_finalize2(p, sum_a, ssq_a, rstd_a, bias_a, t0s, t1s,
                              (blk * 4, blk * 4 + 4), D)
        if split and y4bf is not None:
            r0, r1 = blk * 4 * 128, (blk + 1) * 4 * 128
            y4flat = bufs["y4flat"]
            p.op("pool", lambda e, r0=r0, r1=r1, y4flat=y4flat: e.dma_start(
                out=y4bf[r0:r1, :], in_=y4flat[r0:r1, :]),
                reads=tuple(("y4w", blk * 4 + tt) for tt in range(4)),
                writes=(("y4bfw", blk),), inc=16)
            for tt in range(4):
                tile = blk * 4 + tt
                for j in range(8):
                    p.op("sp", lambda e, tile=tile, tt=tt, j=j: e.dma_start_transpose(
                        yfm[:, j, tt * 128:(tt + 1) * 128],
                        y4bf[tile * 128:(tile + 1) * 128, j * 128:(j + 1) * 128]),
                        reads=(("y4bfw", blk),), writes=(("b4yfm", j, tt),), inc=16)
        else:
            for tt in range(4):
                tile = blk * 4 + tt
                yt = ytm[tt % 2]
                y4v = y4t[:, tt % 2, :]
                if split:
                    p.op("pool", lambda e, tile=tile, yt=yt: e.dma_start(out=yt[:], in_=y4_out[tile, :, :]),
                         reads=(("ytm", tt % 2), ("y4w", tile)), writes=(("ytm", tt % 2),), inc=16)
                else:
                    emit_ln_norm(p, X[:, tile, :], (("X", tile),), y4v, (("y4t", tt % 2),),
                                 rstd_a, bias_a, tile, tag=f"b4 ln {tile}")
                    p.op("pool", lambda e, tile=tile, y4v=y4v: e.dma_start(out=y4_out[tile, :, :], in_=y4v),
                         reads=(("y4t", tt % 2),), writes=(("y4w", tile),), inc=16)
                    p.op("dve", lambda e, yt=yt, y4v=y4v: e.tensor_copy(yt[:], y4v),
                         reads=(("y4t", tt % 2),), writes=(("ytm", tt % 2),))
                for j in range(8):
                    transpose_into(yt[:, j * 128:(j + 1) * 128], (("ytm", tt % 2),),
                                   yfm[:, j, tt * 128:(tt + 1) * 128], (("b4yfm", j, tt),),
                                   tt * 8 + j)
        for tt in range(4):
            tile = blk * 4 + tt
            sl = tt % SLW
            s1 = sc1[:, sl, :]
            s2 = sc2[:, sl, :]
            gb = gbuf[:, sl, :]
            shb = shbf[:, sl, :]
            ft = ftile[:, sl, :]
            if NPJ >= 6:
                bk1, bk2, bk3 = tt % 2, 2 + tt % 2, 4
            else:
                bk1, bk2, bk3 = 0, 1, 2
            for w_, wname, bank in ((sw1T, "sw1", bk1), (sw2T, "sw2", bk2)):
                pk = pj[bank]
                for j in range(8):
                    p.op("pe",
                         lambda e, j=j, tt=tt, pk=pk, w_=w_: e.matmul(
                             pk[:], yfm[:, j, tt * 128:(tt + 1) * 128], w_[:, j, :],
                             start=(j == 0), stop=(j == 7)),
                         reads=(("WT", wname, j), ("b4yfm", j, tt)), writes=(("pj", bank),))
            p.op("dve", lambda e, s1=s1, bk1=bk1: e.tensor_tensor(s1, pj[bk1][:], sb1bc[:], ALU.add),
                 reads=(("pj", bk1), ("b4c",)), writes=(("sc1", sl),))
            p.op("act", lambda e, s1=s1: e.activation(out=s1, in_=s1, func=AF.Silu),
                 reads=(("sc1", sl),), writes=(("sc1", sl),))
            p.op("dve", lambda e, s2=s2, bk2=bk2: e.tensor_tensor(s2, pj[bk2][:], sb2bc[:], ALU.add),
                 reads=(("pj", bk2), ("b4c",)), writes=(("sc2", sl),))
            p.op("dve", lambda e, s1=s1, s2=s2, gb=gb: e.tensor_tensor(gb, s1, s2, ALU.mult),
                 reads=(("sc1", sl), ("sc2", sl)), writes=(("gbuf", tt),))
            scl = sl % scr.shape[1]
            p.op("act",
                 lambda e, tile=tile, gb=gb, scl=scl: e.activation(out=scr[:, scl, 0:512], in_=gb, func=AF.Square,
                                                                   accum_out=ssq_b[:, tile:tile + 1]),
                 reads=(("gbuf", tt),), writes=(("lnscr", scl), ("ssq", id(ssq_b), tile)))
            p.op("act",
                 lambda e, tile=tile, gb=gb, scl=scl: e.activation(out=scr[:, scl, 0:512], in_=gb, func=AF.Identity,
                                                                   accum_out=sum_b[:, tile:tile + 1]),
                 reads=(("gbuf", tt),), writes=(("lnscr", scl), ("lsum", id(sum_b), tile)))
            emit_ln_finalize2(p, sum_b, ssq_b, rstd_b, bias_b, t0s, t1s,
                              (tile, tile + 1), 512)
            emit_ln_norm(p, gb, (("gbuf", tt),), s1, (("sc1", sl),),
                         rstd_b, bias_b, tile)
            p.op("dve", lambda e, s1=s1: e.tensor_tensor(s1, s1, slgbc[:], ALU.mult),
                 reads=(("sc1", sl), ("b4c",)), writes=(("sc1", sl),))
            p.op("dve", lambda e, s1=s1, shb=shb: e.tensor_tensor(shb, s1, slbbc[:], ALU.add),
                 reads=(("sc1", sl), ("b4c",)), writes=(("shbf", sl),))
            for c in range(4):
                transpose_into(shb[:, c * 128:(c + 1) * 128], (("shbf", sl),),
                               shfm[:, sl, c, :], (("shfm", sl, c),), c)
            for half in range(2):
                bank = bk3 + half
                pk = pj[bank]
                for c in range(4):
                    p.op("pe",
                         lambda e, c=c, half=half, pk=pk, sl=sl: e.matmul(
                             pk[:], shfm[:, sl, c, :], sw3T[:, c, half * 512:(half + 1) * 512],
                             start=(c == 0), stop=(c == 3)),
                         reads=(("WT", "sw3", c), ("shfm", sl, c)), writes=(("pj", bank),))
                p.op("dve",
                     lambda e, half=half, pk=pk, ft=ft: e.tensor_tensor(
                         ft[:, half * 512:(half + 1) * 512], pk[:],
                         sb3bc[:, half * 512:(half + 1) * 512], ALU.add),
                     reads=(("pj", bank), ("b4c",)), writes=(("ftile", sl, half),))
            p.op("pool", lambda e, tile=tile, ft=ft: e.dma_start(out=f_out[tile, :, :], in_=ft),
                 reads=(("ftile", sl, 0), ("ftile", sl, 1)), writes=(("fw", tile),), inc=16)
            f2 = bufs.get("f_out2")
            if f2 is not None:
                p.op("sp", lambda e, tile=tile, ft=ft, f2=f2: e.dma_start(out=f2[tile, :, :], in_=ft),
                     reads=(("ftile", sl, 0), ("ftile", sl, 1)), writes=(("fw2", tile),), inc=16)


def build_b4a_program():
    nc = bass.Bass("TRN2", target_bir_lowering=False)
    stack = contextlib.ExitStack()
    x_d = nc.dram_tensor("x", [NTILE, 128, D], F32, kind="ExternalInput").ap()
    sw1_d = nc.dram_tensor("sw1", [512, D], F32, kind="ExternalInput").ap()
    sb1_d = nc.dram_tensor("sb1", [512], F32, kind="ExternalInput").ap()
    sw2_d = nc.dram_tensor("sw2", [512, D], F32, kind="ExternalInput").ap()
    sb2_d = nc.dram_tensor("sb2", [512], F32, kind="ExternalInput").ap()
    sw3_d = nc.dram_tensor("sw3", [D, 512], F32, kind="ExternalInput").ap()
    sb3_d = nc.dram_tensor("sb3", [D], F32, kind="ExternalInput").ap()
    slg_d = nc.dram_tensor("slg", [512], F32, kind="ExternalInput").ap()
    slb_d = nc.dram_tensor("slb", [512], F32, kind="ExternalInput").ap()
    y4_out = nc.dram_tensor("y4", [NTILE, 128, D], F32, kind="ExternalOutput").ap()
    f_out = nc.dram_tensor("f", [NTILE, 128, D], F32, kind="ExternalOutput").ap()
    scr_w = {n: nc.dram_tensor(f"wscr_{n}", [sh[0], sh[1]], BF16, kind="Internal").ap()
             for n, sh in (("sw1", (512, D)), ("sw2", (512, D)), ("sw3", (D, 512)))}

    p = Prog(nc)
    sb = lambda name, shape, dt: stack.enter_context(nc.sbuf_tensor(name, shape, dt))
    ps = lambda name, shape, dt: stack.enter_context(nc.psum_tensor(name, shape, dt))

    bufs = {}
    bufs["X"] = sb("X", [128, NTILE, D], BF16)
    bufs["sw1T"] = sb("sw1T", [128, 8, 512], BF16)
    bufs["sw2T"] = sb("sw2T", [128, 8, 512], BF16)
    bufs["sw3T"] = sb("sw3T", [128, 4, D], BF16)
    bufs["b4_yfm"] = sb("b4_yfm", [128, 8, 512], BF16)
    bufs["shfm"] = sb("shfm", [128, 2, 4, 128], BF16)
    bufs["sc1"] = sb("sc1", [128, 2, 512], F32)
    bufs["sc2"] = sb("sc2", [128, 2, 512], F32)
    bufs["gbuf"] = sb("gbuf", [128, 2, 512], F32)
    bufs["shbf"] = sb("shbf", [128, 2, 512], BF16)
    bufs["ftile"] = sb("ftile", [128, 2, D], F32)
    bufs["y4t"] = sb("y4t", [128, 2, D], F32)
    bufs["scr"] = sb("scr", [128, 2, D], BF16)
    bufs["ytm"] = [sb(f"ytm{i}", [128, D], BF16) for i in range(2)]
    bufs["ident"] = sb("ident", [128, 128], BF16)
    bufs["sb1bc"] = sb("sb1bc", [128, 512], F32)
    bufs["sb2bc"] = sb("sb2bc", [128, 512], F32)
    bufs["sb3bc"] = sb("sb3bc", [128, D], F32)
    bufs["slgbc"] = sb("slgbc", [128, 512], F32)
    bufs["slbbc"] = sb("slbbc", [128, 512], F32)
    bufs["pj"] = [ps(f"pj{i}", [128, 512], F32) for i in range(4)]
    bufs["psumT"] = [ps(f"psT{i}", [128, 128], BF16) for i in range(2)]
    stats = {}
    for nmm in ("sum_a", "ssq_a", "rstd_a", "bias_a", "sum_b", "ssq_b", "rstd_b", "bias_b", "t0", "t1"):
        stats[nmm] = sb("st_" + nmm, [128, NTILE], F32)
    bufs["stats"] = stats

    p.op("pool", lambda e: e.memset(bufs["ident"][:], 1.0), writes=(("ident",),))
    p.op("pool", lambda e: e.affine_select(bufs["ident"][:], bufs["ident"][:],
                                           [[-1, 128]], ALU.is_equal, 0.0,
                                           base=0, channel_multiplier=1),
         reads=(("ident",),), writes=(("ident",),))
    for bc, dd in ((bufs["sb1bc"], sb1_d), (bufs["sb2bc"], sb2_d), (bufs["sb3bc"], sb3_d),
                   (bufs["slgbc"], slg_d), (bufs["slbbc"], slb_d)):
        p.op("pool", lambda e, bc=bc, dd=dd: e.dma_start(
            out=bc[:, :], in_=dd[None, :].broadcast_to([128, dd.shape[0]])),
            writes=(("b4c",),), inc=16)
    wtmps = [bufs["ytm"][0], bufs["ytm"][1]]
    wtkeys = [("ytm", 0), ("ytm", 1)]
    emit_weight_load_T(p, sw1_d, scr_w["sw1"], bufs["sw1T"], "sw1", 512, D, wtmps, wtkeys)
    emit_weight_load_T(p, sw2_d, scr_w["sw2"], bufs["sw2T"], "sw2", 512, D, wtmps, wtkeys)
    emit_weight_load_T(p, sw3_d, scr_w["sw3"], bufs["sw3T"], "sw3", D, 512, wtmps, wtkeys)
    for t in range(NTILE):
        p.op("pool", lambda e, t=t: e.dma_start(out=bufs["X"][:, t, :], in_=x_d[t, :, :]),
             writes=(("X", t),), inc=16)

    emit_block4_shared(p, nc, bufs, y4_out, f_out)

    p.build(stack)
    return nc, stack





def emit_block4b(p, nc, B, dr, fused=False, hook=None):
    ibs = B['ibs']
    its = B['its']
    flat = B['flat']
    xm = B['xm']
    xmfm = B['xmfm']
    xmbf = B['xmbf']
    gsb = B['gsb']
    GT = B['GT']
    bias9 = B['bias9']
    expbc = B['expbc']
    sig = B['sig']
    tmp8 = B['tmp8']
    mx1 = B['mx1']
    oh = B['oh']
    ohT = B['ohT']
    t1sc = B['t1sc']
    wsh = B['wsh']
    wex = B['wex']
    identf = B['identf']
    identb = B['identb']
    ew1T = B['ew1T']
    ew2T = B['ew2T']
    ew3T = B['ew3T']
    eb1bc = B['eb1bc']
    eb2bc = B['eb2bc']
    eb3sb = B['eb3sb']
    selgbc = B['selgbc']
    selbbc = B['selbbc']
    sc1 = B['sc1']
    sc2 = B['sc2']
    gbuf = B['gbuf']
    zm = B['zm']
    zmfm = B['zmfm']
    epart = B['epart']
    scrd = B['scrd']
    sum_b = B['sum_b']
    ssq_b = B['ssq_b']
    rstd_b = B['rstd_b']
    bias_b = B['bias_b']
    t0s = B['t0s']
    t1s = B['t1s']
    pT = B['pT']
    ph = B['ph']
    pout = B['pout']
    wtmpb = B['wtmpb']
    y4_d = dr.get('y4_d')
    ib_d = dr.get('ib_d')
    it_d = dr.get('it_d')
    sgw_d = dr.get('sgw_d')
    sgb_d = dr.get('sgb_d')
    egw_d = dr.get('egw_d')
    egb_d = dr.get('egb_d')
    expb_d = dr.get('expb_d')
    ew1_d = dr.get('ew1_d')
    eb1_d = dr.get('eb1_d')
    ew2_d = dr.get('ew2_d')
    eb2_d = dr.get('eb2_d')
    ew3_d = dr.get('ew3_d')
    eb3_d = dr.get('eb3_d')
    selg_d = dr.get('selg_d')
    selb_d = dr.get('selb_d')
    ep_out = dr.get('ep_out')
    wsh_out = dr.get('wsh_out')
    ew1scr = dr.get('ew1scr')
    ew2scr = dr.get('ew2scr')
    ew3scr = dr.get('ew3scr')
    MSL = 256
    p.op("pool", lambda e: e.memset(identf[:], 1.0), writes=(("identf",),))
    p.op("pool", lambda e: e.affine_select(identf[:], identf[:], [[-1, 128]],
                                           ALU.is_equal, 0.0, base=0, channel_multiplier=1),
         reads=(("identf",),), writes=(("identf",),))
    p.op("pool", lambda e: e.memset(identb[:], 1.0), writes=(("identb",),))
    p.op("pool", lambda e: e.affine_select(identb[:], identb[:], [[-1, 128]],
                                           ALU.is_equal, 0.0, base=0, channel_multiplier=1),
         reads=(("identb",),), writes=(("identb",),))
    p.op("pool", lambda e: e.dma_start(out=bias9[:, 0:1], in_=sgb_d[None, :].broadcast_to([128, 1])),
         writes=(("b4c",),), inc=16)
    p.op("pool", lambda e: e.dma_start(out=bias9[:, 1:9], in_=egb_d[None, :].broadcast_to([128, 8])),
         writes=(("b4c",),), inc=16)
    p.op("pool", lambda e: e.dma_start(out=expbc[:, :], in_=expb_d[None, :].broadcast_to([128, 8])),
         writes=(("b4c",),), inc=16)
    p.op("pool", lambda e: e.dma_start(out=eb1bc[:, :, :], in_=eb1_d[None, :, :].broadcast_to([128, 8, 512])),
         writes=(("b4c",),), inc=16)
    p.op("pool", lambda e: e.dma_start(out=eb2bc[:, :, :], in_=eb2_d[None, :, :].broadcast_to([128, 8, 512])),
         writes=(("b4c",),), inc=16)
    p.op("pool", lambda e: e.dma_start(out=eb3sb[:, :], in_=eb3_d[:, :]),
         writes=(("b4c",),), inc=16)
    p.op("pool", lambda e: e.dma_start(out=selgbc[:, :], in_=selg_d[None, :].broadcast_to([128, 512])),
         writes=(("b4c",),), inc=16)
    p.op("pool", lambda e: e.dma_start(out=selbbc[:, :], in_=selb_d[None, :].broadcast_to([128, 512])),
         writes=(("b4c",),), inc=16)
    p.op("pool", lambda e: e.memset(gsb[:], 0.0), writes=(("gsb",),))
    p.op("sp", lambda e: e.dma_start(out=gsb[0:1, :], in_=sgw_d[:, :]),
         reads=(("gsb",),), writes=(("gsb",),), inc=16)
    p.op("sp", lambda e: e.dma_start(out=gsb[1:9, :], in_=egw_d[:, :]),
         reads=(("gsb",),), writes=(("gsb",),), inc=16)
    for j in range(8):
        pt = pT[0]
        p.op("pe", lambda e, j=j, pt=pt: e.transpose(pt[:], gsb[:, j * 128:(j + 1) * 128], identf[:]),
             reads=(("gsb",), ("identf",)), writes=(("pT", 0),))
        p.op("dve", lambda e, j=j, pt=pt: e.tensor_copy(GT[:, j, :], pt[:, 0:9]),
             reads=(("pT", 0),), writes=(("GT",),))
    if fused:
        for mt in range(2):
            p.op("sp", lambda e, mt=mt: e.dma_start(out=xm[:, mt, :], in_=dr["ccrs"][mt * 128:(mt + 1) * 128, :]),
                 reads=(("ccrs",),), writes=(("xm", mt),), inc=16)
    else:
        p.op("sp", lambda e: e.dma_start(out=ibs[:, 0], in_=ib_d[0:128]), writes=(("idx",),), inc=16)
        p.op("sp", lambda e: e.dma_start(out=ibs[:, 1], in_=ib_d[128:256]), writes=(("idx",),), inc=16)
        p.op("sp", lambda e: e.dma_start(out=its[:, 0], in_=it_d[0:128]), writes=(("idx",),), inc=16)
        p.op("sp", lambda e: e.dma_start(out=its[:, 1], in_=it_d[128:256]), writes=(("idx",),), inc=16)
        p.op("dve", lambda e: e.tensor_scalar(flat[:], ibs[:], T, None, ALU.mult),
             reads=(("idx",),), writes=(("flat",),))
        p.op("dve", lambda e: e.tensor_tensor(flat[:], flat[:], its[:], ALU.add),
             reads=(("flat",), ("idx",)), writes=(("flat",),))
        for mt in range(2):
            p.op("pool",
                 lambda e, mt=mt: e.indirect_dma_start(
                     out=xm[:, mt, :], out_offset=None,
                     in_=y4_d[:, :], in_offset=IndirectOffsetOnAxis(flat[:, mt:mt + 1], 0)),
                 reads=(("flat",),), writes=(("xm", mt),), inc=16)
    for mt in range(2):
        for j in range(8):
            pt = pT[0]
            p.op("pe", lambda e, mt=mt, j=j, pt=pt: e.transpose(
                pt[:], xm[:, mt, j * 128:(j + 1) * 128], identf[:]),
                reads=(("xm", mt), ("identf",)), writes=(("pT", 0),))
            p.op("dve", lambda e, mt=mt, j=j, pt=pt: e.tensor_copy(
                xmfm[:, j, mt * 128:(mt + 1) * 128], pt[:]),
                reads=(("pT", 0),), writes=(("xmfm", j),))
    p.op("dve", lambda e: e.tensor_copy(xmbf[:], xmfm[:]),
         reads=tuple(("xmfm", j) for j in range(8)), writes=(("xmbf",),))
    for mt in range(2):
        pg = ph[mt]
        for j in range(8):
            p.op("pe", lambda e, mt=mt, j=j, pg=pg: e.matmul(
                pg[:, 0:9], xmfm[:, j, mt * 128:(mt + 1) * 128], GT[:, j, :],
                start=(j == 0), stop=(j == 7)),
                reads=(("xmfm", j), ("GT",)), writes=(("ph", mt),))
        p.op("dve", lambda e, mt=mt, pg=pg: e.tensor_tensor(sig[:, mt, :], pg[:, 0:9], bias9[:], ALU.add),
             reads=(("ph", mt), ("b4c",)), writes=(("sig", mt),))
        p.op("act", lambda e, mt=mt: e.activation(out=sig[:, mt, :], in_=sig[:, mt, :], func=AF.Sigmoid),
             reads=(("sig", mt),), writes=(("sig", mt),))
        p.op("dve", lambda e, mt=mt: e.tensor_tensor(tmp8[:], sig[:, mt, 1:9], expbc[:], ALU.add),
             reads=(("sig", mt), ("b4c",)), writes=(("tmp8",),))
        p.op("dve", lambda e: e.tensor_reduce(mx1[:], tmp8[:], AX.X, ALU.max),
             reads=(("tmp8",),), writes=(("mx1",),))
        p.op("dve", lambda e, mt=mt: e.tensor_scalar(oh[:, mt, :], tmp8[:], mx1[:], None, ALU.is_equal),
             reads=(("tmp8",), ("mx1",)), writes=(("oh", mt),))
        p.op("dve", lambda e, mt=mt: e.tensor_tensor(tmp8[:], oh[:, mt, :], sig[:, mt, 1:9], ALU.mult),
             reads=(("oh", mt), ("sig", mt)), writes=(("tmp8",),))
        p.op("dve", lambda e: e.tensor_reduce(t1sc[:], tmp8[:], AX.X, ALU.add),
             reads=(("tmp8",),), writes=(("t1sc",),))
        p.op("dve", lambda e, mt=mt: e.tensor_tensor(wsh[:, mt:mt + 1], sig[:, mt, 0:1], t1sc[:], ALU.subtract),
             reads=(("sig", mt), ("t1sc",)), writes=(("wsh",),))
        p.op("act", lambda e, mt=mt: e.activation(out=wsh[:, mt:mt + 1], in_=wsh[:, mt:mt + 1], func=AF.Sigmoid),
             reads=(("wsh",),), writes=(("wsh",),))
        p.op("dve", lambda e, mt=mt: e.tensor_scalar(wex[:, mt:mt + 1], wsh[:, mt:mt + 1], -1.0, 1.0, ALU.mult, ALU.add),
             reads=(("wsh",),), writes=(("wex",),))
        pt = pT[0]
        p.op("pe", lambda e, mt=mt, pt=pt: e.transpose(pt[0:8, 0:128], oh[:, mt, :], identf[:]),
             reads=(("oh", mt), ("identf",)), writes=(("pT", 0),))
        p.op("dve", lambda e, mt=mt, pt=pt: e.tensor_copy(ohT[:, mt, :], pt[0:8, 0:128]),
             reads=(("pT", 0),), writes=(("ohT",),))
    if not fused:
        for src_d, scr_ in ((ew1_d, ew1scr), (ew2_d, ew2scr), (ew3_d, ew3scr)):
            for ei in range(8):
                n = scr_.shape[0] // 8
                p.op("pool", lambda e, ei=ei, src_d=src_d, scr_=scr_, n=n: e.dma_start(
                    out=scr_[ei * n:(ei + 1) * n, :], in_=src_d[ei * n:(ei + 1) * n, :]),
                    reads=(), writes=(("wscrb", id(scr_), ei),), inc=16)

    for ei in range(8):
        if hook is not None:
            hook(ei)
        par = ei % 2
        w1, w2, w3 = ew1T[par], ew2T[par], ew3T[par]
        for j in range(8):
            if not (fused and ei == 0):
                p.op("sp", lambda e, ei=ei, j=j, w1=w1: e.dma_start_transpose(
                    w1[:, j, :], ew1scr[ei * 512:(ei + 1) * 512, j * 128:(j + 1) * 128]),
                    reads=(("wscrb", id(ew1scr), ei), ("ew1T", par)), writes=(("ew1T", par),), inc=16)
            p.op("sp", lambda e, ei=ei, j=j, w2=w2: e.dma_start_transpose(
                w2[:, j, :], ew2scr[ei * 512:(ei + 1) * 512, j * 128:(j + 1) * 128]),
                reads=(("wscrb", id(ew2scr), ei), ("ew2T", par)), writes=(("ew2T", par),), inc=16)
        for c in range(4):
            p.op("sp", lambda e, ei=ei, c=c, w3=w3: e.dma_start_transpose(
                w3[:, c, :], ew3scr[ei * D:(ei + 1) * D, c * 128:(c + 1) * 128]),
                reads=(("wscrb", id(ew3scr), ei), ("ew3T", par)), writes=(("ew3T", par),), inc=16)
        for mt in range(2):
            sl = (ei * 2 + mt) % 2
            s1 = sc1[:, sl, :]
            s2 = sc2[:, sl, :]
            gb = gbuf[:, sl, :]
            zmv = zm[:, sl, :]
            for w_, wkey, bank in ((w1, ("ew1T", par), 0), (w2, ("ew2T", par), 1)):
                pk = ph[bank]
                for j in range(8):
                    p.op("pe", lambda e, j=j, mt=mt, pk=pk, w_=w_: e.matmul(
                        pk[:], xmbf[:, j, mt * 128:(mt + 1) * 128], w_[:, j, :],
                        start=(j == 0), stop=(j == 7)),
                        reads=(("xmbf",), wkey), writes=(("ph", bank),))
            p.op("dve", lambda e, ei=ei, s1=s1: e.tensor_tensor(s1, ph[0][:], eb1bc[:, ei, :], ALU.add),
                 reads=(("ph", 0), ("b4c",)), writes=(("sc1", sl),))
            p.op("act", lambda e, s1=s1: e.activation(out=s1, in_=s1, func=AF.Silu),
                 reads=(("sc1", sl),), writes=(("sc1", sl),))
            p.op("dve", lambda e, ei=ei, s2=s2: e.tensor_tensor(s2, ph[1][:], eb2bc[:, ei, :], ALU.add),
                 reads=(("ph", 1), ("b4c",)), writes=(("sc2", sl),))
            p.op("dve", lambda e, s1=s1, s2=s2, gb=gb: e.tensor_tensor(gb, s1, s2, ALU.mult),
                 reads=(("sc1", sl), ("sc2", sl)), writes=(("gbuf", sl),))
            stcol = ei * 2 + mt
            p.op("act", lambda e, gb=gb, stcol=stcol: e.activation(
                out=scrd[:, 0, :], in_=gb, func=AF.Square,
                accum_out=ssq_b[:, stcol:stcol + 1]),
                 reads=(("gbuf", sl),), writes=(("lnscr", sl), ("ssq", id(ssq_b), stcol)))
            p.op("act", lambda e, gb=gb, stcol=stcol: e.activation(
                out=scrd[:, 0, :], in_=gb, func=AF.Identity,
                accum_out=sum_b[:, stcol:stcol + 1]),
                 reads=(("gbuf", sl),), writes=(("lnscr", sl), ("lsum", id(sum_b), stcol)))
            emit_ln_finalize2(p, sum_b, ssq_b, rstd_b, bias_b, t0s, t1s, (stcol, stcol + 1), 512)
            emit_ln_norm(p, gb, (("gbuf", sl),), s1, (("sc1", sl),), rstd_b, bias_b, stcol)
            p.op("dve", lambda e, s1=s1: e.tensor_tensor(s1, s1, selgbc[:], ALU.mult),
                 reads=(("sc1", sl), ("b4c",)), writes=(("sc1", sl),))
            p.op("dve", lambda e, s1=s1, s2=s2: e.tensor_tensor(s2, s1, selbbc[:], ALU.add),
                 reads=(("sc1", sl), ("b4c",)), writes=(("sc2", sl),))
            p.op("dve", lambda e, ei=ei, mt=mt, s2=s2, zmv=zmv: e.tensor_scalar(
                zmv, s2, oh[:, mt, ei:ei + 1], None, ALU.mult),
                reads=(("sc2", sl), ("oh", mt)), writes=(("zm", sl),))
            for c in range(4):
                pt = pT[1]
                p.op("pe", lambda e, c=c, pt=pt, zmv=zmv: e.transpose(pt[:], zmv[:, c * 128:(c + 1) * 128], identb[:]),
                     reads=(("zm", sl), ("identb",)), writes=(("pT", 1),))
                p.op("dve", lambda e, c=c, pt=pt, sl=sl: e.tensor_copy(zmfm[:, sl, c, :], pt[:]),
                     reads=(("pT", 1),), writes=(("zmfm", sl, c),))
            for half in range(2):
                bank = mt * 2 + half
                po = pout[bank]
                if ei == 0:
                    p.op("pe", lambda e, mt=mt, half=half, po=po: e.matmul(
                        po[:], ohT[:, mt, :], eb3sb[:, half * 512:(half + 1) * 512],
                        start=True, stop=False, skip_group_check=True),
                        reads=(("ohT",), ("b4c",)), writes=(("pout", bank),))
                for c in range(4):
                    p.op("pe", lambda e, c=c, ei=ei, half=half, po=po, sl=sl: e.matmul(
                        po[:], zmfm[:, sl, c, :], ew3T[ei % 2][:, c, half * 512:(half + 1) * 512],
                        start=False, stop=(ei == 7 and c == 3), skip_group_check=True),
                        reads=(("zmfm", sl, c), ("ew3T", ei % 2)), writes=(("pout", bank),))
    for mt in range(2):
        for half in range(2):
            bank = mt * 2 + half
            p.op("dve", lambda e, mt=mt, half=half, bank=bank: e.tensor_scalar(
                epart[:, half * 512:(half + 1) * 512], pout[bank][:], wex[:, mt:mt + 1], None, ALU.mult),
                reads=(("pout", bank), ("wex",)), writes=(("epart", half),))
        p.op("pool", lambda e, mt=mt: e.dma_start(out=dr["ep_out"][mt, :, :], in_=epart[:]),
             reads=(("epart", 0), ("epart", 1)), writes=(("epart", 0), ("epart", 1)), inc=16)
    for mt in range(2):
        p.op("pool", lambda e, mt=mt: e.dma_start(out=dr["wsh_out"][mt, :], in_=wsh[:, mt]),
             reads=(("wsh",),), writes=(), inc=16)


def build_b4b_program():
    nc = bass.Bass("TRN2", target_bir_lowering=False)
    stack = contextlib.ExitStack()
    MSL = 256
    y4_d = nc.dram_tensor("y4full", [16 * T, D], F32, kind="ExternalInput").ap()
    ib_d = nc.dram_tensor("ib", [MSL], I32, kind="ExternalInput").ap()
    it_d = nc.dram_tensor("it", [MSL], I32, kind="ExternalInput").ap()
    sgw_d = nc.dram_tensor("sgw", [1, D], F32, kind="ExternalInput").ap()
    sgb_d = nc.dram_tensor("sgb", [1], F32, kind="ExternalInput").ap()
    egw_d = nc.dram_tensor("egw", [8, D], F32, kind="ExternalInput").ap()
    egb_d = nc.dram_tensor("egb", [8], F32, kind="ExternalInput").ap()
    expb_d = nc.dram_tensor("expb", [8], F32, kind="ExternalInput").ap()
    ew1_d = nc.dram_tensor("ew1", [8 * 512, D], F32, kind="ExternalInput").ap()
    eb1_d = nc.dram_tensor("eb1", [8, 512], F32, kind="ExternalInput").ap()
    ew2_d = nc.dram_tensor("ew2", [8 * 512, D], F32, kind="ExternalInput").ap()
    eb2_d = nc.dram_tensor("eb2", [8, 512], F32, kind="ExternalInput").ap()
    ew3_d = nc.dram_tensor("ew3", [8 * D, 512], F32, kind="ExternalInput").ap()
    eb3_d = nc.dram_tensor("eb3", [8, D], F32, kind="ExternalInput").ap()
    selg_d = nc.dram_tensor("selg", [512], F32, kind="ExternalInput").ap()
    selb_d = nc.dram_tensor("selb", [512], F32, kind="ExternalInput").ap()
    ep_out = nc.dram_tensor("epart", [2, 128, D], F32, kind="ExternalOutput").ap()
    wsh_out = nc.dram_tensor("wsh", [2, 128], F32, kind="ExternalOutput").ap()
    ew1scr = nc.dram_tensor("ew1scr", [8 * 512, D], BF16, kind="Internal").ap()
    ew2scr = nc.dram_tensor("ew2scr", [8 * 512, D], BF16, kind="Internal").ap()
    ew3scr = nc.dram_tensor("ew3scr", [8 * D, 512], BF16, kind="Internal").ap()

    p = Prog(nc)
    sb = lambda name, shape, dt: stack.enter_context(nc.sbuf_tensor(name, shape, dt))
    ps = lambda name, shape, dt: stack.enter_context(nc.psum_tensor(name, shape, dt))

    ibs = sb("ibs", [128, 2], I32)
    its = sb("its", [128, 2], I32)
    flat = sb("flat", [128, 2], I32)
    xm = sb("xm", [128, 2, D], F32)
    xmfm = sb("xmfm", [128, 8, MSL], F32)
    xmbf = sb("xmbf", [128, 8, MSL], BF16)
    gsb = sb("gsb", [128, D], F32)
    GT = sb("GT", [128, 8, 9], F32)
    bias9 = sb("bias9", [128, 9], F32)
    expbc = sb("expbc", [128, 8], F32)
    sig = sb("sig", [128, 2, 9], F32)
    tmp8 = sb("tmp8", [128, 8], F32)
    mx1 = sb("mx1", [128, 1], F32)
    oh = sb("oh", [128, 2, 8], F32)
    ohT = sb("ohT", [8, 2, 128], BF16)
    t1sc = sb("t1sc", [128, 1], F32)
    wsh = sb("wsh_sb", [128, 2], F32)
    wex = sb("wex", [128, 2], F32)
    identf = sb("identf", [128, 128], F32)
    identb = sb("identb", [128, 128], BF16)
    ew1T = [sb(f"ew1T{i}", [128, 8, 512], BF16) for i in range(2)]
    ew2T = [sb(f"ew2T{i}", [128, 8, 512], BF16) for i in range(2)]
    ew3T = [sb(f"ew3T{i}", [128, 4, D], BF16) for i in range(2)]
    eb1bc = sb("eb1bc", [128, 8, 512], BF16)
    eb2bc = sb("eb2bc", [128, 8, 512], BF16)
    eb3sb = sb("eb3sb", [8, D], BF16)
    selgbc = sb("selgbc", [128, 512], F32)
    selbbc = sb("selbbc", [128, 512], F32)
    sc1 = sb("sc1", [128, 2, 512], F32)
    sc2 = sb("sc2", [128, 2, 512], F32)
    gbuf = sb("gbuf", [128, 2, 512], F32)
    zm = sb("zm", [128, 2, 512], BF16)
    zmfm = sb("zmfm", [128, 2, 4, 128], BF16)
    epart = sb("epart_sb", [128, D], F32)
    scrd = sb("scrd", [128, 2, 512], BF16)
    stats = {}
    for nmm in ("sum_b", "ssq_b", "rstd_b", "bias_b", "t0", "t1"):
        stats[nmm] = sb("st_" + nmm, [128, 16], F32)
    sum_b, ssq_b = stats["sum_b"], stats["ssq_b"]
    rstd_b, bias_b = stats["rstd_b"], stats["bias_b"]
    t0s, t1s = stats["t0"], stats["t1"]

    pT = [ps("pT0", [128, 128], F32), ps("pT1", [128, 128], BF16)]
    ph = [ps(f"ph{i}", [128, 512], F32) for i in range(2)]
    pout = [ps(f"pout{i}", [128, 512], F32) for i in range(4)]
    wtmpb = sb("wtmpb", [128, D], BF16)

    B = {}
    for _n in ('ibs its flat xm xmfm xmbf gsb GT bias9 expbc sig tmp8 mx1 oh ohT t1sc wsh wex identf identb ew1T ew2T ew3T eb1bc eb2bc eb3sb selgbc selbbc sc1 sc2 gbuf zm zmfm epart scrd sum_b ssq_b rstd_b bias_b t0s t1s pT ph pout wtmpb').split():
        B[_n] = locals()[_n]
    dr = {'y4_d': y4_d, 'ib_d': ib_d, 'it_d': it_d, 'sgw_d': sgw_d, 'sgb_d': sgb_d,
          'egw_d': egw_d, 'egb_d': egb_d, 'expb_d': expb_d, 'ew1_d': ew1_d, 'eb1_d': eb1_d,
          'ew2_d': ew2_d, 'eb2_d': eb2_d, 'ew3_d': ew3_d, 'eb3_d': eb3_d, 'selg_d': selg_d,
          'selb_d': selb_d, 'ep_out': ep_out, 'wsh_out': wsh_out, 'ew1scr': ew1scr,
          'ew2scr': ew2scr, 'ew3scr': ew3scr}
    emit_block4b(p, nc, B, dr, fused=False)

    p.build(stack)
    return nc, stack


def emit_block4c(p, nc, B, dr, fused=False):
    X = B['X']
    ibs = B['ibs']
    its = B['its']
    flat = B['flat']
    flatf = B['flatf']
    own = B['own']
    tmpi = B['tmpi']
    basebc = B['basebc']
    wshs = B['wshs']
    fg = B['fg']
    ep = B['ep']
    comb = B['comb']
    ftile = B['ftile']
    scr = B['scr']
    lnf = B['lnf']
    sum_b = B['sum_b']
    ssq_b = B['ssq_b']
    rstd_b = B['rstd_b']
    bias_b = B['bias_b']
    t0s = B['t0s']
    t1s = B['t1s']
    x_d = dr.get('x_d')
    f_d = dr.get('f_d')
    ib_d = dr.get('ib_d')
    it_d = dr.get('it_d')
    ep_d = dr.get('ep_d')
    wsh_d = dr.get('wsh_d')
    base_d = dr.get('base_d')
    out_d = dr.get('out_d')
    ffix = dr.get('ffix')
    if not fused:
        for t in range(NTILE):
            p.op("pool", lambda e, t=t: e.dma_start(out=X[:, t, :], in_=x_d[t, :, :]),
                 writes=(("X", t),), inc=16)
    if not dr.get("skip_ffix_copy"):
        for t in range(NTILE):
            p.op("sp", lambda e, t=t: e.dma_start(out=ffix[t * 128:(t + 1) * 128, :],
                                                  in_=f_d[t * 128:(t + 1) * 128, :]),
                 writes=(("ffix", t),), inc=16)
    for k in range(16):
        p.op("sp", lambda e, k=k: e.dma_start(out=ibs[:, k], in_=ib_d[k * 128:(k + 1) * 128]),
             writes=(("idx",),), inc=16)
        p.op("sp", lambda e, k=k: e.dma_start(out=its[:, k], in_=it_d[k * 128:(k + 1) * 128]),
             writes=(("idx",),), inc=16)
        p.op("sp", lambda e, k=k: e.dma_start(out=wshs[:, k], in_=wsh_d[k * 128:(k + 1) * 128]),
             writes=(("wshs",),), inc=16)
    p.op("sp", lambda e: e.dma_start(out=basebc[:, :], in_=base_d[None, :].broadcast_to([128, 1])),
         writes=(("base",),), inc=16)
    p.op("dve", lambda e: e.tensor_scalar(flat[:], ibs[:], T, None, ALU.mult),
         reads=(("idx",),), writes=(("flat",),))
    p.op("dve", lambda e: e.tensor_tensor(flat[:], flat[:], its[:], ALU.add),
         reads=(("flat",), ("idx",)), writes=(("flat",),))
    p.op("dve", lambda e: e.tensor_copy(flatf[:], flat[:]),
         reads=(("flat",),), writes=(("flatf",),))
    p.op("dve", lambda e: e.tensor_scalar(flatf[:], flatf[:], basebc[:, 0:1], None, ALU.subtract),
         reads=(("flatf",), ("base",)), writes=(("flatf",),))
    p.op("dve", lambda e: e.tensor_scalar(own[:], flatf[:], 0.0, None, ALU.is_ge),
         reads=(("flatf",),), writes=(("own",),))
    p.op("dve", lambda e: e.tensor_scalar(tmpi[:], flatf[:], float(ROWS), None, ALU.is_lt),
         reads=(("flatf",),), writes=(("tmpi",),))
    p.op("dve", lambda e: e.tensor_tensor(own[:], own[:], tmpi[:], ALU.mult),
         reads=(("own",), ("tmpi",)), writes=(("own",),))
    p.op("dve", lambda e: e.tensor_tensor(flatf[:], flatf[:], own[:], ALU.mult),
         reads=(("flatf",), ("own",)), writes=(("flatf",),))
    p.op("dve", lambda e: e.tensor_scalar(tmpi[:], own[:], -float(ROWS), float(ROWS), ALU.mult, ALU.add),
         reads=(("own",),), writes=(("tmpi",),))
    p.op("dve", lambda e: e.tensor_tensor(flatf[:], flatf[:], tmpi[:], ALU.add),
         reads=(("flatf",), ("tmpi",)), writes=(("flatf",),))
    p.op("dve", lambda e: e.tensor_copy(flat[:], flatf[:]),
         reads=(("flatf",),), writes=(("flat",),))
    fkeys = tuple(("ffix", t) for t in range(NTILE))
    for k in range(16):
        sl = k % 2
        fgv = fg[:, sl, :]
        epv = ep[:, sl, :]
        cbv = comb[:, sl, :]
        p.op("pool",
             lambda e, k=k, fgv=fgv: e.indirect_dma_start(
                 out=fgv, out_offset=None,
                 in_=f_d[:, :], in_offset=IndirectOffsetOnAxis(flat[:, k:k + 1], 0)),
             reads=(("flat",), ("fg", sl)), writes=(("fg", sl),), inc=16)
        p.op("sp", lambda e, k=k, epv=epv: e.dma_start(out=epv, in_=ep_d[k * 128:(k + 1) * 128, :]),
             reads=(("ep", sl),), writes=(("ep", sl),), inc=16)
        p.op("dve",
             lambda e, k=k, fgv=fgv, epv=epv, cbv=cbv: e.scalar_tensor_tensor(
                 cbv, fgv, wshs[:, k:k + 1], epv, ALU.mult, ALU.add),
             reads=(("fg", sl), ("ep", sl), ("wshs",), ("comb", sl)), writes=(("comb", sl),))
        p.op("pool",
             lambda e, k=k, cbv=cbv: e.indirect_dma_start(
                 out=ffix[:, :], out_offset=IndirectOffsetOnAxis(flat[:, k:k + 1], 0),
                 in_=cbv, in_offset=None),
             reads=(("flat",), ("comb", sl)) + fkeys, writes=(("ffixs", k),), inc=16)
    skeys = tuple(("ffixs", k) for k in range(16))
    for pair in range(NTILE // 2):
        for pp in range(2):
            tile = pair * 2 + pp
            sl = tile % 2
            ftv = ftile[:, sl, :]
            p.op("sp", lambda e, tile=tile, ftv=ftv: e.dma_start(
                out=ftv, in_=ffix[tile * 128:(tile + 1) * 128, :]),
                 reads=skeys + (("ftile", sl),), writes=(("ftile", sl),), inc=16)
            p.op("act",
                 lambda e, tile=tile, ftv=ftv, sl=sl: e.activation(
                     out=scr[:, sl, :], in_=ftv, func=AF.Square,
                     accum_out=ssq_b[:, tile:tile + 1]),
                 reads=(("ftile", sl),), writes=(("lnscr", sl), ("ssq", id(ssq_b), tile)))
            p.op("act",
                 lambda e, tile=tile, ftv=ftv, sl=sl: e.activation(
                     out=scr[:, sl, :], in_=ftv, func=AF.Identity,
                     accum_out=sum_b[:, tile:tile + 1]),
                 reads=(("ftile", sl),), writes=(("lnscr", sl), ("lsum", id(sum_b), tile)))
        emit_ln_finalize2(p, sum_b, ssq_b, rstd_b, bias_b, t0s, t1s,
                          (pair * 2, pair * 2 + 2), D)
        for pp in range(2):
            tile = pair * 2 + pp
            sl = tile % 2
            ftv = ftile[:, sl, :]
            emit_ln_norm(p, ftv, (("ftile", sl),), lnf[:, sl, :], (("lnf", sl),),
                         rstd_b, bias_b, tile)
            p.op("pool",
                 lambda e, tile=tile, sl=sl: e.tensor_tensor(
                     X[:, tile, :], X[:, tile, :], lnf[:, sl, :], ALU.add),
                 reads=(("lnf", sl), ("X", tile)), writes=(("X", tile),))
            p.op("pool", lambda e, tile=tile: e.dma_start(out=out_d[tile, :, :], in_=X[:, tile, :]),
                 reads=(("X", tile),), writes=(), inc=16)


def build_b4c_program():
    nc = bass.Bass("TRN2", target_bir_lowering=False)
    stack = contextlib.ExitStack()
    M = 2048
    x_d = nc.dram_tensor("x", [NTILE, 128, D], F32, kind="ExternalInput").ap()
    f_d = nc.dram_tensor("fpad", [ROWS + 128, D], F32, kind="ExternalInput").ap()
    ib_d = nc.dram_tensor("ib", [M], I32, kind="ExternalInput").ap()
    it_d = nc.dram_tensor("it", [M], I32, kind="ExternalInput").ap()
    ep_d = nc.dram_tensor("epart", [M, D], F32, kind="ExternalInput").ap()
    wsh_d = nc.dram_tensor("wshall", [M], F32, kind="ExternalInput").ap()
    base_d = nc.dram_tensor("base", [1], F32, kind="ExternalInput").ap()
    out_d = nc.dram_tensor("out", [NTILE, 128, D], F32, kind="ExternalOutput").ap()
    ffix = nc.dram_tensor("ffix", [ROWS + 128, D], F32, kind="Internal").ap()

    p = Prog(nc)
    sb = lambda name, shape, dt: stack.enter_context(nc.sbuf_tensor(name, shape, dt))
    ps = lambda name, shape, dt: stack.enter_context(nc.psum_tensor(name, shape, dt))

    X = sb("X", [128, NTILE, D], BF16)
    ibs = sb("ibs", [128, 16], I32)
    its = sb("its", [128, 16], I32)
    flat = sb("flat", [128, 16], I32)
    flatf = sb("flatf", [128, 16], F32)
    own = sb("own", [128, 16], F32)
    tmpi = sb("tmpi", [128, 16], F32)
    basebc = sb("basebc", [128, 1], F32)
    wshs = sb("wshs", [128, 16], F32)
    fg = sb("fg", [128, 2, D], F32)
    ep = sb("ep", [128, 2, D], F32)
    comb = sb("comb", [128, 2, D], F32)
    ftile = sb("ftile", [128, 2, D], F32)
    scr = sb("scr", [128, 2, D], BF16)
    lnf = sb("lnf", [128, 2, D], BF16)
    stats = {}
    for nmm in ("sum_b", "ssq_b", "rstd_b", "bias_b", "t0", "t1"):
        stats[nmm] = sb("st_" + nmm, [128, NTILE], F32)
    sum_b, ssq_b = stats["sum_b"], stats["ssq_b"]
    rstd_b, bias_b = stats["rstd_b"], stats["bias_b"]
    t0s, t1s = stats["t0"], stats["t1"]

    B = {}
    for _n in ('X ibs its flat flatf own tmpi basebc wshs fg ep comb ftile scr lnf sum_b ssq_b rstd_b bias_b t0s t1s').split():
        B[_n] = locals()[_n]
    dr = {'x_d': x_d, 'f_d': f_d, 'ib_d': ib_d, 'it_d': it_d, 'ep_d': ep_d,
          'wsh_d': wsh_d, 'base_d': base_d, 'out_d': out_d, 'ffix': ffix}
    emit_block4c(p, nc, B, dr, fused=False)

    p.build(stack)
    return nc, stack


def build_b2_program():
    nc = bass.Bass("TRN2", target_bir_lowering=False)
    stack = contextlib.ExitStack()
    x_d = nc.dram_tensor("x", [NTILE, 128, D], F32, kind="ExternalInput").ap()
    wr_d = nc.dram_tensor("wr", [D, D], F32, kind="ExternalInput").ap()
    wk_d = nc.dram_tensor("wk", [D, D], F32, kind="ExternalInput").ap()
    wv_d = nc.dram_tensor("wv", [D, D], F32, kind="ExternalInput").ap()
    mur_d = nc.dram_tensor("mur", [D], F32, kind="ExternalInput").ap()
    muk_d = nc.dram_tensor("muk", [D], F32, kind="ExternalInput").ap()
    out_d = nc.dram_tensor("out", [NTILE, 128, D], F32, kind="ExternalOutput").ap()
    scr_w = {n: nc.dram_tensor(f"wscr_{n}", [D, D], BF16, kind="Internal").ap()
             for n in ("wr", "wk", "wv")}

    p = Prog(nc)
    sb = lambda name, shape, dt: stack.enter_context(nc.sbuf_tensor(name, shape, dt))
    ps = lambda name, shape, dt: stack.enter_context(nc.psum_tensor(name, shape, dt))

    bufs = {}
    bufs["X"] = sb("X", [128, NTILE, D], BF16)
    bufs["wrT"] = sb("wrT", [128, 8, D], BF16)
    bufs["wkT"] = sb("wkT", [128, 8, D], BF16)
    bufs["wvT"] = sb("wvT", [128, 8, D], BF16)
    bufs["yfm"] = sb("yfm", [128, 8, 513], BF16)
    bufs["xr"] = sb("xr", [128, 8, 512], BF16)
    bufs["xk"] = sb("xk", [128, 8, 512], BF16)
    bufs["kact"] = sb("kact", [128, 8, 512], BF16)
    bufs["scr"] = sb("scr", [128, D], F32)
    bufs["lnf"] = sb("lnf", [128, 2, D], F32)
    bufs["ytm"] = [sb(f"ytm{i}", [128, D], BF16) for i in range(2)]
    bufs["krelu"] = [sb(f"krelu{i}", [128, 512], F32) for i in range(2)]
    bufs["rgate"] = sb("rgate", [128, 2, D], F32)
    bufs["fblk"] = sb("fblk", [128, 4, D], F32)
    bufs["dif"] = [sb(f"dif{i}", [128, 512], BF16) for i in range(2)]
    bufs["ident"] = sb("ident", [128, 128], BF16)
    bufs["mur"] = sb("mur_sb", [128, 8], F32)
    bufs["muk"] = sb("muk_sb", [128, 8], F32)
    bufs["psum"] = [ps(f"pb{i}", [128, 512], F32) for i in range(7)]
    bufs["psumT"] = ps("psT", [128, 128], BF16)
    stats = {}
    for nmm in ("sum_a", "ssq_a", "rstd_a", "bias_a", "sum_b", "ssq_b", "rstd_b", "bias_b", "t0", "t1"):
        stats[nmm] = sb("st_" + nmm, [128, NTILE], F32)
    bufs["stats"] = stats
    p.op("pool", lambda e: e.memset(bufs["ident"][:], 1.0), writes=(("ident",),))
    p.op("pool", lambda e: e.affine_select(bufs["ident"][:], bufs["ident"][:],
                                           [[-1, 128]], ALU.is_equal, 0.0,
                                           base=0, channel_multiplier=1),
         reads=(("ident",),), writes=(("ident",),))
    for j in range(8):
        p.op("sp", lambda e, j=j: e.dma_start(out=bufs["mur"][:, j],
                                              in_=mur_d[j * 128:(j + 1) * 128]),
             writes=(("mu",),), inc=16)
        p.op("sp", lambda e, j=j: e.dma_start(out=bufs["muk"][:, j],
                                              in_=muk_d[j * 128:(j + 1) * 128]),
             writes=(("mu",),), inc=16)
    wtmps = [bufs["ytm"][0], bufs["ytm"][1]]
    wtkeys = [("ytm", 0), ("ytm", 1)]
    emit_weight_load_T(p, wr_d, scr_w["wr"], bufs["wrT"], "wr", D, D, wtmps, wtkeys)
    emit_weight_load_T(p, wk_d, scr_w["wk"], bufs["wkT"], "wk", D, D, wtmps, wtkeys)
    emit_weight_load_T(p, wv_d, scr_w["wv"], bufs["wvT"], "wv", D, D, wtmps, wtkeys)
    for t in range(NTILE):
        p.op("pool", lambda e, t=t: e.dma_start(out=bufs["X"][:, t, :], in_=x_d[t, :, :]),
             writes=(("X", t),), inc=16)

    emit_block2(p, nc, bufs, {})

    for t in range(NTILE):
        p.op("pool", lambda e, t=t: e.dma_start(out=out_d[t, :, :], in_=bufs["X"][:, t, :]),
             reads=(("X", t),), writes=(), inc=16)

    p.build(stack)
    return nc, stack


def _shard_rows(a):
    B = a.shape[0]
    rest = a.shape[2:]
    a = np.ascontiguousarray(a.reshape(8, B_LOC * T, *rest))
    return [np.ascontiguousarray(a[i].reshape(NTILE, 128, *rest)) for i in range(8)]





def build_fused_program():
    nc = bass.Bass("TRN2", target_bir_lowering=False)
    outer = contextlib.ExitStack()
    di = lambda n, sh: nc.dram_tensor(n, sh, F32, kind="ExternalInput").ap()
    x_d = di("x", [NTILE, 128, D])
    wq1, wk1, wv1, wo1 = di("wq1", [D, D]), di("wk1", [D, D]), di("wv1", [D, D]), di("wo1", [D, D])
    bo1 = di("bo1", [D])
    wr2, wk2b, wv2b = di("wr2", [D, D]), di("wk2b", [D, D]), di("wv2b", [D, D])
    mur2, muk2 = di("mur2", [D]), di("muk2", [D])
    wq3, wk3, wv3 = di("wq3", [D, D]), di("wk3", [D, D]), di("wv3", [D, D])
    bq3, bk3, bv3, beta3 = di("bq3", [D]), di("bk3", [D]), di("bv3", [D]), di("beta3", [8])
    sw1_d, sw2_d = di("sw1", [512, D]), di("sw2", [512, D])
    sb1_d, sb2_d = di("sb1", [512]), di("sb2", [512])
    sw3_d, sb3_d = di("sw3", [D, 512]), di("sb3", [D])
    slg_d, slb_d = di("slg", [512]), di("slb", [512])
    sgw_d, sgb_d = di("sgw", [1, D]), di("sgb", [1])
    egw_d, egb_d, expb_d = di("egw", [8, D]), di("egb", [8]), di("expb", [8])
    ew1_d, eb1_d = di("ew1", [8 * 512, D]), di("eb1", [8, 512])
    ew2_d, eb2_d = di("ew2", [8 * 512, D]), di("eb2", [8, 512])
    ew3_d, eb3_d = di("ew3", [8 * D, 512]), di("eb3", [8, D])
    selg_d, selb_d = di("selg", [512]), di("selb", [512])
    ib_d = nc.dram_tensor("ib", [2048], I32, kind="ExternalInput").ap()
    it_d = nc.dram_tensor("it", [2048], I32, kind="ExternalInput").ap()
    base_d = di("basef", [1])
    out_d = nc.dram_tensor("out", [ROWS + 128, D], F32, kind="ExternalOutput").ap()
    out_view = out_d[0:ROWS, :].rearrange("(t p) c -> t p c", p=128)

    scrw = {}
    for n, sh in (("wq1", (D, D)), ("wk1", (D, D)), ("wv1", (D, D)), ("wo1", (D, D)),
                  ("wr2", (D, D)), ("wk2b", (D, D)), ("wv2b", (D, D)),
                  ("sw1", (512, D)), ("sw2", (512, D)), ("sw3", (D, 512))):
        scrw[n] = nc.dram_tensor("scr_" + n, list(sh), BF16, kind="Internal").ap()
    ew1scr = nc.dram_tensor("ew1scr", [8 * 512, D], BF16, kind="Internal").ap()
    ew2scr = nc.dram_tensor("ew2scr", [8 * 512, D], BF16, kind="Internal").ap()
    ew3scr = nc.dram_tensor("ew3scr", [8 * D, 512], BF16, kind="Internal").ap()
    y4loc = nc.dram_tensor("y4loc", [ROWS + 128, D], F32, kind="Internal").ap()
    y4bf = nc.dram_tensor("y4bf", [ROWS, D], BF16, kind="Internal").ap()
    x3loc = nc.dram_tensor("x3loc", [ROWS + 128, D], BF16, kind="Internal").ap()
    floc = nc.dram_tensor("floc", [ROWS + 128, D], F32, kind="Internal").ap()
    ffix = nc.dram_tensor("ffix", [ROWS + 128, D], F32, kind="Internal").ap()
    cc_in = nc.dram_tensor("cc_in", [2048, D], F32, kind="Internal").ap()
    cc_rs = nc.dram_tensor("cc_rs", [256, D], F32, kind="Internal").ap()
    agp_in = nc.dram_tensor("agp_in", [256, D], F32, kind="Internal").ap()
    agp_out = nc.dram_tensor("agp_out", [2048, D], F32, kind="Internal").ap()
    agw_in = nc.dram_tensor("agw_in", [256, 1], F32, kind="Internal").ap()
    agw_out = nc.dram_tensor("agw_out", [2048, 1], F32, kind="Internal").ap()
    RG = [list(range(8))]

    p = Prog(nc)
    osb = lambda name, shape, dt: outer.enter_context(nc.sbuf_tensor(name, shape, dt))

    X = osb("X", [128, NTILE, D], BF16)
    w0 = osb("w0", [128, 8, D], BF16)
    w1 = osb("w1", [128, 8, D], BF16)
    w2 = osb("w2", [128, 8, D], BF16)
    ident = osb("ident", [128, 128], BF16)
    identf = osb("identf", [128, 128], F32)
    stats = {}
    for nmm in ("sum_a", "ssq_a", "rstd_a", "bias_a", "sum_b", "ssq_b", "rstd_b",
                "bias_b", "t0", "t1"):
        stats[nmm] = osb("st_" + nmm, [128, NTILE], F32)
    flatperm = osb("flatperm", [128, 16], I32)

    p.alloc_sems(outer)
    block = outer.enter_context(nc.Block())

    def phase_done():
        p.barrier()
        p.flush(block)

    p.op("pool", lambda e: e.memset(ident[:], 1.0), writes=(("ident",),))
    p.op("pool", lambda e: e.affine_select(ident[:], ident[:], [[-1, 128]],
                                           ALU.is_equal, 0.0, base=0, channel_multiplier=1),
         reads=(("ident",),), writes=(("ident",),))
    p.op("pool", lambda e: e.memset(identf[:], 1.0), writes=(("identf",),))
    p.op("pool", lambda e: e.affine_select(identf[:], identf[:], [[-1, 128]],
                                           ALU.is_equal, 0.0, base=0, channel_multiplier=1),
         reads=(("identf",),), writes=(("identf",),))
    for n, wd in (("wq", wq1), ("wk", wk1), ("wv", wv1), ("wo", wo1)):
        emit_weight_cast(p, wd, scrw[n + "1"], n)
    for t in range(NTILE):
        p.op("pool", lambda e, t=t: e.dma_start(out=X[:, t, :], in_=x_d[t, :, :]),
             writes=(("X", t),), inc=16)

    with contextlib.ExitStack() as ph:
        sb = lambda name, shape, dt: ph.enter_context(nc.sbuf_tensor(name, shape, dt))
        ps = lambda name, shape, dt: ph.enter_context(nc.psum_tensor(name, shape, dt))
        bufs = {"X": X, "ident": ident, "stats": stats,
                "wqT": w0, "wkT": w1, "wvT": w2}
        bufs["woT"] = sb("woT", [128, 8, D], BF16)
        bufs["b1_yfm"] = sb("b1_yfm", [128, 8, 512], BF16)
        bufs["qfm"] = sb("qfm", [128, 8, 512], BF16)
        bufs["kev"] = sb("kev", [128, 8, 640], BF16)
        bufs["kod"] = sb("kod", [128, 8, 640], BF16)
        bufs["vpad"] = sb("vpad", [128, 5, 16, 65], BF16)
        bufs["expS"] = [sb(f"expS{i}", [128, 512], BF16) for i in range(2)]
        bufs["otm"] = sb("otm", [128, 2, D], BF16)
        bufs["ofm"] = sb("ofm", [128, 2, 8, 128], BF16)
        bufs["fblk"] = sb("fblk", [128, 2, D], BF16)
        bufs["scr"] = sb("scr", [128, D], BF16)
        bufs["lnf"] = sb("lnf", [128, 2, D], BF16)
        bufs["ytm"] = [sb(f"ytm{i}", [128, D], BF16) for i in range(2)]
        bufs["bo_bc"] = sb("bo_bc", [128, D], BF16)
        bufs["maskA"] = sb("maskA", [128, 512], BF16)
        bufs["rcol"] = sb("rcol", [128, 16], F32)
        bufs["psum"] = [ps(f"pb{i}", [128, 512], F32) for i in range(4)]
        bufs["psumT"] = [ps(f"psT{i}", [128, 128], BF16) for i in range(2)]
        bufs["psum_av"] = [ps(f"pav{i}", [128, 65], F32) for i in range(2)]
        emit_b1_setup(p, nc, bufs, bo1)
        wtmps = [bufs["ytm"][0], bufs["ytm"][1]]
        wtkeys = [("ytm", 0), ("ytm", 1)]
        for n, wd in (("wq", wq1), ("wk", wk1), ("wv", wv1), ("wo", wo1)):
            emit_weight_load_T(p, wd, scrw[n + "1"], bufs[n + "T"], n, D, D, cast=False)
        emit_weight_cast(p, wr2, scrw["wr2"], "wr")
        emit_weight_cast(p, wk2b, scrw["wk2b"], "wk2")
        emit_weight_cast(p, wv2b, scrw["wv2b"], "wv2")
        emit_weight_cast(p, sw1_d, scrw["sw1"], "sw1")
        emit_weight_cast(p, sw2_d, scrw["sw2"], "sw2")
        emit_weight_cast(p, sw3_d, scrw["sw3"], "sw3")
        emit_block1(p, nc, bufs)
        phase_done()

    with contextlib.ExitStack() as ph:
        sb = lambda name, shape, dt: ph.enter_context(nc.sbuf_tensor(name, shape, dt))
        ps = lambda name, shape, dt: ph.enter_context(nc.psum_tensor(name, shape, dt))
        bufs = {"X": X, "ident": ident, "stats": stats,
                "wrT": w0, "wkT": w1, "wvT": w2}
        bufs["yfm"] = sb("yfm2", [128, 8, 513], BF16)
        bufs["xr"] = sb("xr", [128, 8, 512], BF16)
        bufs["xk"] = sb("xk", [128, 8, 512], BF16)
        bufs["kact"] = sb("kact", [128, 8, 512], BF16)
        bufs["scr"] = sb("scr2", [128, D], F32)
        bufs["lnf"] = sb("lnf2", [128, 2, D], F32)
        bufs["ytm"] = [sb(f"ytm2{i}", [128, D], BF16) for i in range(2)]
        bufs["krelu"] = [sb(f"krelu{i}", [128, 512], F32) for i in range(2)]
        bufs["rgate"] = sb("rgate", [128, 2, D], F32)
        bufs["fblk"] = sb("fblk2", [128, 4, D], F32)
        bufs["dif"] = [sb(f"dif{i}", [128, 512], BF16) for i in range(2)]
        bufs["mur"] = sb("mur_sb", [128, 8], F32)
        bufs["muk"] = sb("muk_sb", [128, 8], F32)
        bufs["psum"] = [ps(f"pb2{i}", [128, 512], F32) for i in range(7)]
        bufs["psumT"] = ps("psT2", [128, 128], BF16)
        for j in range(8):
            p.op("sp", lambda e, j=j: e.dma_start(out=bufs["mur"][:, j],
                                                  in_=mur2[j * 128:(j + 1) * 128]),
                 writes=(("mu",),), inc=16)
            p.op("sp", lambda e, j=j: e.dma_start(out=bufs["muk"][:, j],
                                                  in_=muk2[j * 128:(j + 1) * 128]),
                 writes=(("mu",),), inc=16)
        wtmps = [bufs["ytm"][0], bufs["ytm"][1]]
        wtkeys = [("ytm", 0), ("ytm", 1)]
        for src_d, scr_ in ((ew1_d, ew1scr), (ew2_d, ew2scr), (ew3_d, ew3scr)):
            for ei in range(8):
                n_ = scr_.shape[0] // 8
                p.op("pool", lambda e, ei=ei, src_d=src_d, scr_=scr_, n_=n_: e.dma_start(
                    out=scr_[ei * n_:(ei + 1) * n_, :], in_=src_d[ei * n_:(ei + 1) * n_, :]),
                    reads=(), writes=(("wscrb", id(scr_), ei),), inc=16)
        emit_weight_load_T(p, wr2, scrw["wr2"], bufs["wrT"], "wr", D, D, cast=False)
        emit_weight_load_T(p, wk2b, scrw["wk2b"], bufs["wkT"], "wk", D, D, cast=False, cast_name="wk2")
        emit_weight_load_T(p, wv2b, scrw["wv2b"], bufs["wvT"], "wv", D, D, cast=False, cast_name="wv2")
        emit_block2(p, nc, bufs, {})
        phase_done()

    with contextlib.ExitStack() as ph:
        sb = lambda name, shape, dt: ph.enter_context(nc.sbuf_tensor(name, shape, dt))
        ps = lambda name, shape, dt: ph.enter_context(nc.psum_tensor(name, shape, dt))
        bufs = {"X": X, "ident": ident, "stats": stats,
                "wqN": w0, "wkN": w1, "wvN": w2}
        bufs["b3_yfm"] = sb("b3_yfm", [128, 8, 512], BF16)
        bufs["qfm3"] = sb("qfm3", [128, 8, 512], BF16)
        bufs["kfm3"] = sb("kfm3", [128, 8, 512], BF16)
        bufs["sqf"] = sb("sqf", [128, 8, 512], BF16)
        bufs["skf"] = sb("skf", [128, 8, 512], BF16)
        bufs["vpad3"] = sb("vpad3", [128, 4, 8, 129], BF16)
        bufs["esb"] = sb("esb", [128, 4, 512], BF16)
        bufs["ablk"] = sb("ablk", [128, 4, D], BF16)
        bufs["Mz"] = sb("Mz", [128, 8, 129], F32)
        bufs["Mzbf"] = sb("Mzbf", [128, 8, 129], BF16)
        bufs["vdp"] = [sb(f"vdp{i}", [128, 129], BF16) for i in range(2)]
        bufs["sktm"] = [sb(f"sktm{i}", [128, 128], BF16) for i in range(2)]
        bufs["atmp"] = sb("atmp", [128, 2, 128], F32)
        bufs["es1"] = sb("es1", [128, 2, 512], F32)
        bufs["rq_t"] = sb("rq_t", [128, 4], F32)
        bufs["rd_t"] = sb("rd_t", [128, 4], F32)
        bufs["rk_t"] = sb("rk_t", [128, 4], F32)
        bufs["scr"] = sb("scr3", [128, D], BF16)
        bufs["lnf"] = sb("lnf3", [128, 2, D], BF16)
        bufs["ytm"] = [sb(f"ytm3{i}", [128, D], BF16) for i in range(2)]
        bufs["cmask"] = sb("cmask", [128, 4, 512], BF16)
        bufs["gbc"] = sb("gbc", [128, 8], F32)
        bufs["ombc"] = sb("ombc", [128, 8], F32)
        bufs["bqc"] = sb("bqc", [128, 8], F32)
        bufs["bkc"] = sb("bkc", [128, 8], F32)
        bufs["bvbc"] = sb("bvbc", [128, 8, 128], BF16)
        bufs["pj"] = [ps(f"pj3{i}", [128, 512], F32) for i in range(2)]
        bufs["pdot"] = [ps(f"pdot{i}", [128, 129], F32) for i in range(2)]
        bufs["pmem"] = [ps("pmem0", [128, 258], F32)] * 2
        bufs["pMz"] = ps("pMz", [128, 129], F32)
        bufs["psumT"] = [ps(f"psT3{i}", [128, 128], BF16) for i in range(2)]
        y4t = sb("y4t3", [128, 1, D], F32)
        y4view = y4loc[0:ROWS, :].rearrange("(t p) c -> t p c", p=128)
        stats_ = bufs["stats"]
        sum_a, ssq_a = stats_["sum_a"], stats_["ssq_a"]
        rstd_a, bias_a = stats_["rstd_a"], stats_["bias_a"]
        t0s, t1s = stats_["t0"], stats_["t1"]
        scr3 = bufs["scr"]

        def b3_tail(blk):
            for tt in range(4):
                tile = blk * 4 + tt
                emit_ln_stats(p, X[:, tile, :], tile, sum_a, ssq_a, tile, scr3, ("X", tile))
            emit_ln_finalize2(p, sum_a, ssq_a, rstd_a, bias_a, t0s, t1s,
                              (blk * 4, blk * 4 + 4), D)
            for tt in range(4):
                tile = blk * 4 + tt
                y4v = y4t[:, 0, :]
                emit_ln_norm(p, X[:, tile, :], (("X", tile),), y4v, (("y4t", 0),),
                             rstd_a, bias_a, tile, tag=f"y4 {tile}")
                p.op("pool", lambda e, tile=tile, y4v=y4v: e.dma_start(out=y4view[tile, :, :], in_=y4v),
                     reads=(("y4t", 0),), writes=(("y4w", tile),), inc=16)
                p.op("act", lambda e, tile=tile: e.dma_start(
                    out=x3loc[tile * 128:(tile + 1) * 128, :], in_=X[:, tile, :]),
                     reads=(("X", tile),), writes=(("x3w", tile),), inc=16)

        bufs["tail_hook"] = b3_tail
        emit_b3_setup(p, nc, bufs, bq3, bk3, bv3, beta3)
        emit_weight_load_nat(p, wq3, bufs["wqN"], "wq", D, D)
        emit_weight_load_nat(p, wk3, bufs["wkN"], "wk", D, D)
        emit_weight_load_nat(p, wv3, bufs["wvN"], "wv", D, D)
        emit_block3(p, nc, bufs)
        p.op("dve", lambda e: e.memset(y4t[:, 0, :], 0.0),
             reads=(("y4t", 0),), writes=(("y4t", 0),))
        p.op("sp", lambda e: e.dma_start(out=y4loc[ROWS:ROWS + 128, :], in_=y4t[:, 0, :]),
             reads=(("y4t", 0),), writes=(("y4guard",),), inc=16)
        ibs = sb("ibs5", [128, 16], I32)
        its = sb("its5", [128, 16], I32)
        flat = flatperm
        flatf = sb("flatf5", [128, 16], F32)
        own = sb("own5", [128, 16], F32)
        tmpi = sb("tmpi5", [128, 16], F32)
        basebc = sb("basebc5", [128, 1], F32)
        for k in range(16):
            p.op("sp", lambda e, k=k: e.dma_start(out=ibs[:, k], in_=ib_d[k * 128:(k + 1) * 128]),
                 writes=(("idx5",),), inc=16)
            p.op("sp", lambda e, k=k: e.dma_start(out=its[:, k], in_=it_d[k * 128:(k + 1) * 128]),
                 writes=(("idx5",),), inc=16)
        p.op("sp", lambda e: e.dma_start(out=basebc[:, :], in_=base_d[None, :].broadcast_to([128, 1])),
             writes=(("base5",),), inc=16)
        p.op("dve", lambda e: e.tensor_scalar(flat[:], ibs[:], T, None, ALU.mult),
             reads=(("idx5",),), writes=(("flat5",),))
        p.op("dve", lambda e: e.tensor_tensor(flat[:], flat[:], its[:], ALU.add),
             reads=(("flat5",), ("idx5",)), writes=(("flat5",),))
        p.op("dve", lambda e: e.tensor_copy(flatf[:], flat[:]),
             reads=(("flat5",),), writes=(("flatf5",),))
        p.op("dve", lambda e: e.tensor_scalar(flatf[:], flatf[:], basebc[:, 0:1], None, ALU.subtract),
             reads=(("flatf5",), ("base5",)), writes=(("flatf5",),))
        p.op("dve", lambda e: e.tensor_scalar(own[:], flatf[:], 0.0, None, ALU.is_ge),
             reads=(("flatf5",),), writes=(("own5",),))
        p.op("dve", lambda e: e.tensor_scalar(tmpi[:], flatf[:], float(ROWS), None, ALU.is_lt),
             reads=(("flatf5",),), writes=(("tmpi5",),))
        p.op("dve", lambda e: e.tensor_tensor(own[:], own[:], tmpi[:], ALU.mult),
             reads=(("own5",), ("tmpi5",)), writes=(("own5",),))
        p.op("dve", lambda e: e.tensor_tensor(flatf[:], flatf[:], own[:], ALU.mult),
             reads=(("flatf5",), ("own5",)), writes=(("flatf5",),))
        p.op("dve", lambda e: e.tensor_scalar(tmpi[:], own[:], -float(ROWS), float(ROWS), ALU.mult, ALU.add),
             reads=(("own5",),), writes=(("tmpi5",),))
        p.op("dve", lambda e: e.tensor_tensor(flatf[:], flatf[:], tmpi[:], ALU.add),
             reads=(("flatf5",), ("tmpi5",)), writes=(("flatf5",),))
        p.op("dve", lambda e: e.tensor_copy(flat[:], flatf[:]),
             reads=(("flatf5",),), writes=(("flat5",),))
        phase_done()

    with contextlib.ExitStack() as ph:
        sb = lambda name, shape, dt: ph.enter_context(nc.sbuf_tensor(name, shape, dt))
        ps = lambda name, shape, dt: ph.enter_context(nc.psum_tensor(name, shape, dt))
        bufs = {"X": X, "ident": ident, "stats": stats,
                "sw1T": w0[:, :, 0:512], "sw2T": w0[:, :, 512:1024],
                "sw3T": w1[:, 0:4, :]}
        bufs["b4_yfm"] = sb("b4_yfm", [128, 8, 512], BF16)
        bufs["shfm"] = sb("shfm", [128, 4, 4, 128], BF16)
        bufs["sc1"] = sb("sc1", [128, 4, 512], F32)
        bufs["sc2"] = sb("sc2", [128, 4, 512], F32)
        bufs["gbuf"] = sb("gbuf", [128, 4, 512], F32)
        bufs["shbf"] = sb("shbf", [128, 4, 512], BF16)
        bufs["ftile"] = sb("ftile4", [128, 4, D], F32)
        bufs["y4t"] = sb("y4t", [128, 2, D], F32)
        bufs["scr"] = sb("scr4", [128, 2, D], BF16)
        bufs["ytm"] = [sb(f"ytm4{i}", [128, D], BF16) for i in range(2)]
        bufs["y4bf"] = y4bf
        bufs["y4flat"] = y4loc
        bufs["sb1bc"] = sb("sb1bc", [128, 512], F32)
        bufs["sb2bc"] = sb("sb2bc", [128, 512], F32)
        bufs["sb3bc"] = sb("sb3bc", [128, D], F32)
        bufs["slgbc"] = sb("slgbc", [128, 512], F32)
        bufs["slbbc"] = sb("slbbc", [128, 512], F32)
        bufs["pj"] = [ps(f"pj4{i}", [128, 512], F32) for i in range(6)]
        bufs["psumT"] = [ps(f"psT4{i}", [128, 128], BF16) for i in range(2)]
        for bc, dd in ((bufs["sb1bc"], sb1_d), (bufs["sb2bc"], sb2_d), (bufs["sb3bc"], sb3_d),
                       (bufs["slgbc"], slg_d), (bufs["slbbc"], slb_d)):
            p.op("pool", lambda e, bc=bc, dd=dd: e.dma_start(
                out=bc[:, :], in_=dd[None, :].broadcast_to([128, dd.shape[0]])),
                writes=(("b4c",),), inc=16)
        wtmps = [bufs["ytm"][0], bufs["ytm"][1]]
        wtkeys = [("ytm", 0), ("ytm", 1)]
        emit_weight_load_T(p, sw1_d, scrw["sw1"], bufs["sw1T"], "sw1", 512, D, cast=False)
        emit_weight_load_T(p, sw2_d, scrw["sw2"], bufs["sw2T"], "sw2", 512, D, cast=False)
        emit_weight_load_T(p, sw3_d, scrw["sw3"], bufs["sw3T"], "sw3", D, 512, cast=False)
        fgz = [sb(f"fgz{i}", [128, D], F32) for i in range(2)]
        y4keys = tuple(("y4w", t) for t in range(NTILE)) + (("y4guard",),)
        for k in range(16):
            sl_ = k % 2
            fz = fgz[sl_]
            p.op("pool",
                 lambda e, k=k, fz=fz: e.indirect_dma_start(
                     out=fz[:, :], out_offset=None,
                     in_=y4loc[:, :], in_offset=IndirectOffsetOnAxis(flatperm[:, k:k + 1], 0)),
                 reads=(("flat5",), ("fgz", sl_)) + y4keys, writes=(("fgz", sl_),), inc=16)
            p.op("sp", lambda e, k=k, fz=fz: e.dma_start(out=cc_in[k * 128:(k + 1) * 128, :], in_=fz[:, :]),
                 reads=(("fgz", sl_),), writes=(("ccin", k),), inc=16)
        p.op("pool", lambda e: e.collective_compute(
            "ReduceScatter", ALU.add, RG, [cc_in.opt()], [cc_rs.opt()]),
            reads=tuple(("ccin", k) for k in range(16)), writes=(("ccrs",),), cc=True)
        for j in range(8):
            p.op("sp", lambda e, j=j: e.dma_start_transpose(
                w2[:, j, 0:512],
                ew1scr[0:512, j * 128:(j + 1) * 128]),
                reads=(("wscrb", id(ew1scr), 0),), writes=(("ew1T", 0),), inc=16)
        bufs["split_y4"] = True
        bufs["skip_y4_pass"] = True
        emit_block4_shared(p, nc, bufs,
                           y4loc[0:ROWS, :].rearrange("(t p) c -> t p c", p=128),
                           floc[0:ROWS, :].rearrange("(t p) c -> t p c", p=128))
        phase_done()

    with contextlib.ExitStack() as ph:
        sb = lambda name, shape, dt: ph.enter_context(nc.sbuf_tensor(name, shape, dt))
        ps = lambda name, shape, dt: ph.enter_context(nc.psum_tensor(name, shape, dt))
        ftile6 = sb("ftile6", [128, 4, D], F32)
        scr6 = sb("scr6", [128, 1, D], BF16)
        lnf6 = sb("lnf6", [128, 2, D], BF16)
        sum_b6 = sb("sum_p", [128, NTILE], F32)
        ssq_b6 = sb("ssq_p", [128, NTILE], F32)
        rstd_b6 = sb("rstd_p", [128, NTILE], F32)
        bias_b6 = sb("bias_p", [128, NTILE], F32)
        t06 = sb("t0p", [128, NTILE], F32)
        t16 = sb("t1p", [128, NTILE], F32)
        flview = floc[0:ROWS, :].rearrange("(t p) c -> t p c", p=128)

        def emit_pass_pair(pair):
            for pp in range(2):
                tile = pair * 2 + pp
                sl = tile % 4
                ftv = ftile6[:, sl, :]
                p.op("act", lambda e, tile=tile, ftv=ftv: e.dma_start(
                    out=ftv, in_=flview[tile, :, :]),
                     reads=(("ftile6", sl), ("fw", tile)), writes=(("ftile6", sl),), inc=16)
                p.op("act",
                     lambda e, tile=tile, ftv=ftv, sl=sl: e.activation(
                         out=scr6[:, 0, :], in_=ftv, func=AF.Square,
                         accum_out=ssq_b6[:, tile:tile + 1]),
                     reads=(("ftile6", sl),), writes=(("lnscr6", sl % 2), ("ssq", id(ssq_b6), tile)))
                p.op("act",
                     lambda e, tile=tile, ftv=ftv, sl=sl: e.activation(
                         out=scr6[:, 0, :], in_=ftv, func=AF.Identity,
                         accum_out=sum_b6[:, tile:tile + 1]),
                     reads=(("ftile6", sl),), writes=(("lnscr6", sl % 2), ("lsum", id(sum_b6), tile)))
            emit_ln_finalize2(p, sum_b6, ssq_b6, rstd_b6, bias_b6, t06, t16,
                              (pair * 2, pair * 2 + 2), D)
            for pp in range(2):
                tile = pair * 2 + pp
                sl = tile % 4
                ftv = ftile6[:, sl, :]
                emit_ln_norm(p, ftv, (("ftile6", sl),), lnf6[:, sl % 2, :], (("lnf6", sl % 2),),
                             rstd_b6, bias_b6, tile)
                p.op("pool",
                     lambda e, tile=tile, sl=sl: e.tensor_tensor(
                         X[:, tile, :], X[:, tile, :], lnf6[:, sl % 2, :], ALU.add),
                     reads=(("lnf6", sl % 2), ("X", tile), ("x3w", tile)), writes=(("X", tile),))
                p.op("pool", lambda e, tile=tile: e.dma_start(out=out_view[tile, :, :], in_=X[:, tile, :]),
                     reads=(("X", tile),), writes=(("outw", tile),), inc=16)

        def pass_hook(ei):
            emit_pass_pair(2 * ei)
            emit_pass_pair(2 * ei + 1)
        B = {}
        B["ibs"] = B["its"] = B["flat"] = None
        B["xm"] = sb("xm", [128, 2, D], F32)
        B["xmfm"] = sb("xmfm", [128, 8, 256], F32)
        B["xmbf"] = sb("xmbf", [128, 8, 256], BF16)
        B["gsb"] = sb("gsb", [128, D], F32)
        B["GT"] = sb("GT", [128, 8, 9], F32)
        B["bias9"] = sb("bias9", [128, 9], F32)
        B["expbc"] = sb("expbc", [128, 8], F32)
        B["sig"] = sb("sig", [128, 2, 9], F32)
        B["tmp8"] = sb("tmp8", [128, 8], F32)
        B["mx1"] = sb("mx1", [128, 1], F32)
        B["oh"] = sb("oh", [128, 2, 8], F32)
        B["ohT"] = sb("ohT", [8, 2, 128], BF16)
        B["t1sc"] = sb("t1sc", [128, 1], F32)
        B["wsh"] = sb("wsh_sb", [128, 2], F32)
        B["wex"] = sb("wex", [128, 2], F32)
        B["identf"] = identf
        B["identb"] = ident
        B["ew1T"] = [w2[:, :, 0:512], w2[:, :, 512:1024]]
        B["ew2T"] = [w0[:, :, 0:512], w0[:, :, 512:1024]]
        B["ew3T"] = [w1[:, 0:4, :], w1[:, 4:8, :]]
        B["eb1bc"] = sb("eb1bc", [128, 8, 512], BF16)
        B["eb2bc"] = sb("eb2bc", [128, 8, 512], BF16)
        B["eb3sb"] = sb("eb3sb", [8, D], BF16)
        B["selgbc"] = sb("selgbc", [128, 512], F32)
        B["selbbc"] = sb("selbbc", [128, 512], F32)
        B["sc1"] = sb("sc15", [128, 2, 512], F32)
        B["sc2"] = sb("sc25", [128, 2, 512], F32)
        B["gbuf"] = sb("gbuf5", [128, 2, 512], F32)
        B["zm"] = sb("zm", [128, 2, 512], BF16)
        B["zmfm"] = sb("zmfm", [128, 2, 4, 128], BF16)
        B["epart"] = sb("epart_sb", [128, D], F32)
        B["scrd"] = sb("scrd", [128, 1, 512], BF16)
        B["sum_b"] = stats["sum_b"]; B["ssq_b"] = stats["ssq_b"]
        B["rstd_b"] = stats["rstd_b"]; B["bias_b"] = stats["bias_b"]
        B["t0s"] = stats["t0"]; B["t1s"] = stats["t1"]
        B["pT"] = [ps("pT5a", [128, 128], F32), ps("pT5b", [128, 128], BF16)]
        B["ph"] = [ps(f"ph5{i}", [128, 512], F32) for i in range(2)]
        B["pout"] = [ps(f"pout5{i}", [128, 512], F32) for i in range(4)]
        B["wtmpb"] = sb("wtmpb5", [128, D], BF16)
        dr = {"ccrs": cc_rs, "sgw_d": sgw_d, "sgb_d": sgb_d, "egw_d": egw_d,
              "egb_d": egb_d, "expb_d": expb_d, "ew1_d": ew1_d, "eb1_d": eb1_d,
              "ew2_d": ew2_d, "eb2_d": eb2_d, "ew3_d": ew3_d, "eb3_d": eb3_d,
              "selg_d": selg_d, "selb_d": selb_d,
              "ep_out": agp_in.rearrange("(a p) c -> a p c", p=128),
              "wsh_out": agw_in.rearrange("(a p) x -> a (p x)", p=128),
              "ew1scr": ew1scr, "ew2scr": ew2scr, "ew3scr": ew3scr}
        emit_block4b(p, nc, B, dr, fused=True, hook=pass_hook)
        p.op("pool", lambda e: e.collective_compute(
            "AllGather", ALU.bypass, RG, [agp_in.opt()], [agp_out.opt()]),
            reads=(("epart", 0), ("epart", 1)), writes=(("agp",),), cc=True)
        p.op("pool", lambda e: e.collective_compute(
            "AllGather", ALU.bypass, RG, [agw_in.opt()], [agw_out.opt()]),
            reads=(("wsh",),), writes=(("agw",),), cc=True)
        phase_done()

    with contextlib.ExitStack() as ph:
        sb = lambda name, shape, dt: ph.enter_context(nc.sbuf_tensor(name, shape, dt))
        wshs = sb("wshs", [128, 16], F32)
        fg = sb("fg", [128, 2, D], F32)
        ep = sb("ep", [128, 2, D], F32)
        comb = sb("comb", [128, 2, D], F32)
        xg = sb("xg", [128, 2, D], BF16)
        orow = sb("orow", [128, 2, D], F32)
        scrm = sb("scrm", [128, 2, D], BF16)
        sum_m, ssq_m = stats["sum_a"], stats["ssq_a"]
        rstd_m, bias_m = stats["rstd_a"], stats["bias_a"]
        t0m, t1m = stats["t0"], stats["t1"]
        for k in range(16):
            p.op("sp", lambda e, k=k: e.dma_start(out=wshs[:, k], in_=agw_out[k * 128:(k + 1) * 128, 0]),
                 reads=(("agw",),), writes=(("wshs",),), inc=16)
        outk = tuple(("outw", t) for t in range(NTILE))
        x3k = tuple(("x3w", t) for t in range(NTILE))
        for k in range(16):
            sl = k % 2
            fgv, epv, cbv, xgv, orv = fg[:, sl, :], ep[:, sl, :], comb[:, sl, :], xg[:, sl, :], orow[:, sl, :]
            p.op("pool",
                 lambda e, k=k, fgv=fgv: e.indirect_dma_start(
                     out=fgv, out_offset=None,
                     in_=floc[:, :], in_offset=IndirectOffsetOnAxis(flatperm[:, k:k + 1], 0)),
                 reads=(("flat5",), ("fg", sl)), writes=(("fg", sl),), inc=16)
            p.op("pool",
                 lambda e, k=k, xgv=xgv: e.indirect_dma_start(
                     out=xgv, out_offset=None,
                     in_=x3loc[:, :], in_offset=IndirectOffsetOnAxis(flatperm[:, k:k + 1], 0)),
                 reads=(("flat5",), ("xg", sl)) + x3k, writes=(("xg", sl),), inc=16)
            p.op("sp", lambda e, k=k, epv=epv: e.dma_start(out=epv, in_=agp_out[k * 128:(k + 1) * 128, :]),
                 reads=(("ep", sl), ("agp",)), writes=(("ep", sl),), inc=16)
            p.op("dve",
                 lambda e, k=k, fgv=fgv, epv=epv, cbv=cbv: e.scalar_tensor_tensor(
                     cbv, fgv, wshs[:, k:k + 1], epv, ALU.mult, ALU.add),
                 reads=(("fg", sl), ("ep", sl), ("wshs",), ("comb", sl)), writes=(("comb", sl),))
            p.op("act",
                 lambda e, k=k, cbv=cbv, sl=sl: e.activation(
                     out=scrm[:, sl, :], in_=cbv, func=AF.Square,
                     accum_out=ssq_m[:, k:k + 1]),
                 reads=(("comb", sl),), writes=(("scrm", sl), ("ssq", id(ssq_m), k)))
            p.op("act",
                 lambda e, k=k, cbv=cbv, sl=sl: e.activation(
                     out=scrm[:, sl, :], in_=cbv, func=AF.Identity,
                     accum_out=sum_m[:, k:k + 1]),
                 reads=(("comb", sl),), writes=(("scrm", sl), ("lsum", id(sum_m), k)))
            emit_ln_finalize2(p, sum_m, ssq_m, rstd_m, bias_m, t0m, t1m, (k, k + 1), D)
            emit_ln_norm(p, cbv, (("comb", sl),), orv, (("orow", sl),), rstd_m, bias_m, k)
            p.op("dve",
                 lambda e, orv=orv, xgv=xgv: e.tensor_tensor(orv, orv, xgv, ALU.add),
                 reads=(("orow", sl), ("xg", sl)), writes=(("orow", sl),))
            p.op("pool",
                 lambda e, k=k, orv=orv: e.indirect_dma_start(
                     out=out_d[:, :], out_offset=IndirectOffsetOnAxis(flatperm[:, k:k + 1], 0),
                     in_=orv, in_offset=None),
                 reads=(("flat5",), ("orow", sl)) + outk, writes=(("outs", k),), inc=16)
        phase_done()

    p.finish(block)
    return nc, outer


_PROG_CACHE = {}


def _get_prog(name, builder):
    if name not in _PROG_CACHE:
        _PROG_CACHE[name] = builder()
    return _PROG_CACHE[name][0]


USE_FUSED = True


def kernel(**inputs):
    if USE_FUSED:
        return kernel_fused(**inputs)
    return kernel_unfused(**inputs)


def kernel_fused(**inputs):
    ncores = 8
    shards = _shard_rows(np.asarray(inputs["L0_attn_query"], np.float32))
    w = {k: np.ascontiguousarray(np.asarray(inputs[k], np.float32))
         for k in inputs if k != "L0_attn_query" and "index" not in k}
    idx_b = np.ascontiguousarray(np.asarray(inputs["L1_ffn_index_b"], np.int32))
    idx_t = np.ascontiguousarray(np.asarray(inputs["L1_ffn_index_t"], np.int32))
    nc = _get_prog("fused", build_fused_program)
    common = {
        "wq1": w["L0_attn_Wq"], "wk1": w["L0_attn_Wk"], "wv1": w["L0_attn_Wv"],
        "wo1": w["L0_attn_Wo"], "bo1": w["L0_attn_bo"],
        "wr2": w["L0_ffn_W_r"], "wk2b": w["L0_ffn_W_k"], "wv2b": w["L0_ffn_W_v"],
        "mur2": w["L0_ffn_mu_r"], "muk2": w["L0_ffn_mu_k"],
        "wq3": w["L1_attn_Wq"], "bq3": w["L1_attn_bq"], "wk3": w["L1_attn_Wk"],
        "bk3": w["L1_attn_bk"], "wv3": w["L1_attn_Wv"], "bv3": w["L1_attn_bv"],
        "beta3": w["L1_attn_beta"],
        "sw1": w["L1_ffn_sw1"], "sb1": w["L1_ffn_sb1"],
        "sw2": w["L1_ffn_sw2"], "sb2": w["L1_ffn_sb2"],
        "sw3": w["L1_ffn_sw3"], "sb3": w["L1_ffn_sb3"],
        "slg": w["L1_ffn_s_ln_g"], "slb": w["L1_ffn_s_ln_b"],
        "sgw": w["L1_ffn_sg_w"], "sgb": w["L1_ffn_sg_b"],
        "egw": w["L1_ffn_eg_w"], "egb": w["L1_ffn_eg_b"],
        "expb": w["L1_ffn_exp_bias"],
        "ew1": w["L1_ffn_ew1"].reshape(8 * 512, D), "eb1": w["L1_ffn_eb1"],
        "ew2": w["L1_ffn_ew2"].reshape(8 * 512, D), "eb2": w["L1_ffn_eb2"],
        "ew3": w["L1_ffn_ew3"].reshape(8 * D, 512), "eb3": w["L1_ffn_eb3"],
        "selg": w["L1_ffn_sel_ln_g"], "selb": w["L1_ffn_sel_ln_b"],
        "ib": idx_b, "it": idx_t,
    }
    in_maps = [dict(common, x=shards[c], basef=np.array([ROWS * c], np.float32))
               for c in range(ncores)]
    res = run_bass_kernel_spmd(nc, in_maps, core_ids=list(range(ncores)))
    out = np.concatenate([np.asarray(r["out"])[:ROWS].reshape(B_LOC, T, D)
                          for r in res.results], axis=0)
    return out


def kernel_unfused(**inputs):
    ncores = 8
    cores = list(range(ncores))

    def run(name, builder, in_maps):
        nc = _get_prog(name, builder)
        return run_bass_kernel_spmd(nc, in_maps, core_ids=cores).results

    shards = _shard_rows(np.asarray(inputs["L0_attn_query"], np.float32))
    w = {k: np.ascontiguousarray(np.asarray(inputs[k], np.float32))
         for k in inputs if k != "L0_attn_query" and "index" not in k}
    idx_b = np.ascontiguousarray(np.asarray(inputs["L1_ffn_index_b"], np.int32))
    idx_t = np.ascontiguousarray(np.asarray(inputs["L1_ffn_index_t"], np.int32))

    r1 = run("b1", build_b1_program,
             [{"x": s, "wq": w["L0_attn_Wq"], "wk": w["L0_attn_Wk"],
               "wv": w["L0_attn_Wv"], "wo": w["L0_attn_Wo"], "bo": w["L0_attn_bo"]}
              for s in shards])
    x1 = [np.asarray(r["out"]) for r in r1]
    r2 = run("b2", build_b2_program,
             [{"x": x1[c], "wr": w["L0_ffn_W_r"], "wk": w["L0_ffn_W_k"],
               "wv": w["L0_ffn_W_v"], "mur": w["L0_ffn_mu_r"], "muk": w["L0_ffn_mu_k"]}
              for c in cores])
    x2 = [np.asarray(r["out"]) for r in r2]
    r3 = run("b3", build_b3_program,
             [{"x": x2[c], "wq": w["L1_attn_Wq"], "bq": w["L1_attn_bq"],
               "wk": w["L1_attn_Wk"], "bk": w["L1_attn_bk"], "wv": w["L1_attn_Wv"],
               "bv": w["L1_attn_bv"], "beta": w["L1_attn_beta"]} for c in cores])
    x3 = [np.asarray(r["out"]) for r in r3]
    r4a = run("b4a", build_b4a_program,
              [{"x": x3[c], "sw1": w["L1_ffn_sw1"], "sb1": w["L1_ffn_sb1"],
                "sw2": w["L1_ffn_sw2"], "sb2": w["L1_ffn_sb2"],
                "sw3": w["L1_ffn_sw3"], "sb3": w["L1_ffn_sb3"],
                "slg": w["L1_ffn_s_ln_g"], "slb": w["L1_ffn_s_ln_b"]} for c in cores])
    y4full = np.ascontiguousarray(
        np.concatenate([np.asarray(r["y4"]).reshape(ROWS, D) for r in r4a], axis=0))
    f_slices = [np.asarray(r["f"]).reshape(ROWS, D) for r in r4a]
    MSL = 2048 // ncores
    ew1 = w["L1_ffn_ew1"].reshape(8 * 512, D)
    ew2 = w["L1_ffn_ew2"].reshape(8 * 512, D)
    ew3 = w["L1_ffn_ew3"].reshape(8 * D, 512)
    r4b = run("b4b", build_b4b_program,
              [{"y4full": y4full, "ib": idx_b[c * MSL:(c + 1) * MSL],
                "it": idx_t[c * MSL:(c + 1) * MSL],
                "sgw": w["L1_ffn_sg_w"], "sgb": w["L1_ffn_sg_b"],
                "egw": w["L1_ffn_eg_w"], "egb": w["L1_ffn_eg_b"],
                "expb": w["L1_ffn_exp_bias"],
                "ew1": ew1, "eb1": w["L1_ffn_eb1"],
                "ew2": ew2, "eb2": w["L1_ffn_eb2"],
                "ew3": ew3, "eb3": w["L1_ffn_eb3"],
                "selg": w["L1_ffn_sel_ln_g"], "selb": w["L1_ffn_sel_ln_b"]}
               for c in cores])
    epart = np.ascontiguousarray(
        np.concatenate([np.asarray(r["epart"]).reshape(MSL, D) for r in r4b], axis=0))
    wshall = np.ascontiguousarray(
        np.concatenate([np.asarray(r["wsh"]).reshape(MSL) for r in r4b], axis=0))
    zrow = np.zeros((128, D), np.float32)
    r4c = run("b4c", build_b4c_program,
              [{"x": x3[c],
                "fpad": np.ascontiguousarray(np.concatenate([f_slices[c], zrow], 0)),
                "ib": idx_b, "it": idx_t, "epart": epart, "wshall": wshall,
                "base": np.array([ROWS * c], np.float32)} for c in cores])
    out = np.concatenate([np.asarray(r["out"]).reshape(B_LOC, T, D) for r in r4c], axis=0)
    return out


# version 21
# speedup vs baseline: 1.2297x; 1.2297x over previous
import contextlib
import numpy as np
import ml_dtypes
import concourse.bass as bass
import concourse.mybir as mybir
from concourse.bass_utils import run_bass_kernel_spmd
from concourse.bass_core_rust import IndirectOffsetOnAxis

F32 = mybir.dt.float32
BF16 = mybir.dt.bfloat16
I32 = mybir.dt.int32
AF = mybir.ActivationFunctionType
ALU = mybir.AluOpType
AX = mybir.AxisListType

B_LOC = 2
T = 2048
D = 1024
ROWS = B_LOC * T
NTILE = ROWS // 128

ENGINES = ("pe", "act", "dve", "pool", "sp")
SECTION = {"pe": "tensor", "act": "scalar", "dve": "vector", "pool": "gpsimd", "sp": "sync"}


DMA_ROT = 4


class Prog:

    def __init__(self, nc):
        self.nc = nc
        self.streams = list(ENGINES) + [f"{e}.d{r}" for e in ("pool", "sp", "act", "dve")
                                        for r in range(DMA_ROT)]
        self.cnt = {s: 0 for s in self.streams}
        self.dmacnt = {e: 0 for e in ("pool", "sp", "act", "dve")}
        self.q = {e: [] for e in ENGINES}
        self.writer = {}
        self.readers = {}
        self.waited = {}
        self.sems = {}

    def _add_wait(self, waits, eng, dep):
        pstream, pcnt = dep
        if pstream == eng == "pe":
            return
        if self.waited.get((eng, pstream), 0) >= pcnt:
            return
        waits[pstream] = max(waits.get(pstream, 0), pcnt)

    def op(self, eng, emit, reads=(), writes=(), inc=1, tag="", cc=False):
        is_dma = inc == 16
        waits = {}
        pend = getattr(self, "pending", {}).pop(eng, None)
        if pend:
            for pstream, v in pend.items():
                if pstream != eng and v > 0:
                    self._add_wait(waits, eng, (pstream, v))
        if cc:
            ccn = getattr(self, "cccnt", 0)
            self.cccnt = ccn + 1
            stream = "pool.cc"
            if ccn > 0:
                self._add_wait(waits, eng, (stream, ccn))
            for k in reads:
                d = self.writer.get(k)
                if d is not None:
                    self._add_wait(waits, eng, d)
            for k in writes:
                d = self.writer.get(k)
                if d is not None:
                    self._add_wait(waits, eng, d)
                for pstream, pcnt in self.readers.get(k, {}).items():
                    self._add_wait(waits, eng, (pstream, pcnt))
            for pstream, v in waits.items():
                self.waited[(eng, pstream)] = max(self.waited.get((eng, pstream), 0), v)
            self.cnt[stream] = self.cnt.get(stream, 0) + 1
            newcnt = self.cnt[stream]
            self.q[eng].append((waits, emit, stream, 1, tag))
            for k in writes:
                self.writer[k] = (stream, newcnt)
                self.readers[k] = {}
            for k in reads:
                self.readers.setdefault(k, {})[stream] = newcnt
            return newcnt
        if is_dma:
            qn = self.dmacnt[eng]
            self.dmacnt[eng] = qn + 1
            stream = f"{eng}.d{qn % DMA_ROT}"
            prev = (qn // DMA_ROT) * 16
            if prev > 0:
                self._add_wait(waits, eng, (stream, prev))
        else:
            stream = eng
        for k in reads:
            d = self.writer.get(k)
            if d is not None:
                self._add_wait(waits, eng, d)
        for k in writes:
            d = self.writer.get(k)
            if d is not None:
                self._add_wait(waits, eng, d)
            for pstream, pcnt in self.readers.get(k, {}).items():
                self._add_wait(waits, eng, (pstream, pcnt))
        for pstream, v in waits.items():
            self.waited[(eng, pstream)] = max(self.waited.get((eng, pstream), 0), v)
        self.cnt[stream] += inc
        newcnt = self.cnt[stream]
        self.q[eng].append((waits, emit, stream, inc, tag))
        for k in writes:
            self.writer[k] = (stream, newcnt)
            self.readers[k] = {}
        for k in reads:
            self.readers.setdefault(k, {})[stream] = newcnt
        return newcnt

    def barrier(self):
        snap = {s: c for s, c in self.cnt.items() if c > 0 and s != "pool.cc"}
        self.pending = {e: dict(snap) for e in ENGINES}

    def alloc_sems(self, stack):
        for s in self.streams + ["pool.cc"]:
            if s not in self.sems:
                self.sems[s] = stack.enter_context(
                    self.nc.semaphore("sem_" + s.replace(".", "_")))

    def flush(self, block):
        for e in ENGINES:
            if not self.q[e]:
                continue
            q = self.q[e]
            self.q[e] = []

            def f(engobj, q=q, e=e):
                for waits, emit, stream, inc, tag in q:
                    for pstream, v in waits.items():
                        engobj.wait_ge(self.sems[pstream], v)
                    ins = emit(engobj)
                    ins.then_inc(self.sems[stream], inc)
            getattr(block, SECTION[e])(f)

    def finish(self, block):
        final = dict(self.cnt)

        def mk(e):
            def f(engobj):
                for s, c in final.items():
                    if c > 0 and (s == e or s.startswith(e + ".")):
                        engobj.wait_ge(self.sems[s], c)
                if e == "sp":
                    for s, c in final.items():
                        if c > 0 and not (s == "sp" or s.startswith("sp.")):
                            engobj.wait_ge(self.sems[s], c)
            return f

        for e in ENGINES:
            getattr(block, SECTION[e])(mk(e))

    def build(self, stack):
        nc = self.nc
        self.alloc_sems(stack)
        block = stack.enter_context(nc.Block())
        self.flush(block)
        self.finish(block)



def emit_weight_cast(p, w_dram, scr_dram, name):
    p.op("pool",
         lambda e: e.dma_start(out=scr_dram[:, :], in_=w_dram[:, :]),
         reads=(), writes=(("wscr", name),), inc=16, tag=f"wcast {name}")


def emit_weight_load_T(p, w_dram, scr_dram, wt_sb, name, dout, din, tmps=None,
                       tmp_keys=None, cast=True, cast_name=None):
    ncol = din // 128
    cn = cast_name or name
    if cast:
        emit_weight_cast(p, w_dram, scr_dram, cn)
    for j in range(ncol):
        p.op("sp",
             lambda e, j=j: e.dma_start_transpose(
                 wt_sb[:, j, :],
                 scr_dram[:, j * 128:(j + 1) * 128]),
             reads=(("wscr", cn),), writes=(("WT", name, j),), inc=16, tag=f"wT {name} {j}")


def emit_ln_stats(p, x_ap, tile, sum_sb, ssq_sb, col, scr_sb, xkey):
    bnscr = p.bnscr
    n = int(x_ap.shape[-1])
    xkey = xkey if (xkey and isinstance(xkey[0], tuple)) else (xkey,)
    mvk = (("mv", id(sum_sb), col),)
    if n <= 512:
        p.op("dve", lambda e: e.bn_stats(bnscr[:, 0:6], x_ap),
             reads=xkey, writes=(("bnscr",),), tag=f"bn {tile}")
        p.op("dve", lambda e: e.bn_aggr(sum_sb[:, col, :], bnscr[:, 0:6]),
             reads=(("bnscr",),), writes=mvk, tag=f"bnag {tile}")
    else:
        h = n // 2
        p.op("dve", lambda e: e.bn_stats(bnscr[:, 0:6], x_ap[:, 0:h]),
             reads=xkey, writes=(("bnscr",),), tag=f"bn {tile}")
        p.op("dve", lambda e: e.bn_stats(bnscr[:, 6:12], x_ap[:, h:n]),
             reads=xkey, writes=(("bnscr",),), tag=f"bn {tile}b")
        p.op("dve", lambda e: e.bn_aggr(sum_sb[:, col, :], bnscr[:, :]),
             reads=(("bnscr",),), writes=mvk, tag=f"bnag {tile}")


def emit_ln_finalize2(p, sum_sb, ssq_sb, rstd_sb, bias_sb, t0_sb, t1_sb, cols, n, eps=1e-5):
    c0, c1 = cols
    mean = sum_sb[:, c0:c1, 0]
    var = sum_sb[:, c0:c1, 1]
    mvk = tuple(("mv", id(sum_sb), c) for c in range(c0, c1))
    rstd = rstd_sb[:, c0:c1]
    bias = bias_sb[:, c0:c1]
    K = id(rstd_sb)
    krstd = tuple(("rstd", K, c) for c in range(c0, c1))
    kbias = tuple(("lnbias", K, c) for c in range(c0, c1))
    p.op("dve", lambda e: e.tensor_scalar(rstd, var, float(eps), None, ALU.add),
         reads=mvk, writes=krstd)
    p.op("act", lambda e: e.activation(out=rstd, in_=rstd, func=AF.Sqrt),
         reads=krstd, writes=krstd)
    p.op("dve", lambda e: e.reciprocal(rstd, rstd), reads=krstd, writes=krstd)
    p.op("dve", lambda e: e.scalar_tensor_tensor(bias, mean, -1.0, rstd, ALU.mult, ALU.mult),
         reads=mvk + krstd, writes=kbias)



def emit_block2(p, nc, bufs, dram):
    X = bufs["X"]
    yfm = bufs["yfm"]
    xr = bufs["xr"]
    xk = bufs["xk"]
    kact = bufs["kact"]
    scr = bufs["scr"]
    ytm = bufs["ytm"]
    krelu = bufs["krelu"]
    rgate = bufs["rgate"]
    fblk = bufs["fblk"]
    dif = bufs["dif"]
    ident = bufs["ident"]
    psum = bufs["psum"]
    psumT = bufs["psumT"]
    stats = bufs["stats"]
    wrT, wkT, wvT = bufs["wrT"], bufs["wkT"], bufs["wvT"]
    mur, muk = bufs["mur"], bufs["muk"]

    sum_a, ssq_a = stats["sum_a"], stats["ssq_a"]
    rstd_a, bias_a = stats["rstd_a"], stats["bias_a"]
    sum_b, ssq_b = stats["sum_b"], stats["ssq_b"]
    rstd_b, bias_b = stats["rstd_b"], stats["bias_b"]
    t0s, t1s = stats["t0"], stats["t1"]

    NB = NTILE // 4
    for blk in range(NB):
        for tt in range(4):
            tile = blk * 4 + tt
            xa = X[:, tile, :]
            emit_ln_stats(p, xa, tile, sum_a, ssq_a, tile, scr, ("X", tile))
        emit_ln_finalize2(p, sum_a, ssq_a, rstd_a, bias_a, t0s, t1s,
                          (blk * 4, blk * 4 + 4), D)
        if blk % (NB // B_LOC) == 0:
            p.op("dve", lambda e: e.memset(yfm[:, :, 0:1], 0.0),
                 reads=(), writes=(("yfmc",),), tag="carry0")
        else:
            p.op("dve", lambda e: e.tensor_copy(yfm[:, :, 0:1], yfm[:, :, 512:513]),
                 reads=(("yfm", 7, 3),) + tuple(("yfm", j, 3) for j in range(8)),
                 writes=(("yfmc",),), tag="carry")
        for tt in range(4):
            tile = blk * 4 + tt
            xa = X[:, tile, :]
            yt = ytm[tt % 2]
            p.op("act",
                 lambda e, xa=xa, yt=yt, tile=tile: e.activation(
                     out=yt[:], in_=xa, func=AF.Identity,
                     scale=rstd_a[:, tile:tile + 1], bias=bias_a[:, tile:tile + 1]),
                 reads=(("X", tile), ("rstd", id(rstd_a), tile), ("lnbias", id(rstd_a), tile)),
                 writes=(("ytm", tt % 2),), tag=f"ln y {tile}")
            for j in range(8):
                p.op("pe",
                     lambda e, yt=yt, j=j: e.transpose(psumT[:], yt[:, j * 128:(j + 1) * 128], ident[:]),
                     reads=(("ytm", tt % 2), ("ident",)), writes=(("psumT",),), tag=f"T {tile} {j}")
                p.op("dve",
                     lambda e, j=j, tt=tt: e.tensor_copy(
                         yfm[:, j, 1 + tt * 128:1 + tt * 128 + 128], psumT[:]),
                     reads=(("psumT",),), writes=(("yfm", j, tt),), tag=f"Tc {tile} {j}")
        for j in range(8):
            ykeys = tuple(("yfm", j, tt) for tt in range(4)) + (("yfmc",),)
            d = dif[j % 2]
            p.op("dve",
                 lambda e, j=j, d=d: e.tensor_tensor(
                     d[:], yfm[:, j, 1:513], yfm[:, j, 0:512], ALU.subtract),
                 reads=ykeys, writes=(("dif", j % 2),), tag=f"dif {j}")
            p.op("dve",
                 lambda e, j=j, d=d: e.scalar_tensor_tensor(
                     xr[:, j, :], d[:], mur[:, j:j + 1], yfm[:, j, 0:512], ALU.mult, ALU.add),
                 reads=(("dif", j % 2), ("mu",)) + ykeys, writes=(("xr", j),), tag=f"xr {j}")
            p.op("dve",
                 lambda e, j=j, d=d: e.scalar_tensor_tensor(
                     xk[:, j, :], d[:], muk[:, j:j + 1], yfm[:, j, 0:512], ALU.mult, ALU.add),
                 reads=(("dif", j % 2), ("mu",)) + ykeys, writes=(("xk", j),), tag=f"xk {j}")
        for jj in range(8):
            bank = jj % 3
            pk = psum[bank]
            for j in range(8):
                p.op("pe",
                     lambda e, j=j, jj=jj, pk=pk: e.matmul(
                         pk[:], wkT[:, j, jj * 128:(jj + 1) * 128], xk[:, j, :],
                         start=(j == 0), stop=(j == 7)),
                     reads=(("WT", "wk", j), ("xk", j)), writes=(("psum", bank),),
                     tag=f"mmk {jj} {j}")
            kr = krelu[jj % 2]
            p.op("dve",
                 lambda e, kr=kr, pk=pk: e.tensor_scalar_max(kr[:], pk[:], 0.0),
                 reads=(("psum", bank),), writes=(("krelu", jj % 2),), tag=f"relu {jj}")
            p.op("act",
                 lambda e, kr=kr, jj=jj: e.activation(out=kact[:, jj, :], in_=kr[:], func=AF.Square),
                 reads=(("krelu", jj % 2),), writes=(("kact", jj),), tag=f"sq {jj}")
        for tt in range(4):
            tile = blk * 4 + tt
            for half in range(2):
                bank = 3 + half
                pr = psum[bank]
                for j in range(8):
                    p.op("pe",
                         lambda e, j=j, half=half, tt=tt, pr=pr: e.matmul(
                             pr[:], xr[:, j, tt * 128:(tt + 1) * 128],
                             wrT[:, j, half * 512:(half + 1) * 512],
                             start=(j == 0), stop=(j == 7)),
                         reads=(("WT", "wr", j), ("xr", j)), writes=(("psum", bank),),
                         tag=f"mmr {tile} {half} {j}")
                p.op("act",
                     lambda e, half=half, pr=pr, tt=tt: e.activation(
                         out=rgate[:, tt % 2, half * 512:(half + 1) * 512], in_=pr[:], func=AF.Sigmoid),
                     reads=(("psum", bank),), writes=(("rgate", tt % 2, half),), tag=f"sig {tile} {half}")
            for half in range(2):
                bank = 5 + half
                po = psum[bank]
                for jj in range(8):
                    p.op("pe",
                         lambda e, jj=jj, half=half, tt=tt, po=po: e.matmul(
                             po[:], kact[:, jj, tt * 128:(tt + 1) * 128],
                             wvT[:, jj, half * 512:(half + 1) * 512],
                             start=(jj == 0), stop=(jj == 7)),
                         reads=(("WT", "wv", jj), ("kact", jj)), writes=(("psum", bank),),
                         tag=f"mmv {tile} {half} {jj}")
                p.op("dve",
                     lambda e, half=half, tt=tt, po=po: e.tensor_tensor(
                         fblk[:, tt, half * 512:(half + 1) * 512], po[:],
                         rgate[:, tt % 2, half * 512:(half + 1) * 512], ALU.mult),
                     reads=(("psum", bank), ("rgate", tt % 2, half)), writes=(("fblk", tt, half),),
                     tag=f"fmul {tile} {half}")
        for tt in range(4):
            tile = blk * 4 + tt
            fa = fblk[:, tt, :]
            fk = (("fblk", tt, 0), ("fblk", tt, 1))
            emit_ln_stats(p, fa, tile, sum_b, None, tile, None, fk)
        emit_ln_finalize2(p, sum_b, ssq_b, rstd_b, bias_b, t0s, t1s,
                          (blk * 4, blk * 4 + 4), D)
        for tt in range(4):
            tile = blk * 4 + tt
            slot = tile % 2
            fa = fblk[:, tt, :]
            fk = (("fblk", tt, 0), ("fblk", tt, 1))
            lnf = bufs["lnf"]
            p.op("act",
                 lambda e, fa=fa, tile=tile, lnf=lnf, slot=slot: e.activation(
                     out=lnf[:, slot, :], in_=fa, func=AF.Identity,
                     scale=rstd_b[:, tile:tile + 1], bias=bias_b[:, tile:tile + 1]),
                 reads=fk + (("rstd", id(rstd_b), tile), ("lnbias", id(rstd_b), tile)),
                 writes=(("lnf", slot),), tag=f"pln y {tile}")
            p.op("pool",
                 lambda e, tile=tile, lnf=lnf, slot=slot: e.tensor_tensor(
                     X[:, tile, :], X[:, tile, :], lnf[:, slot, :], ALU.add),
                 reads=(("lnf", slot), ("X", tile)), writes=(("X", tile),), tag=f"res {tile}")





def emit_ln_norm(p, x_ap, xkeys, out_ap, outkeys, rstd_sb, bias_sb, tile, tag=""):
    K = id(rstd_sb)
    p.op("act",
         lambda e: e.activation(out=out_ap, in_=x_ap, func=AF.Identity,
                                scale=rstd_sb[:, tile:tile + 1],
                                bias=bias_sb[:, tile:tile + 1]),
         reads=tuple(xkeys) + (("rstd", K, tile), ("lnbias", K, tile)),
         writes=tuple(outkeys), tag=tag)


def emit_post_ln_residual2(p, X, fblk, scr, lnf, sum_b, ssq_b, rstd_b, bias_b, t0s, t1s, tile0):
    for pp in range(2):
        tile = tile0 + pp
        slot = tile % 2
        fk = (("fblk", slot, 0), ("fblk", slot, 1))
        fa = fblk[:, slot, :]
        emit_ln_stats(p, fa, tile, sum_b, None, tile, None, fk)
    emit_ln_finalize2(p, sum_b, ssq_b, rstd_b, bias_b, t0s, t1s, (tile0, tile0 + 2), D)
    for pp in range(2):
        tile = tile0 + pp
        slot = tile % 2
        fk = (("fblk", slot, 0), ("fblk", slot, 1))
        emit_ln_norm(p, fblk[:, slot, :], fk, lnf[:, slot, :], (("lnf", slot),),
                     rstd_b, bias_b, tile)
        p.op("pool",
             lambda e, tile=tile, slot=slot: e.tensor_tensor(
                 X[:, tile, :], X[:, tile, :], lnf[:, slot, :], ALU.add),
             reads=(("lnf", slot), ("X", tile)), writes=(("X", tile),))


B1_STAGE = 5


def emit_block1(p, nc, bufs):
    stage = B1_STAGE
    X = bufs["X"]
    yfm = bufs["b1_yfm"]
    qfm = bufs["qfm"]
    kev = bufs["kev"]
    kod = bufs["kod"]
    vpad = bufs["vpad"]
    expS = bufs["expS"]
    otm = bufs["otm"]
    ofm = bufs["ofm"]
    fblk = bufs["fblk"]
    scr = bufs["scr"]
    lnf = bufs["lnf"]
    ytm = bufs["ytm"]
    ident = bufs["ident"]
    bo_bc = bufs["bo_bc"]
    maskA = bufs["maskA"]
    rcol = bufs["rcol"]
    psum = bufs["psum"]
    psumT = bufs["psumT"]
    psum_av = bufs["psum_av"]
    stats = bufs["stats"]
    wqT, wkT, wvT, woT = bufs["wqT"], bufs["wkT"], bufs["wvT"], bufs["woT"]

    sum_a, ssq_a = stats["sum_a"], stats["ssq_a"]
    rstd_a, bias_a = stats["rstd_a"], stats["bias_a"]
    sum_b, ssq_b = stats["sum_b"], stats["ssq_b"]
    rstd_b, bias_b = stats["rstd_b"], stats["bias_b"]
    t0s, t1s = stats["t0"], stats["t1"]

    def transpose_into(src_ap, src_keys, dst_ap, dst_keys, i):
        pt = psumT[i % 2]
        p.op("pe",
             lambda e: e.transpose(pt[:], src_ap, ident[:]),
             reads=tuple(src_keys) + (("ident",),), writes=(("psumT", i % 2),))
        p.op("dve",
             lambda e: e.tensor_copy(dst_ap, pt[:]),
             reads=(("psumT", i % 2),), writes=tuple(dst_keys))

    NB = NTILE // 4
    for blk in range(NB):
        batch_start = (blk % (NB // B_LOC) == 0)
        if batch_start:
            p.op("pool", lambda e: e.memset(kev[:, :, 0:128], 0.0),
                 reads=(("kfmhalo",),), writes=(("kfmhalo",),))
            p.op("pool", lambda e: e.memset(kod[:, :, 0:128], 0.0),
                 reads=(("kfmhalo",),), writes=(("kfmhalo",),))
            p.op("pool", lambda e: e.memset(vpad[:, 0, :, :], 0.0),
                 reads=(("vpadhalo",),), writes=(("vpadhalo",),))
        for tt in range(4):
            tile = blk * 4 + tt
            emit_ln_stats(p, X[:, tile, :], tile, sum_a, ssq_a, tile, scr, ("X", tile))
        emit_ln_finalize2(p, sum_a, ssq_a, rstd_a, bias_a, t0s, t1s,
                          (blk * 4, blk * 4 + 4), D)
        for tt in range(4):
            tile = blk * 4 + tt
            yt = ytm[tt % 2]
            emit_ln_norm(p, X[:, tile, :], (("X", tile),), yt[:], (("ytm", tt % 2),),
                         rstd_a, bias_a, tile, tag=f"b1 ln {tile}")
            for j in range(8):
                transpose_into(yt[:, j * 128:(j + 1) * 128], (("ytm", tt % 2),),
                               yfm[:, j, tt * 128:(tt + 1) * 128], (("b1yfm", j, tt),),
                               tt * 8 + j)
        ykeys = lambda j: tuple(("b1yfm", j, t_) for t_ in range(4))
        if stage < 2:
            continue
        for wT, wname, okey in ((wqT, "wq", "qfm"), (wkT, "wk", "kfm")):
            for jj in range(8):
                bank = jj % 2
                pk = psum[bank]
                for j in range(8):
                    p.op("pe",
                         lambda e, j=j, jj=jj, pk=pk, wT=wT: e.matmul(
                             pk[:], wT[:, j, jj * 128:(jj + 1) * 128], yfm[:, j, :],
                             start=(j == 0), stop=(j == 7)),
                         reads=(("WT", wname, j),) + ykeys(j),
                         writes=(("psum", bank),))
                if okey == "qfm":
                    p.op("dve",
                         lambda e, jj=jj, pk=pk: e.tensor_copy(qfm[:, jj, :], pk[:]),
                         reads=(("psum", bank),), writes=(("qfm", jj),))
                else:
                    p.op("dve",
                         lambda e, jj=jj, pk=pk: e.tensor_copy(kev[0:64, jj, 128:640], pk[0:64, :]),
                         reads=(("psum", bank),), writes=(("kfm", jj),))
                    p.op("act",
                         lambda e, jj=jj, pk=pk: e.activation(out=kod[64:128, jj, 128:640],
                                                              in_=pk[64:128, :], func=AF.Identity),
                         reads=(("psum", bank),), writes=(("kfm", jj),))
        for tt in range(4):
            for half in range(2):
                bank = 2 + half
                pv = psum[bank]
                for j in range(8):
                    p.op("pe",
                         lambda e, j=j, tt=tt, half=half, pv=pv: e.matmul(
                             pv[:], yfm[:, j, tt * 128:(tt + 1) * 128],
                             wvT[:, j, half * 512:(half + 1) * 512],
                             start=(j == 0), stop=(j == 7)),
                         reads=(("WT", "wv", j),) + ykeys(j), writes=(("psum", bank),))
                p.op("dve",
                     lambda e, tt=tt, half=half, pv=pv: e.tensor_copy(
                         vpad[:, tt + 1, half * 8:(half + 1) * 8, 0:64], pv[:]),
                     reads=(("psum", bank),), writes=(("vpad", tt + 1, half),))
        def vpad_keys(slot):
            if slot == 0:
                return (("vpadhalo",), ("vpadones",))
            return (("vpad", slot, 0), ("vpad", slot, 1), ("vpadones",))
        if stage < 2.25:
            continue
        for tq in range(4):
            mask = maskA
            mkey = ("maskA",)
            for hp in range(8):
                bank = 2 + hp % 2
                ppsc = psum[bank]
                es = expS[hp % 2]
                for hi in range(2):
                    ksrc = kev if hi == 0 else kod
                    for kt in range(2):
                        kslot = tq + kt
                        kread = (("kfmhalo",),) if kslot == 0 else (("kfm", hp),)
                        idx = hi * 2 + kt
                        p.op("pe",
                             lambda e, hp=hp, kslot=kslot, tq=tq, ppsc=ppsc, idx=idx, ksrc=ksrc: e.matmul(
                                 ppsc[:, idx * 128:(idx + 1) * 128],
                                 ksrc[:, hp, kslot * 128:(kslot + 1) * 128],
                                 qfm[:, hp, tq * 128:(tq + 1) * 128],
                                 start=(idx == 0), stop=(idx == 3), skip_group_check=True),
                             reads=(("qfm", hp), ("kzero",)) + kread,
                             writes=(("psum", bank),))
                if stage < 2.5:
                    continue
                p.op("act",
                     lambda e, es=es, ppsc=ppsc: e.activation(out=es[:], in_=ppsc[:],
                                                              func=AF.Exp, scale=0.125),
                     reads=(("psum", bank),), writes=(("expS", hp % 2),))
                if stage < 2.75:
                    continue
                p.op("dve",
                     lambda e, es=es, mask=mask: e.tensor_tensor(es[:], es[:], mask[:], ALU.mult),
                     reads=(("expS", hp % 2), mkey), writes=(("expS", hp % 2),))
                if stage < 4:
                    continue
                for hi in range(2):
                    h = 2 * hp + hi
                    pav = psum_av[hi]
                    for kt in range(2):
                        kslot = tq + kt
                        p.op("pe",
                             lambda e, hi=hi, kt=kt, h=h, kslot=kslot, pav=pav, es=es: e.matmul(
                                 pav[:],
                                 es[:, (hi * 2 + kt) * 128:(hi * 2 + kt + 1) * 128],
                                 vpad[:, kslot, h, :],
                                 start=(kt == 0), stop=(kt == 1)),
                             reads=(("expS", hp % 2),) + vpad_keys(kslot),
                             writes=(("psum_av", hi),))
                    p.op("dve",
                         lambda e, pav=pav, h=h: e.reciprocal(rcol[:, h:h + 1], pav[:, 64:65]),
                         reads=(("psum_av", hi),), writes=(("rcol", h),))
                    if hi == 0:
                        p.op("act",
                             lambda e, h=h, tq=tq, pav=pav: e.activation(
                                 out=otm[:, tq % 2, h * 64:(h + 1) * 64], in_=pav[:, 0:64],
                                 func=AF.Identity, scale=rcol[:, h:h + 1]),
                             reads=(("psum_av", hi), ("rcol", h)), writes=(("otm", tq % 2, h),))
                    else:
                        p.op("dve",
                             lambda e, h=h, tq=tq, pav=pav: e.tensor_scalar(
                                 otm[:, tq % 2, h * 64:(h + 1) * 64], pav[:, 0:64],
                                 rcol[:, h:h + 1], None, ALU.mult),
                             reads=(("psum_av", hi), ("rcol", h)), writes=(("otm", tq % 2, h),))
            if stage < 5:
                continue
            okeys = tuple(("otm", tq % 2, h) for h in range(16))
            for j in range(8):
                transpose_into(otm[:, tq % 2, j * 128:(j + 1) * 128], okeys,
                               ofm[:, tq % 2, j, :], (("ofm", j, tq % 2),),
                               tq * 8 + j)
            for half in range(2):
                bank = half
                po = psum[bank]
                for j in range(8):
                    p.op("pe",
                         lambda e, j=j, tq=tq, half=half, po=po: e.matmul(
                             po[:], ofm[:, tq % 2, j, :],
                             woT[:, j, half * 512:(half + 1) * 512],
                             start=(j == 0), stop=(j == 7)),
                         reads=(("WT", "wo", j), ("ofm", j, tq % 2)), writes=(("psum", bank),))
                p.op("dve",
                     lambda e, tq=tq, half=half, po=po: e.tensor_tensor(
                         fblk[:, tq % 2, half * 512:(half + 1) * 512], po[:],
                         bo_bc[:, half * 512:(half + 1) * 512], ALU.add),
                     reads=(("psum", bank), ("bo_bc",)), writes=(("fblk", tq % 2, half),))
            if tq % 2 == 1:
                emit_post_ln_residual2(p, X, fblk, scr, lnf, sum_b, ssq_b, rstd_b,
                                       bias_b, t0s, t1s, blk * 4 + tq - 1)
        if stage < 3:
            continue
        p.op("dve", lambda e: e.tensor_copy(kev[:, :, 0:128], kev[:, :, 512:640]),
             reads=tuple(("kfm", jj) for jj in range(8)), writes=(("kfmhalo",),))
        p.op("dve", lambda e: e.tensor_copy(kod[:, :, 0:128], kod[:, :, 512:640]),
             reads=tuple(("kfm", jj) for jj in range(8)), writes=(("kfmhalo",),))
        p.op("dve", lambda e: e.tensor_copy(vpad[:, 0, :, :], vpad[:, 4, :, :]),
             reads=(("vpad", 4, 0), ("vpad", 4, 1), ("vpadones",)), writes=(("vpadhalo",),))


def emit_b1_setup(p, nc, bufs, bo_d):
    maskA = bufs["maskA"]
    vpad = bufs["vpad"]
    kev, kod = bufs["kev"], bufs["kod"]
    bo_bc = bufs["bo_bc"]
    for m, key, with_prev in ((maskA, "maskA", True),):
        p.op("pool", lambda e, m=m: e.memset(m[:], 1.0), writes=((key,),))
        for rep in range(2):
            off = rep * 256
            p.op("pool", lambda e, m=m, off=off: e.affine_select(
                m[:, off + 128:off + 256], m[:, off + 128:off + 256],
                [[1, 128]], ALU.is_ge, 0.0, base=0, channel_multiplier=-1),
                reads=((key,),), writes=((key,),))
            p.op("pool", lambda e, m=m, off=off: e.affine_select(
                m[:, off + 128:off + 256], m[:, off + 128:off + 256],
                [[-1, 128]], ALU.is_ge, 0.0, base=63, channel_multiplier=1),
                reads=((key,),), writes=((key,),))
            if with_prev:
                p.op("pool", lambda e, m=m, off=off: e.affine_select(
                    m[:, off:off + 128], m[:, off:off + 128],
                    [[-1, 128]], ALU.is_ge, 0.0, base=-65, channel_multiplier=1),
                    reads=((key,),), writes=((key,),))
            else:
                p.op("pool", lambda e, m=m, off=off: e.memset(m[:, off:off + 128], 0.0),
                     reads=((key,),), writes=((key,),))
    p.op("pool", lambda e: e.memset(vpad[:, :, :, 64:65], 1.0), writes=(("vpadones",),))
    p.op("pool", lambda e: e.memset(kev[64:128, :, :], 0.0), writes=(("kzero",),))
    p.op("pool", lambda e: e.memset(kod[0:64, :, :], 0.0), writes=(("kzero",),))
    p.op("pool", lambda e: e.dma_start(out=bo_bc[:, :], in_=bo_d[None, :].broadcast_to([128, D])),
         writes=(("bo_bc",),), inc=16)


def build_b1_program():
    nc = bass.Bass("TRN2", target_bir_lowering=False)
    stack = contextlib.ExitStack()
    x_d = nc.dram_tensor("x", [NTILE, 128, D], F32, kind="ExternalInput").ap()
    wq_d = nc.dram_tensor("wq", [D, D], F32, kind="ExternalInput").ap()
    wk_d = nc.dram_tensor("wk", [D, D], F32, kind="ExternalInput").ap()
    wv_d = nc.dram_tensor("wv", [D, D], F32, kind="ExternalInput").ap()
    wo_d = nc.dram_tensor("wo", [D, D], F32, kind="ExternalInput").ap()
    bo_d = nc.dram_tensor("bo", [D], F32, kind="ExternalInput").ap()
    out_d = nc.dram_tensor("out", [NTILE, 128, D], F32, kind="ExternalOutput").ap()
    scr_w = {n: nc.dram_tensor(f"wscr_{n}", [D, D], BF16, kind="Internal").ap()
             for n in ("wq", "wk", "wv", "wo")}

    p = Prog(nc)
    sb = lambda name, shape, dt: stack.enter_context(nc.sbuf_tensor(name, shape, dt))
    ps = lambda name, shape, dt: stack.enter_context(nc.psum_tensor(name, shape, dt))

    bufs = {}
    bufs["X"] = sb("X", [128, NTILE, D], BF16)
    for n in ("wq", "wk", "wv", "wo"):
        bufs[n + "T"] = sb(n + "T", [128, 8, D], BF16)
    bufs["b1_yfm"] = sb("b1_yfm", [128, 8, 512], BF16)
    bufs["qfm"] = sb("qfm", [128, 8, 512], BF16)
    bufs["kev"] = sb("kev", [128, 8, 640], BF16)
    bufs["kod"] = sb("kod", [128, 8, 640], BF16)
    bufs["vpad"] = sb("vpad", [128, 5, 16, 65], BF16)
    bufs["expS"] = [sb(f"expS{i}", [128, 512], BF16) for i in range(2)]
    bufs["otm"] = sb("otm", [128, 2, D], BF16)
    bufs["ofm"] = sb("ofm", [128, 2, 8, 128], BF16)
    bufs["fblk"] = sb("fblk", [128, 2, D], BF16)
    bufs["scr"] = sb("scr", [128, D], BF16)
    bufs["lnf"] = sb("lnf", [128, 2, D], BF16)
    bufs["ytm"] = [sb(f"ytm{i}", [128, D], BF16) for i in range(2)]
    bufs["ident"] = sb("ident", [128, 128], BF16)
    bufs["bo_bc"] = sb("bo_bc", [128, D], BF16)
    bufs["maskA"] = sb("maskA", [128, 512], BF16)
    bufs["rcol"] = sb("rcol", [128, 16], F32)
    bufs["psum"] = [ps(f"pb{i}", [128, 512], F32) for i in range(4)]
    bufs["psumT"] = [ps(f"psT{i}", [128, 128], BF16) for i in range(2)]
    bufs["psum_av"] = [ps(f"pav{i}", [128, 65], F32) for i in range(2)]
    stats = {}
    for nmm in ("sum_a", "ssq_a", "rstd_a", "bias_a", "sum_b", "ssq_b", "rstd_b", "bias_b", "t0", "t1"):
        stats[nmm] = sb("st_" + nmm, [128, NTILE, 2] if nmm.startswith("sum") else [128, NTILE], F32)
    p.bnscr = sb("bnscr", [128, 12], F32)
    bufs["stats"] = stats
    p.op("pool", lambda e: e.memset(bufs["ident"][:], 1.0), writes=(("ident",),))
    p.op("pool", lambda e: e.affine_select(bufs["ident"][:], bufs["ident"][:],
                                           [[-1, 128]], ALU.is_equal, 0.0,
                                           base=0, channel_multiplier=1),
         reads=(("ident",),), writes=(("ident",),))
    emit_b1_setup(p, nc, bufs, bo_d)
    wtmps = [bufs["ytm"][0], bufs["ytm"][1]]
    wtkeys = [("ytm", 0), ("ytm", 1)]
    for n, wd in (("wq", wq_d), ("wk", wk_d), ("wv", wv_d), ("wo", wo_d)):
        emit_weight_load_T(p, wd, scr_w[n], bufs[n + "T"], n, D, D, wtmps, wtkeys)
    for t in range(NTILE):
        p.op("pool", lambda e, t=t: e.dma_start(out=bufs["X"][:, t, :], in_=x_d[t, :, :]),
             writes=(("X", t),), inc=16)

    emit_block1(p, nc, bufs)

    for t in range(NTILE):
        p.op("pool", lambda e, t=t: e.dma_start(out=out_d[t, :, :], in_=bufs["X"][:, t, :]),
             reads=(("X", t),), writes=(), inc=16)

    p.build(stack)
    return nc, stack





def emit_weight_load_nat(p, w_dram, w_sb, name, din, dout):
    for j in range(din // 128):
        p.op("pool",
             lambda e, j=j: e.dma_start(out=w_sb[:, j, :], in_=w_dram[j * 128:(j + 1) * 128, :]),
             writes=(("WN", name, j),), inc=16, tag=f"wnat {name} {j}")


def emit_block3(p, nc, bufs):
    X = bufs["X"]
    yfm = bufs["b3_yfm"]
    qfm = bufs["qfm3"]
    kfm = bufs["kfm3"]
    sqf = bufs["sqf"]
    skf = bufs["skf"]
    vpad3 = bufs["vpad3"]
    esb = bufs["esb"]
    ablk = bufs["ablk"]
    Mz = bufs["Mz"]
    Mzbf = bufs["Mzbf"]
    vdp = bufs["vdp"]
    sktm = bufs["sktm"]
    atmp = bufs["atmp"]
    es1 = bufs["es1"]
    rq_t, rd_t, rk_t = bufs["rq_t"], bufs["rd_t"], bufs["rk_t"]
    scr = bufs["scr"]
    lnf = bufs["lnf"]
    ytm = bufs["ytm"]
    ident = bufs["ident"]
    cmask = bufs["cmask"]
    gbc, ombc = bufs["gbc"], bufs["ombc"]
    bqc, bkc = bufs["bqc"], bufs["bkc"]
    bvbc = bufs["bvbc"]
    wqN, wkN, wvN = bufs["wqN"], bufs["wkN"], bufs["wvN"]
    pj = bufs["pj"]
    pdot, pmem, pMz = bufs["pdot"], bufs["pmem"], bufs["pMz"]
    psumT = bufs["psumT"]
    stats = bufs["stats"]
    sum_a, ssq_a = stats["sum_a"], stats["ssq_a"]
    rstd_a, bias_a = stats["rstd_a"], stats["bias_a"]
    sum_b, ssq_b = stats["sum_b"], stats["ssq_b"]
    rstd_b, bias_b = stats["rstd_b"], stats["bias_b"]
    t0s, t1s = stats["t0"], stats["t1"]
    SCALE = float(1.0 / np.sqrt(128.0))
    EPS = 1e-6

    def transpose_into(src_ap, src_keys, dst_ap, dst_keys, i):
        pt = psumT[i % 2]
        p.op("pe", lambda e: e.transpose(pt[:], src_ap, ident[:]),
             reads=tuple(src_keys) + (("ident",),), writes=(("psumT", i % 2),))
        p.op("dve", lambda e: e.tensor_copy(dst_ap, pt[:]),
             reads=(("psumT", i % 2),), writes=tuple(dst_keys))

    NB = NTILE // 4
    for blk in range(NB):
        seg = blk % 4
        if seg == 0:
            p.op("dve", lambda e: e.memset(Mz[:], 0.0),
                 reads=(("Mz",),), writes=(("Mz",),))
            p.op("dve", lambda e: e.memset(Mzbf[:], 0.0),
                 reads=(("Mzbf",),), writes=(("Mzbf",),))
        for tt in range(4):
            tile = blk * 4 + tt
            emit_ln_stats(p, X[:, tile, :], tile, sum_a, ssq_a, tile, scr, ("X", tile))
        emit_ln_finalize2(p, sum_a, ssq_a, rstd_a, bias_a, t0s, t1s,
                          (blk * 4, blk * 4 + 4), D)
        for tt in range(4):
            tile = blk * 4 + tt
            yt = ytm[tt % 2]
            emit_ln_norm(p, X[:, tile, :], (("X", tile),), yt[:], (("ytm", tt % 2),),
                         rstd_a, bias_a, tile, tag=f"b3 ln {tile}")
            for j in range(8):
                transpose_into(yt[:, j * 128:(j + 1) * 128], (("ytm", tt % 2),),
                               yfm[:, j, tt * 128:(tt + 1) * 128], (("b3yfm", j, tt),),
                               tt * 8 + j)
        ykeys = lambda j: tuple(("b3yfm", j, t_) for t_ in range(4))
        for wN, wname, dst, dkey, bias in ((wqN, "wq", qfm, "qfm3", bqc),
                                           (wkN, "wk", kfm, "kfm3", bkc)):
            for jj in range(8):
                bank = jj % 2
                pk = pj[bank]
                for j in range(8):
                    p.op("pe",
                         lambda e, j=j, jj=jj, pk=pk, wN=wN: e.matmul(
                             pk[:], wN[:, j, jj * 128:(jj + 1) * 128], yfm[:, j, :],
                             start=(j == 0), stop=(j == 7)),
                         reads=(("WN", wname, j),) + ykeys(j), writes=(("pj", bank),))
                p.op("act",
                     lambda e, jj=jj, pk=pk, dst=dst, bias=bias: e.activation(
                         out=dst[:, jj, :], in_=pk[:], func=AF.Identity,
                         bias=bias[:, jj:jj + 1]),
                     reads=(("pj", bank), ("bias3",)), writes=((dkey, jj),))
        for tt in range(4):
            for half in range(2):
                bank = half
                pv = pj[bank]
                for j in range(8):
                    p.op("pe",
                         lambda e, j=j, tt=tt, half=half, pv=pv: e.matmul(
                             pv[:], yfm[:, j, tt * 128:(tt + 1) * 128],
                             wvN[:, j, half * 512:(half + 1) * 512],
                             start=(j == 0), stop=(j == 7)),
                         reads=(("WN", "wv", j),) + ykeys(j), writes=(("pj", bank),))
                p.op("dve",
                     lambda e, tt=tt, half=half, pv=pv: e.tensor_tensor(
                         vpad3[:, tt, half * 4:(half + 1) * 4, 0:128], pv[:],
                         bvbc[:, half * 4:(half + 1) * 4, :], ALU.add),
                     reads=(("pj", bank), ("bias3",)), writes=(("vp3", tt, half),))
        vkeys = lambda tt: (("vp3", tt, 0), ("vp3", tt, 1), ("vp3ones",))
        for src_t, skey, dst, dkey in ((qfm, "qfm3", sqf, "sqf"), (kfm, "kfm3", skf, "skf")):
            for h in range(8):
                sl = h % 2
                e1 = es1[:, sl, :]
                p.op("dve",
                     lambda e, h=h, src_t=src_t, e1=e1: e.tensor_scalar_min(e1, src_t[:, h, :], 0.0),
                     reads=((skey, h),), writes=(("es1", sl),))
                p.op("act",
                     lambda e, e1=e1: e.activation(out=e1, in_=e1, func=AF.Exp),
                     reads=(("es1", sl),), writes=(("es1", sl),))
                p.op("dve",
                     lambda e, h=h, src_t=src_t, dst=dst, e1=e1: e.scalar_tensor_tensor(
                         dst[:, h, :], src_t[:, h, :], 0.0, e1, ALU.max, ALU.add),
                     reads=((skey, h), ("es1", sl)), writes=((dkey, h),))
        for h in range(8):
            for j in range(4):
                bank = j % 2
                psc = pj[bank]
                n = 512 - 128 * j
                p.op("pe",
                     lambda e, h=h, j=j, psc=psc, n=n: e.matmul(
                         psc[:, 0:n], kfm[:, h, j * 128:(j + 1) * 128],
                         qfm[:, h, j * 128:512], start=True, stop=True),
                     reads=(("kfm3", h), ("qfm3", h)), writes=(("pj", bank),))
                p.op("act",
                     lambda e, h=h, j=j, psc=psc, n=n: e.activation(
                         out=esb[:, j, j * 128:512], in_=psc[:, 0:n], func=AF.Exp,
                         scale=SCALE),
                     reads=(("pj", bank),), writes=(("esb", j),))
                p.op("dve",
                     lambda e, j=j, n=n: e.tensor_tensor(
                         esb[:, j, j * 128:512], esb[:, j, j * 128:512],
                         cmask[:, j, j * 128:512], ALU.mult),
                     reads=(("esb", j), ("cmask",)), writes=(("esb", j),))
            for i in range(4):
                pmn = pmem[i % 2]
                pd = pdot[i % 2]
                p.op("pe",
                     lambda e, h=h, i=i, pmn=pmn: e.matmul(
                         pmn[:, 0:129], sqf[:, h, i * 128:(i + 1) * 128], Mzbf[:, h, :],
                         start=True, stop=False, skip_group_check=True),
                     reads=(("sqf", h), ("Mzbf",)), writes=(("pmem", 0),))
                p.op("pe",
                     lambda e, h=h, i=i, pmn=pmn: e.matmul(
                         pmn[:, 129:258], skf[:, h, i * 128:(i + 1) * 128], Mzbf[:, h, :],
                         start=False, stop=True, skip_group_check=True),
                     reads=(("skf", h), ("Mzbf",)), writes=(("pmem", 0),))
                for j in range(i + 1):
                    p.op("pe",
                         lambda e, h=h, i=i, j=j, pd=pd: e.matmul(
                             pd[:], esb[:, j, i * 128:(i + 1) * 128],
                             vpad3[:, j, h, :], start=(j == 0), stop=(j == i)),
                         reads=(("esb", j),) + vkeys(j), writes=(("pdot", i % 2),))
                rkv = rk_t[:, i:i + 1]
                p.op("dve", lambda e, rkv=rkv, pmn=pmn: e.tensor_scalar(rkv, pmn[:, 257:258], -1.0, -EPS, ALU.mult, ALU.add),
                     reads=(("pmem", 0),), writes=(("rk_t", i),))
                p.op("dve", lambda e, rkv=rkv: e.reciprocal(rkv, rkv),
                     reads=(("rk_t", i),), writes=(("rk_t", i),))
                vd = vdp[i % 2]
                p.op("dve",
                     lambda e, h=h, i=i, vd=vd, rkv=rkv, pmn=pmn: e.scalar_tensor_tensor(
                         vd[:, 0:128], pmn[:, 129:257], rkv,
                         vpad3[:, i, h, 0:128], ALU.mult, ALU.add),
                     reads=(("pmem", 0), ("rk_t", i)) + vkeys(i), writes=(("vdp", i % 2),))
                transpose_into(skf[:, h, i * 128:(i + 1) * 128], (("skf", h),),
                               sktm[i % 2][:], (("sktm", i % 2),), i)
                p.op("pe",
                     lambda e, i=i, vd=vd: e.matmul(
                         pMz[:], sktm[i % 2][:], vd[:], start=(i == 0), stop=(i == 3)),
                     reads=(("sktm", i % 2), ("vdp", i % 2), ("vdpones",)),
                     writes=(("pMz",),))
                rqv = rq_t[:, i:i + 1]
                rdv = rd_t[:, i:i + 1]
                at = atmp[:, i % 2, :]
                p.op("dve", lambda e, rqv=rqv, pmn=pmn: e.tensor_scalar(rqv, pmn[:, 128:129], 1.0, EPS, ALU.mult, ALU.add),
                     reads=(("pmem", 0),), writes=(("rq_t", i),))
                p.op("dve", lambda e, rqv=rqv: e.reciprocal(rqv, rqv),
                     reads=(("rq_t", i),), writes=(("rq_t", i),))
                p.op("dve", lambda e, rdv=rdv, pd=pd: e.reciprocal(rdv, pd[:, 128:129]),
                     reads=(("pdot", i % 2),), writes=(("rd_t", i),))
                p.op("dve",
                     lambda e, h=h, rdv=rdv: e.tensor_scalar(rdv, rdv, ombc[:, h:h + 1], None, ALU.mult),
                     reads=(("rd_t", i), ("gom",)), writes=(("rd_t", i),))
                p.op("act",
                     lambda e, h=h, at=at, rqv=rqv, pmn=pmn: e.activation(
                         out=at, in_=pmn[:, 0:128], func=AF.Identity, scale=rqv),
                     reads=(("pmem", 0), ("rq_t", i), ("gom",)), writes=(("atmp", i % 2),))
                p.op("dve",
                     lambda e, h=h, at=at: e.tensor_scalar(at, at, gbc[:, h:h + 1], None, ALU.mult),
                     reads=(("atmp", i % 2), ("gom",)), writes=(("atmp", i % 2),))
                p.op("dve",
                     lambda e, h=h, i=i, at=at, rdv=rdv, pd=pd: e.scalar_tensor_tensor(
                         ablk[:, i, h * 128:(h + 1) * 128], pd[:, 0:128],
                         rdv, at, ALU.mult, ALU.add),
                     reads=(("pdot", i % 2), ("rd_t", i), ("atmp", i % 2)), writes=(("ablk", i, h),))
            p.op("dve",
                 lambda e, h=h: e.tensor_tensor(Mz[:, h, :], Mz[:, h, :], pMz[:], ALU.add),
                 reads=(("pMz",), ("Mz",)), writes=(("Mz",),))
            p.op("dve",
                 lambda e, h=h: e.tensor_copy(Mzbf[:, h, :], Mz[:, h, :]),
                 reads=(("Mz",),), writes=(("Mzbf",),))
        for tt in range(4):
            tile = blk * 4 + tt
            akeys = tuple(("ablk", tt, h) for h in range(8))
            fa = ablk[:, tt, :]
            emit_ln_stats(p, fa, tile, sum_b, None, tile, None, akeys)
        emit_ln_finalize2(p, sum_b, ssq_b, rstd_b, bias_b, t0s, t1s,
                          (blk * 4, blk * 4 + 4), D)
        for tt in range(4):
            tile = blk * 4 + tt
            slot = tile % 2
            akeys = tuple(("ablk", tt, h) for h in range(8))
            emit_ln_norm(p, ablk[:, tt, :], akeys, lnf[:, slot, :], (("lnf", slot),),
                         rstd_b, bias_b, tile)
            p.op("pool",
                 lambda e, tile=tile, slot=slot: e.tensor_tensor(
                     X[:, tile, :], X[:, tile, :], lnf[:, slot, :], ALU.add),
                 reads=(("lnf", slot), ("X", tile)), writes=(("X", tile),))
        hook = bufs.get("tail_hook")
        if hook is not None:
            hook(blk)


def emit_b3_setup(p, nc, bufs, bq_d, bk_d, bv_d, beta_d):
    cmask = bufs["cmask"]
    vpad3 = bufs["vpad3"]
    vdp = bufs["vdp"]
    gbc, ombc = bufs["gbc"], bufs["ombc"]
    bqc, bkc, bvbc = bufs["bqc"], bufs["bkc"], bufs["bvbc"]
    for j in range(4):
        p.op("pool", lambda e, j=j: e.memset(cmask[:, j, :], 1.0),
             reads=(("cmask",),), writes=(("cmask",),))
        p.op("pool", lambda e, j=j: e.affine_select(
            cmask[:, j, :], cmask[:, j, :], [[1, 512]], ALU.is_ge, 0.0,
            base=-128 * j, channel_multiplier=-1),
            reads=(("cmask",),), writes=(("cmask",),))
    p.op("pool", lambda e: e.memset(vpad3[:, :, :, 128:129], 1.0), writes=(("vp3ones",),))
    for i in range(2):
        p.op("pool", lambda e, i=i: e.memset(vdp[i][:, 128:129], 1.0), writes=(("vdpones",),))
    for jj in range(8):
        p.op("sp", lambda e, jj=jj: e.dma_start(out=bqc[:, jj], in_=bq_d[jj * 128:(jj + 1) * 128]),
             writes=(("bias3",),), inc=16)
        p.op("sp", lambda e, jj=jj: e.dma_start(out=bkc[:, jj], in_=bk_d[jj * 128:(jj + 1) * 128]),
             writes=(("bias3",),), inc=16)
    p.op("pool", lambda e: e.dma_start(
        out=bvbc[:, :, :], in_=bv_d.rearrange("(a c) -> a c", a=8)[None, :, :].broadcast_to([128, 8, 128])),
        writes=(("bias3",),), inc=16)
    p.op("pool", lambda e: e.dma_start(
        out=gbc[:, :], in_=beta_d[None, :].broadcast_to([128, 8])),
        writes=(("gom",),), inc=16)
    p.op("act", lambda e: e.activation(out=gbc[:], in_=gbc[:], func=AF.Sigmoid),
         reads=(("gom",),), writes=(("gom",),))
    p.op("dve", lambda e: e.tensor_scalar(ombc[:], gbc[:], -1.0, 1.0, ALU.mult, ALU.add),
         reads=(("gom",),), writes=(("gom",),))


def build_b3_program():
    nc = bass.Bass("TRN2", target_bir_lowering=False)
    stack = contextlib.ExitStack()
    x_d = nc.dram_tensor("x", [NTILE, 128, D], F32, kind="ExternalInput").ap()
    wq_d = nc.dram_tensor("wq", [D, D], F32, kind="ExternalInput").ap()
    bq_d = nc.dram_tensor("bq", [D], F32, kind="ExternalInput").ap()
    wk_d = nc.dram_tensor("wk", [D, D], F32, kind="ExternalInput").ap()
    bk_d = nc.dram_tensor("bk", [D], F32, kind="ExternalInput").ap()
    wv_d = nc.dram_tensor("wv", [D, D], F32, kind="ExternalInput").ap()
    bv_d = nc.dram_tensor("bv", [D], F32, kind="ExternalInput").ap()
    beta_d = nc.dram_tensor("beta", [8], F32, kind="ExternalInput").ap()
    out_d = nc.dram_tensor("out", [NTILE, 128, D], F32, kind="ExternalOutput").ap()

    p = Prog(nc)
    sb = lambda name, shape, dt: stack.enter_context(nc.sbuf_tensor(name, shape, dt))
    ps = lambda name, shape, dt: stack.enter_context(nc.psum_tensor(name, shape, dt))

    bufs = {}
    bufs["X"] = sb("X", [128, NTILE, D], BF16)
    bufs["wqN"] = sb("wqN", [128, 8, D], BF16)
    bufs["wkN"] = sb("wkN", [128, 8, D], BF16)
    bufs["wvN"] = sb("wvN", [128, 8, D], BF16)
    bufs["b3_yfm"] = sb("b3_yfm", [128, 8, 512], BF16)
    bufs["qfm3"] = sb("qfm3", [128, 8, 512], BF16)
    bufs["kfm3"] = sb("kfm3", [128, 8, 512], BF16)
    bufs["sqf"] = sb("sqf", [128, 8, 512], BF16)
    bufs["skf"] = sb("skf", [128, 8, 512], BF16)
    bufs["vpad3"] = sb("vpad3", [128, 4, 8, 129], BF16)
    bufs["esb"] = sb("esb", [128, 4, 512], BF16)
    bufs["ablk"] = sb("ablk", [128, 4, D], BF16)
    bufs["Mz"] = sb("Mz", [128, 8, 129], F32)
    bufs["Mzbf"] = sb("Mzbf", [128, 8, 129], BF16)
    bufs["vdp"] = [sb(f"vdp{i}", [128, 129], BF16) for i in range(2)]
    bufs["sktm"] = [sb(f"sktm{i}", [128, 128], BF16) for i in range(2)]
    bufs["atmp"] = sb("atmp", [128, 2, 128], F32)
    bufs["es1"] = sb("es1", [128, 2, 512], F32)
    bufs["rq_t"] = sb("rq_t", [128, 4], F32)
    bufs["rd_t"] = sb("rd_t", [128, 4], F32)
    bufs["rk_t"] = sb("rk_t", [128, 4], F32)
    bufs["scr"] = sb("scr", [128, D], BF16)
    bufs["lnf"] = sb("lnf", [128, 2, D], BF16)
    bufs["ytm"] = [sb(f"ytm{i}", [128, D], BF16) for i in range(2)]
    bufs["ident"] = sb("ident", [128, 128], BF16)
    bufs["cmask"] = sb("cmask", [128, 4, 512], BF16)
    bufs["gbc"] = sb("gbc", [128, 8], F32)
    bufs["ombc"] = sb("ombc", [128, 8], F32)
    bufs["bqc"] = sb("bqc", [128, 8], F32)
    bufs["bkc"] = sb("bkc", [128, 8], F32)
    bufs["bvbc"] = sb("bvbc", [128, 8, 128], BF16)
    bufs["pj"] = [ps(f"pj{i}", [128, 512], F32) for i in range(2)]
    bufs["pdot"] = [ps(f"pdot{i}", [128, 129], F32) for i in range(2)]
    bufs["pmem"] = [ps("pmem0", [128, 258], F32)] * 2
    bufs["pMz"] = ps("pMz", [128, 129], F32)
    bufs["psumT"] = [ps(f"psT{i}", [128, 128], BF16) for i in range(2)]
    stats = {}
    for nmm in ("sum_a", "ssq_a", "rstd_a", "bias_a", "sum_b", "ssq_b", "rstd_b", "bias_b", "t0", "t1"):
        stats[nmm] = sb("st_" + nmm, [128, NTILE, 2] if nmm.startswith("sum") else [128, NTILE], F32)
    p.bnscr = sb("bnscr", [128, 12], F32)
    bufs["stats"] = stats

    p.op("pool", lambda e: e.memset(bufs["ident"][:], 1.0), writes=(("ident",),))
    p.op("pool", lambda e: e.affine_select(bufs["ident"][:], bufs["ident"][:],
                                           [[-1, 128]], ALU.is_equal, 0.0,
                                           base=0, channel_multiplier=1),
         reads=(("ident",),), writes=(("ident",),))
    emit_b3_setup(p, nc, bufs, bq_d, bk_d, bv_d, beta_d)
    emit_weight_load_nat(p, wq_d, bufs["wqN"], "wq", D, D)
    emit_weight_load_nat(p, wk_d, bufs["wkN"], "wk", D, D)
    emit_weight_load_nat(p, wv_d, bufs["wvN"], "wv", D, D)
    for t in range(NTILE):
        p.op("pool", lambda e, t=t: e.dma_start(out=bufs["X"][:, t, :], in_=x_d[t, :, :]),
             writes=(("X", t),), inc=16)

    emit_block3(p, nc, bufs)

    for t in range(NTILE):
        p.op("pool", lambda e, t=t: e.dma_start(out=out_d[t, :, :], in_=bufs["X"][:, t, :]),
             reads=(("X", t),), writes=(), inc=16)

    p.build(stack)
    return nc, stack





def emit_block4_shared(p, nc, bufs, y4_out, f_out):
    X = bufs["X"]
    yfm = bufs["b4_yfm"]
    shfm = bufs["shfm"]
    sc1 = bufs["sc1"]
    sc2 = bufs["sc2"]
    gbuf = bufs["gbuf"]
    shbf = bufs["shbf"]
    ftile = bufs["ftile"]
    y4t = bufs["y4t"]
    scr = bufs["scr"]
    ytm = bufs["ytm"]
    ident = bufs["ident"]
    sw1T, sw2T, sw3T = bufs["sw1T"], bufs["sw2T"], bufs["sw3T"]
    sb1bc, sb2bc, sb3bc = bufs["sb1bc"], bufs["sb2bc"], bufs["sb3bc"]
    slgbc, slbbc = bufs["slgbc"], bufs["slbbc"]
    pj = bufs["pj"]
    psumT = bufs["psumT"]
    stats = bufs["stats"]
    sum_a, ssq_a = stats["sum_a"], stats["ssq_a"]
    rstd_a, bias_a = stats["rstd_a"], stats["bias_a"]
    sum_b, ssq_b = stats["sum_b"], stats["ssq_b"]
    rstd_b, bias_b = stats["rstd_b"], stats["bias_b"]
    t0s, t1s = stats["t0"], stats["t1"]

    def transpose_into(src_ap, src_keys, dst_ap, dst_keys, i):
        pt = psumT[i % 2]
        p.op("pe", lambda e: e.transpose(pt[:], src_ap, ident[:]),
             reads=tuple(src_keys) + (("ident",),), writes=(("psumT", i % 2),))
        p.op("dve", lambda e: e.tensor_copy(dst_ap, pt[:]),
             reads=(("psumT", i % 2),), writes=tuple(dst_keys))

    NB = NTILE // 4
    split = bool(bufs.get("split_y4"))
    y4bf = bufs.get("y4bf")
    SLW = sc1.shape[1]
    NPJ = len(pj)
    if split and not bufs.get("skip_y4_pass"):
        for blk in range(NB):
            for tt in range(4):
                tile = blk * 4 + tt
                emit_ln_stats(p, X[:, tile, :], tile, sum_a, ssq_a, tile, scr[:, 0, :], ("X", tile))
            emit_ln_finalize2(p, sum_a, ssq_a, rstd_a, bias_a, t0s, t1s,
                              (blk * 4, blk * 4 + 4), D)
            for tt in range(4):
                tile = blk * 4 + tt
                y4v = y4t[:, tt % 2, :]
                emit_ln_norm(p, X[:, tile, :], (("X", tile),), y4v, (("y4t", tt % 2),),
                             rstd_a, bias_a, tile, tag=f"b4 ln {tile}")
                p.op("pool", lambda e, tile=tile, y4v=y4v: e.dma_start(out=y4_out[tile, :, :], in_=y4v),
                     reads=(("y4t", tt % 2),), writes=(("y4w", tile),), inc=16)
        hook = bufs.get("rs_hook")
        if hook is not None:
            hook()
    for blk in range(NB):
        if not split:
            for tt in range(4):
                tile = blk * 4 + tt
                emit_ln_stats(p, X[:, tile, :], tile, sum_a, ssq_a, tile, scr[:, 0, :], ("X", tile))
            emit_ln_finalize2(p, sum_a, ssq_a, rstd_a, bias_a, t0s, t1s,
                              (blk * 4, blk * 4 + 4), D)
        if split and y4bf is not None:
            r0, r1 = blk * 4 * 128, (blk + 1) * 4 * 128
            y4flat = bufs["y4flat"]
            p.op("pool", lambda e, r0=r0, r1=r1, y4flat=y4flat: e.dma_start(
                out=y4bf[r0:r1, :], in_=y4flat[r0:r1, :]),
                reads=tuple(("y4w", blk * 4 + tt) for tt in range(4)),
                writes=(("y4bfw", blk),), inc=16)
            for tt in range(4):
                tile = blk * 4 + tt
                for j in range(8):
                    p.op("sp", lambda e, tile=tile, tt=tt, j=j: e.dma_start_transpose(
                        yfm[:, j, tt * 128:(tt + 1) * 128],
                        y4bf[tile * 128:(tile + 1) * 128, j * 128:(j + 1) * 128]),
                        reads=(("y4bfw", blk),), writes=(("b4yfm", j, tt),), inc=16)
        else:
            for tt in range(4):
                tile = blk * 4 + tt
                yt = ytm[tt % 2]
                y4v = y4t[:, tt % 2, :]
                if split:
                    p.op("pool", lambda e, tile=tile, yt=yt: e.dma_start(out=yt[:], in_=y4_out[tile, :, :]),
                         reads=(("ytm", tt % 2), ("y4w", tile)), writes=(("ytm", tt % 2),), inc=16)
                else:
                    emit_ln_norm(p, X[:, tile, :], (("X", tile),), y4v, (("y4t", tt % 2),),
                                 rstd_a, bias_a, tile, tag=f"b4 ln {tile}")
                    p.op("pool", lambda e, tile=tile, y4v=y4v: e.dma_start(out=y4_out[tile, :, :], in_=y4v),
                         reads=(("y4t", tt % 2),), writes=(("y4w", tile),), inc=16)
                    p.op("dve", lambda e, yt=yt, y4v=y4v: e.tensor_copy(yt[:], y4v),
                         reads=(("y4t", tt % 2),), writes=(("ytm", tt % 2),))
                for j in range(8):
                    transpose_into(yt[:, j * 128:(j + 1) * 128], (("ytm", tt % 2),),
                                   yfm[:, j, tt * 128:(tt + 1) * 128], (("b4yfm", j, tt),),
                                   tt * 8 + j)
        for tt in range(4):
            tile = blk * 4 + tt
            sl = tt % SLW
            s1 = sc1[:, sl, :]
            s2 = sc2[:, sl, :]
            gb = gbuf[:, sl, :]
            shb = shbf[:, sl, :]
            ft = ftile[:, sl, :]
            if NPJ >= 6:
                bk1, bk2, bk3 = tt % 2, 2 + tt % 2, 4
            else:
                bk1, bk2, bk3 = 0, 1, 2
            for w_, wname, bank in ((sw1T, "sw1", bk1), (sw2T, "sw2", bk2)):
                pk = pj[bank]
                for j in range(8):
                    p.op("pe",
                         lambda e, j=j, tt=tt, pk=pk, w_=w_: e.matmul(
                             pk[:], yfm[:, j, tt * 128:(tt + 1) * 128], w_[:, j, :],
                             start=(j == 0), stop=(j == 7)),
                         reads=(("WT", wname, j), ("b4yfm", j, tt)), writes=(("pj", bank),))
            p.op("dve", lambda e, s1=s1, bk1=bk1: e.tensor_tensor(s1, pj[bk1][:], sb1bc[:], ALU.add),
                 reads=(("pj", bk1), ("b4c",)), writes=(("sc1", sl),))
            p.op("act", lambda e, s1=s1: e.activation(out=s1, in_=s1, func=AF.Silu),
                 reads=(("sc1", sl),), writes=(("sc1", sl),))
            p.op("dve", lambda e, s2=s2, bk2=bk2: e.tensor_tensor(s2, pj[bk2][:], sb2bc[:], ALU.add),
                 reads=(("pj", bk2), ("b4c",)), writes=(("sc2", sl),))
            p.op("dve", lambda e, s1=s1, s2=s2, gb=gb: e.tensor_tensor(gb, s1, s2, ALU.mult),
                 reads=(("sc1", sl), ("sc2", sl)), writes=(("gbuf", tt),))
            emit_ln_stats(p, gb, tile, sum_b, None, tile, None, (("gbuf", tt),))
            emit_ln_finalize2(p, sum_b, ssq_b, rstd_b, bias_b, t0s, t1s,
                              (tile, tile + 1), 512)
            emit_ln_norm(p, gb, (("gbuf", tt),), s1, (("sc1", sl),),
                         rstd_b, bias_b, tile)
            p.op("dve", lambda e, s1=s1: e.tensor_tensor(s1, s1, slgbc[:], ALU.mult),
                 reads=(("sc1", sl), ("b4c",)), writes=(("sc1", sl),))
            p.op("dve", lambda e, s1=s1, shb=shb: e.tensor_tensor(shb, s1, slbbc[:], ALU.add),
                 reads=(("sc1", sl), ("b4c",)), writes=(("shbf", sl),))
            for c in range(4):
                transpose_into(shb[:, c * 128:(c + 1) * 128], (("shbf", sl),),
                               shfm[:, sl, c, :], (("shfm", sl, c),), c)
            for half in range(2):
                bank = bk3 + half
                pk = pj[bank]
                for c in range(4):
                    p.op("pe",
                         lambda e, c=c, half=half, pk=pk, sl=sl: e.matmul(
                             pk[:], shfm[:, sl, c, :], sw3T[:, c, half * 512:(half + 1) * 512],
                             start=(c == 0), stop=(c == 3)),
                         reads=(("WT", "sw3", c), ("shfm", sl, c)), writes=(("pj", bank),))
                p.op("dve",
                     lambda e, half=half, pk=pk, ft=ft: e.tensor_tensor(
                         ft[:, half * 512:(half + 1) * 512], pk[:],
                         sb3bc[:, half * 512:(half + 1) * 512], ALU.add),
                     reads=(("pj", bank), ("b4c",)), writes=(("ftile", sl, half),))
            p.op("pool", lambda e, tile=tile, ft=ft: e.dma_start(out=f_out[tile, :, :], in_=ft),
                 reads=(("ftile", sl, 0), ("ftile", sl, 1)), writes=(("fw", tile),), inc=16)
            f2 = bufs.get("f_out2")
            if f2 is not None:
                p.op("sp", lambda e, tile=tile, ft=ft, f2=f2: e.dma_start(out=f2[tile, :, :], in_=ft),
                     reads=(("ftile", sl, 0), ("ftile", sl, 1)), writes=(("fw2", tile),), inc=16)


def build_b4a_program():
    nc = bass.Bass("TRN2", target_bir_lowering=False)
    stack = contextlib.ExitStack()
    x_d = nc.dram_tensor("x", [NTILE, 128, D], F32, kind="ExternalInput").ap()
    sw1_d = nc.dram_tensor("sw1", [512, D], F32, kind="ExternalInput").ap()
    sb1_d = nc.dram_tensor("sb1", [512], F32, kind="ExternalInput").ap()
    sw2_d = nc.dram_tensor("sw2", [512, D], F32, kind="ExternalInput").ap()
    sb2_d = nc.dram_tensor("sb2", [512], F32, kind="ExternalInput").ap()
    sw3_d = nc.dram_tensor("sw3", [D, 512], F32, kind="ExternalInput").ap()
    sb3_d = nc.dram_tensor("sb3", [D], F32, kind="ExternalInput").ap()
    slg_d = nc.dram_tensor("slg", [512], F32, kind="ExternalInput").ap()
    slb_d = nc.dram_tensor("slb", [512], F32, kind="ExternalInput").ap()
    y4_out = nc.dram_tensor("y4", [NTILE, 128, D], F32, kind="ExternalOutput").ap()
    f_out = nc.dram_tensor("f", [NTILE, 128, D], F32, kind="ExternalOutput").ap()
    scr_w = {n: nc.dram_tensor(f"wscr_{n}", [sh[0], sh[1]], BF16, kind="Internal").ap()
             for n, sh in (("sw1", (512, D)), ("sw2", (512, D)), ("sw3", (D, 512)))}

    p = Prog(nc)
    sb = lambda name, shape, dt: stack.enter_context(nc.sbuf_tensor(name, shape, dt))
    ps = lambda name, shape, dt: stack.enter_context(nc.psum_tensor(name, shape, dt))

    bufs = {}
    bufs["X"] = sb("X", [128, NTILE, D], BF16)
    bufs["sw1T"] = sb("sw1T", [128, 8, 512], BF16)
    bufs["sw2T"] = sb("sw2T", [128, 8, 512], BF16)
    bufs["sw3T"] = sb("sw3T", [128, 4, D], BF16)
    bufs["b4_yfm"] = sb("b4_yfm", [128, 8, 512], BF16)
    bufs["shfm"] = sb("shfm", [128, 2, 4, 128], BF16)
    bufs["sc1"] = sb("sc1", [128, 2, 512], F32)
    bufs["sc2"] = sb("sc2", [128, 2, 512], F32)
    bufs["gbuf"] = sb("gbuf", [128, 2, 512], F32)
    bufs["shbf"] = sb("shbf", [128, 2, 512], BF16)
    bufs["ftile"] = sb("ftile", [128, 2, D], F32)
    bufs["y4t"] = sb("y4t", [128, 2, D], F32)
    bufs["scr"] = sb("scr", [128, 2, D], BF16)
    bufs["ytm"] = [sb(f"ytm{i}", [128, D], BF16) for i in range(2)]
    bufs["ident"] = sb("ident", [128, 128], BF16)
    bufs["sb1bc"] = sb("sb1bc", [128, 512], F32)
    bufs["sb2bc"] = sb("sb2bc", [128, 512], F32)
    bufs["sb3bc"] = sb("sb3bc", [128, D], F32)
    bufs["slgbc"] = sb("slgbc", [128, 512], F32)
    bufs["slbbc"] = sb("slbbc", [128, 512], F32)
    bufs["pj"] = [ps(f"pj{i}", [128, 512], F32) for i in range(4)]
    bufs["psumT"] = [ps(f"psT{i}", [128, 128], BF16) for i in range(2)]
    stats = {}
    for nmm in ("sum_a", "ssq_a", "rstd_a", "bias_a", "sum_b", "ssq_b", "rstd_b", "bias_b", "t0", "t1"):
        stats[nmm] = sb("st_" + nmm, [128, NTILE, 2] if nmm.startswith("sum") else [128, NTILE], F32)
    p.bnscr = sb("bnscr", [128, 12], F32)
    bufs["stats"] = stats

    p.op("pool", lambda e: e.memset(bufs["ident"][:], 1.0), writes=(("ident",),))
    p.op("pool", lambda e: e.affine_select(bufs["ident"][:], bufs["ident"][:],
                                           [[-1, 128]], ALU.is_equal, 0.0,
                                           base=0, channel_multiplier=1),
         reads=(("ident",),), writes=(("ident",),))
    for bc, dd in ((bufs["sb1bc"], sb1_d), (bufs["sb2bc"], sb2_d), (bufs["sb3bc"], sb3_d),
                   (bufs["slgbc"], slg_d), (bufs["slbbc"], slb_d)):
        p.op("pool", lambda e, bc=bc, dd=dd: e.dma_start(
            out=bc[:, :], in_=dd[None, :].broadcast_to([128, dd.shape[0]])),
            writes=(("b4c",),), inc=16)
    wtmps = [bufs["ytm"][0], bufs["ytm"][1]]
    wtkeys = [("ytm", 0), ("ytm", 1)]
    emit_weight_load_T(p, sw1_d, scr_w["sw1"], bufs["sw1T"], "sw1", 512, D, wtmps, wtkeys)
    emit_weight_load_T(p, sw2_d, scr_w["sw2"], bufs["sw2T"], "sw2", 512, D, wtmps, wtkeys)
    emit_weight_load_T(p, sw3_d, scr_w["sw3"], bufs["sw3T"], "sw3", D, 512, wtmps, wtkeys)
    for t in range(NTILE):
        p.op("pool", lambda e, t=t: e.dma_start(out=bufs["X"][:, t, :], in_=x_d[t, :, :]),
             writes=(("X", t),), inc=16)

    emit_block4_shared(p, nc, bufs, y4_out, f_out)

    p.build(stack)
    return nc, stack





def emit_block4b(p, nc, B, dr, fused=False, hook=None):
    ibs = B['ibs']
    its = B['its']
    flat = B['flat']
    xm = B['xm']
    xmfm = B['xmfm']
    xmbf = B['xmbf']
    gsb = B['gsb']
    GT = B['GT']
    bias9 = B['bias9']
    expbc = B['expbc']
    sig = B['sig']
    tmp8 = B['tmp8']
    mx1 = B['mx1']
    oh = B['oh']
    ohT = B['ohT']
    t1sc = B['t1sc']
    wsh = B['wsh']
    wex = B['wex']
    identf = B['identf']
    identb = B['identb']
    ew1T = B['ew1T']
    ew2T = B['ew2T']
    ew3T = B['ew3T']
    eb1bc = B['eb1bc']
    eb2bc = B['eb2bc']
    eb3sb = B['eb3sb']
    selgbc = B['selgbc']
    selbbc = B['selbbc']
    sc1 = B['sc1']
    sc2 = B['sc2']
    gbuf = B['gbuf']
    zm = B['zm']
    zmfm = B['zmfm']
    epart = B['epart']
    scrd = B['scrd']
    sum_b = B['sum_b']
    ssq_b = B['ssq_b']
    rstd_b = B['rstd_b']
    bias_b = B['bias_b']
    t0s = B['t0s']
    t1s = B['t1s']
    pT = B['pT']
    ph = B['ph']
    pout = B['pout']
    wtmpb = B['wtmpb']
    y4_d = dr.get('y4_d')
    ib_d = dr.get('ib_d')
    it_d = dr.get('it_d')
    sgw_d = dr.get('sgw_d')
    sgb_d = dr.get('sgb_d')
    egw_d = dr.get('egw_d')
    egb_d = dr.get('egb_d')
    expb_d = dr.get('expb_d')
    ew1_d = dr.get('ew1_d')
    eb1_d = dr.get('eb1_d')
    ew2_d = dr.get('ew2_d')
    eb2_d = dr.get('eb2_d')
    ew3_d = dr.get('ew3_d')
    eb3_d = dr.get('eb3_d')
    selg_d = dr.get('selg_d')
    selb_d = dr.get('selb_d')
    ep_out = dr.get('ep_out')
    wsh_out = dr.get('wsh_out')
    ew1scr = dr.get('ew1scr')
    ew2scr = dr.get('ew2scr')
    ew3scr = dr.get('ew3scr')
    MSL = 256
    p.op("pool", lambda e: e.memset(identf[:], 1.0), writes=(("identf",),))
    p.op("pool", lambda e: e.affine_select(identf[:], identf[:], [[-1, 128]],
                                           ALU.is_equal, 0.0, base=0, channel_multiplier=1),
         reads=(("identf",),), writes=(("identf",),))
    p.op("pool", lambda e: e.memset(identb[:], 1.0), writes=(("identb",),))
    p.op("pool", lambda e: e.affine_select(identb[:], identb[:], [[-1, 128]],
                                           ALU.is_equal, 0.0, base=0, channel_multiplier=1),
         reads=(("identb",),), writes=(("identb",),))
    p.op("pool", lambda e: e.dma_start(out=bias9[:, 0:1], in_=sgb_d[None, :].broadcast_to([128, 1])),
         writes=(("b4c",),), inc=16)
    p.op("pool", lambda e: e.dma_start(out=bias9[:, 1:9], in_=egb_d[None, :].broadcast_to([128, 8])),
         writes=(("b4c",),), inc=16)
    p.op("pool", lambda e: e.dma_start(out=expbc[:, :], in_=expb_d[None, :].broadcast_to([128, 8])),
         writes=(("b4c",),), inc=16)
    p.op("pool", lambda e: e.dma_start(out=eb1bc[:, :, :], in_=eb1_d[None, :, :].broadcast_to([128, 8, 512])),
         writes=(("b4c",),), inc=16)
    p.op("pool", lambda e: e.dma_start(out=eb2bc[:, :, :], in_=eb2_d[None, :, :].broadcast_to([128, 8, 512])),
         writes=(("b4c",),), inc=16)
    p.op("pool", lambda e: e.dma_start(out=eb3sb[:, :], in_=eb3_d[:, :]),
         writes=(("b4c",),), inc=16)
    p.op("pool", lambda e: e.dma_start(out=selgbc[:, :], in_=selg_d[None, :].broadcast_to([128, 512])),
         writes=(("b4c",),), inc=16)
    p.op("pool", lambda e: e.dma_start(out=selbbc[:, :], in_=selb_d[None, :].broadcast_to([128, 512])),
         writes=(("b4c",),), inc=16)
    p.op("pool", lambda e: e.memset(gsb[:], 0.0), writes=(("gsb",),))
    p.op("sp", lambda e: e.dma_start(out=gsb[0:1, :], in_=sgw_d[:, :]),
         reads=(("gsb",),), writes=(("gsb",),), inc=16)
    p.op("sp", lambda e: e.dma_start(out=gsb[1:9, :], in_=egw_d[:, :]),
         reads=(("gsb",),), writes=(("gsb",),), inc=16)
    for j in range(8):
        pt = pT[0]
        p.op("pe", lambda e, j=j, pt=pt: e.transpose(pt[:], gsb[:, j * 128:(j + 1) * 128], identf[:]),
             reads=(("gsb",), ("identf",)), writes=(("pT", 0),))
        p.op("dve", lambda e, j=j, pt=pt: e.tensor_copy(GT[:, j, :], pt[:, 0:9]),
             reads=(("pT", 0),), writes=(("GT",),))
    if fused:
        for mt in range(2):
            p.op("sp", lambda e, mt=mt: e.dma_start(out=xm[:, mt, :], in_=dr["ccrs"][mt * 128:(mt + 1) * 128, :]),
                 reads=(("ccrs",),), writes=(("xm", mt),), inc=16)
    else:
        p.op("sp", lambda e: e.dma_start(out=ibs[:, 0], in_=ib_d[0:128]), writes=(("idx",),), inc=16)
        p.op("sp", lambda e: e.dma_start(out=ibs[:, 1], in_=ib_d[128:256]), writes=(("idx",),), inc=16)
        p.op("sp", lambda e: e.dma_start(out=its[:, 0], in_=it_d[0:128]), writes=(("idx",),), inc=16)
        p.op("sp", lambda e: e.dma_start(out=its[:, 1], in_=it_d[128:256]), writes=(("idx",),), inc=16)
        p.op("dve", lambda e: e.tensor_scalar(flat[:], ibs[:], T, None, ALU.mult),
             reads=(("idx",),), writes=(("flat",),))
        p.op("dve", lambda e: e.tensor_tensor(flat[:], flat[:], its[:], ALU.add),
             reads=(("flat",), ("idx",)), writes=(("flat",),))
        for mt in range(2):
            p.op("pool",
                 lambda e, mt=mt: e.indirect_dma_start(
                     out=xm[:, mt, :], out_offset=None,
                     in_=y4_d[:, :], in_offset=IndirectOffsetOnAxis(flat[:, mt:mt + 1], 0)),
                 reads=(("flat",),), writes=(("xm", mt),), inc=16)
    for mt in range(2):
        for j in range(8):
            pt = pT[0]
            p.op("pe", lambda e, mt=mt, j=j, pt=pt: e.transpose(
                pt[:], xm[:, mt, j * 128:(j + 1) * 128], identf[:]),
                reads=(("xm", mt), ("identf",)), writes=(("pT", 0),))
            p.op("dve", lambda e, mt=mt, j=j, pt=pt: e.tensor_copy(
                xmfm[:, j, mt * 128:(mt + 1) * 128], pt[:]),
                reads=(("pT", 0),), writes=(("xmfm", j),))
    p.op("dve", lambda e: e.tensor_copy(xmbf[:], xmfm[:]),
         reads=tuple(("xmfm", j) for j in range(8)), writes=(("xmbf",),))
    for mt in range(2):
        pg = ph[mt]
        for j in range(8):
            p.op("pe", lambda e, mt=mt, j=j, pg=pg: e.matmul(
                pg[:, 0:9], xmfm[:, j, mt * 128:(mt + 1) * 128], GT[:, j, :],
                start=(j == 0), stop=(j == 7)),
                reads=(("xmfm", j), ("GT",)), writes=(("ph", mt),))
        p.op("dve", lambda e, mt=mt, pg=pg: e.tensor_tensor(sig[:, mt, :], pg[:, 0:9], bias9[:], ALU.add),
             reads=(("ph", mt), ("b4c",)), writes=(("sig", mt),))
        p.op("act", lambda e, mt=mt: e.activation(out=sig[:, mt, :], in_=sig[:, mt, :], func=AF.Sigmoid),
             reads=(("sig", mt),), writes=(("sig", mt),))
        p.op("dve", lambda e, mt=mt: e.tensor_tensor(tmp8[:], sig[:, mt, 1:9], expbc[:], ALU.add),
             reads=(("sig", mt), ("b4c",)), writes=(("tmp8",),))
        p.op("dve", lambda e: e.tensor_reduce(mx1[:], tmp8[:], AX.X, ALU.max),
             reads=(("tmp8",),), writes=(("mx1",),))
        p.op("dve", lambda e, mt=mt: e.tensor_scalar(oh[:, mt, :], tmp8[:], mx1[:], None, ALU.is_equal),
             reads=(("tmp8",), ("mx1",)), writes=(("oh", mt),))
        p.op("dve", lambda e, mt=mt: e.tensor_tensor(tmp8[:], oh[:, mt, :], sig[:, mt, 1:9], ALU.mult),
             reads=(("oh", mt), ("sig", mt)), writes=(("tmp8",),))
        p.op("dve", lambda e: e.tensor_reduce(t1sc[:], tmp8[:], AX.X, ALU.add),
             reads=(("tmp8",),), writes=(("t1sc",),))
        p.op("dve", lambda e, mt=mt: e.tensor_tensor(wsh[:, mt:mt + 1], sig[:, mt, 0:1], t1sc[:], ALU.subtract),
             reads=(("sig", mt), ("t1sc",)), writes=(("wsh",),))
        p.op("act", lambda e, mt=mt: e.activation(out=wsh[:, mt:mt + 1], in_=wsh[:, mt:mt + 1], func=AF.Sigmoid),
             reads=(("wsh",),), writes=(("wsh",),))
        p.op("dve", lambda e, mt=mt: e.tensor_scalar(wex[:, mt:mt + 1], wsh[:, mt:mt + 1], -1.0, 1.0, ALU.mult, ALU.add),
             reads=(("wsh",),), writes=(("wex",),))
        pt = pT[0]
        p.op("pe", lambda e, mt=mt, pt=pt: e.transpose(pt[0:8, 0:128], oh[:, mt, :], identf[:]),
             reads=(("oh", mt), ("identf",)), writes=(("pT", 0),))
        p.op("dve", lambda e, mt=mt, pt=pt: e.tensor_copy(ohT[:, mt, :], pt[0:8, 0:128]),
             reads=(("pT", 0),), writes=(("ohT",),))
    if not fused:
        for src_d, scr_ in ((ew1_d, ew1scr), (ew2_d, ew2scr), (ew3_d, ew3scr)):
            for ei in range(8):
                n = scr_.shape[0] // 8
                p.op("pool", lambda e, ei=ei, src_d=src_d, scr_=scr_, n=n: e.dma_start(
                    out=scr_[ei * n:(ei + 1) * n, :], in_=src_d[ei * n:(ei + 1) * n, :]),
                    reads=(), writes=(("wscrb", id(scr_), ei),), inc=16)

    for ei in range(8):
        if hook is not None:
            hook(ei)
        par = ei % 2
        w1, w2, w3 = ew1T[par], ew2T[par], ew3T[par]
        for j in range(8):
            if not (fused and ei == 0):
                p.op("sp", lambda e, ei=ei, j=j, w1=w1: e.dma_start_transpose(
                    w1[:, j, :], ew1scr[ei * 512:(ei + 1) * 512, j * 128:(j + 1) * 128]),
                    reads=(("wscrb", id(ew1scr), ei), ("ew1T", par)), writes=(("ew1T", par),), inc=16)
            p.op("sp", lambda e, ei=ei, j=j, w2=w2: e.dma_start_transpose(
                w2[:, j, :], ew2scr[ei * 512:(ei + 1) * 512, j * 128:(j + 1) * 128]),
                reads=(("wscrb", id(ew2scr), ei), ("ew2T", par)), writes=(("ew2T", par),), inc=16)
        for c in range(4):
            p.op("sp", lambda e, ei=ei, c=c, w3=w3: e.dma_start_transpose(
                w3[:, c, :], ew3scr[ei * D:(ei + 1) * D, c * 128:(c + 1) * 128]),
                reads=(("wscrb", id(ew3scr), ei), ("ew3T", par)), writes=(("ew3T", par),), inc=16)
        for mt in range(2):
            sl = (ei * 2 + mt) % 2
            s1 = sc1[:, sl, :]
            s2 = sc2[:, sl, :]
            gb = gbuf[:, sl, :]
            zmv = zm[:, sl, :]
            for w_, wkey, bank in ((w1, ("ew1T", par), 0), (w2, ("ew2T", par), 1)):
                pk = ph[bank]
                for j in range(8):
                    p.op("pe", lambda e, j=j, mt=mt, pk=pk, w_=w_: e.matmul(
                        pk[:], xmbf[:, j, mt * 128:(mt + 1) * 128], w_[:, j, :],
                        start=(j == 0), stop=(j == 7)),
                        reads=(("xmbf",), wkey), writes=(("ph", bank),))
            p.op("dve", lambda e, ei=ei, s1=s1: e.tensor_tensor(s1, ph[0][:], eb1bc[:, ei, :], ALU.add),
                 reads=(("ph", 0), ("b4c",)), writes=(("sc1", sl),))
            p.op("act", lambda e, s1=s1: e.activation(out=s1, in_=s1, func=AF.Silu),
                 reads=(("sc1", sl),), writes=(("sc1", sl),))
            p.op("dve", lambda e, ei=ei, s2=s2: e.tensor_tensor(s2, ph[1][:], eb2bc[:, ei, :], ALU.add),
                 reads=(("ph", 1), ("b4c",)), writes=(("sc2", sl),))
            p.op("dve", lambda e, s1=s1, s2=s2, gb=gb: e.tensor_tensor(gb, s1, s2, ALU.mult),
                 reads=(("sc1", sl), ("sc2", sl)), writes=(("gbuf", sl),))
            stcol = ei * 2 + mt
            emit_ln_stats(p, gb, stcol, sum_b, None, stcol, None, (("gbuf", sl),))
            emit_ln_finalize2(p, sum_b, ssq_b, rstd_b, bias_b, t0s, t1s, (stcol, stcol + 1), 512)
            emit_ln_norm(p, gb, (("gbuf", sl),), s1, (("sc1", sl),), rstd_b, bias_b, stcol)
            p.op("dve", lambda e, s1=s1: e.tensor_tensor(s1, s1, selgbc[:], ALU.mult),
                 reads=(("sc1", sl), ("b4c",)), writes=(("sc1", sl),))
            p.op("dve", lambda e, s1=s1, s2=s2: e.tensor_tensor(s2, s1, selbbc[:], ALU.add),
                 reads=(("sc1", sl), ("b4c",)), writes=(("sc2", sl),))
            p.op("dve", lambda e, ei=ei, mt=mt, s2=s2, zmv=zmv: e.tensor_scalar(
                zmv, s2, oh[:, mt, ei:ei + 1], None, ALU.mult),
                reads=(("sc2", sl), ("oh", mt)), writes=(("zm", sl),))
            for c in range(4):
                pt = pT[1]
                p.op("pe", lambda e, c=c, pt=pt, zmv=zmv: e.transpose(pt[:], zmv[:, c * 128:(c + 1) * 128], identb[:]),
                     reads=(("zm", sl), ("identb",)), writes=(("pT", 1),))
                p.op("dve", lambda e, c=c, pt=pt, sl=sl: e.tensor_copy(zmfm[:, sl, c, :], pt[:]),
                     reads=(("pT", 1),), writes=(("zmfm", sl, c),))
            for half in range(2):
                bank = mt * 2 + half
                po = pout[bank]
                if ei == 0:
                    p.op("pe", lambda e, mt=mt, half=half, po=po: e.matmul(
                        po[:], ohT[:, mt, :], eb3sb[:, half * 512:(half + 1) * 512],
                        start=True, stop=False, skip_group_check=True),
                        reads=(("ohT",), ("b4c",)), writes=(("pout", bank),))
                for c in range(4):
                    p.op("pe", lambda e, c=c, ei=ei, half=half, po=po, sl=sl: e.matmul(
                        po[:], zmfm[:, sl, c, :], ew3T[ei % 2][:, c, half * 512:(half + 1) * 512],
                        start=False, stop=(ei == 7 and c == 3), skip_group_check=True),
                        reads=(("zmfm", sl, c), ("ew3T", ei % 2)), writes=(("pout", bank),))
    for mt in range(2):
        for half in range(2):
            bank = mt * 2 + half
            p.op("dve", lambda e, mt=mt, half=half, bank=bank: e.tensor_scalar(
                epart[:, half * 512:(half + 1) * 512], pout[bank][:], wex[:, mt:mt + 1], None, ALU.mult),
                reads=(("pout", bank), ("wex",)), writes=(("epart", half),))
        p.op("pool", lambda e, mt=mt: e.dma_start(out=dr["ep_out"][mt, :, :], in_=epart[:]),
             reads=(("epart", 0), ("epart", 1)), writes=(("epart", 0), ("epart", 1)), inc=16)
    for mt in range(2):
        p.op("pool", lambda e, mt=mt: e.dma_start(out=dr["wsh_out"][mt, :], in_=wsh[:, mt]),
             reads=(("wsh",),), writes=(), inc=16)


def build_b4b_program():
    nc = bass.Bass("TRN2", target_bir_lowering=False)
    stack = contextlib.ExitStack()
    MSL = 256
    y4_d = nc.dram_tensor("y4full", [16 * T, D], F32, kind="ExternalInput").ap()
    ib_d = nc.dram_tensor("ib", [MSL], I32, kind="ExternalInput").ap()
    it_d = nc.dram_tensor("it", [MSL], I32, kind="ExternalInput").ap()
    sgw_d = nc.dram_tensor("sgw", [1, D], F32, kind="ExternalInput").ap()
    sgb_d = nc.dram_tensor("sgb", [1], F32, kind="ExternalInput").ap()
    egw_d = nc.dram_tensor("egw", [8, D], F32, kind="ExternalInput").ap()
    egb_d = nc.dram_tensor("egb", [8], F32, kind="ExternalInput").ap()
    expb_d = nc.dram_tensor("expb", [8], F32, kind="ExternalInput").ap()
    ew1_d = nc.dram_tensor("ew1", [8 * 512, D], F32, kind="ExternalInput").ap()
    eb1_d = nc.dram_tensor("eb1", [8, 512], F32, kind="ExternalInput").ap()
    ew2_d = nc.dram_tensor("ew2", [8 * 512, D], F32, kind="ExternalInput").ap()
    eb2_d = nc.dram_tensor("eb2", [8, 512], F32, kind="ExternalInput").ap()
    ew3_d = nc.dram_tensor("ew3", [8 * D, 512], F32, kind="ExternalInput").ap()
    eb3_d = nc.dram_tensor("eb3", [8, D], F32, kind="ExternalInput").ap()
    selg_d = nc.dram_tensor("selg", [512], F32, kind="ExternalInput").ap()
    selb_d = nc.dram_tensor("selb", [512], F32, kind="ExternalInput").ap()
    ep_out = nc.dram_tensor("epart", [2, 128, D], F32, kind="ExternalOutput").ap()
    wsh_out = nc.dram_tensor("wsh", [2, 128], F32, kind="ExternalOutput").ap()
    ew1scr = nc.dram_tensor("ew1scr", [8 * 512, D], BF16, kind="Internal").ap()
    ew2scr = nc.dram_tensor("ew2scr", [8 * 512, D], BF16, kind="Internal").ap()
    ew3scr = nc.dram_tensor("ew3scr", [8 * D, 512], BF16, kind="Internal").ap()

    p = Prog(nc)
    sb = lambda name, shape, dt: stack.enter_context(nc.sbuf_tensor(name, shape, dt))
    ps = lambda name, shape, dt: stack.enter_context(nc.psum_tensor(name, shape, dt))

    ibs = sb("ibs", [128, 2], I32)
    its = sb("its", [128, 2], I32)
    flat = sb("flat", [128, 2], I32)
    xm = sb("xm", [128, 2, D], F32)
    xmfm = sb("xmfm", [128, 8, MSL], F32)
    xmbf = sb("xmbf", [128, 8, MSL], BF16)
    gsb = sb("gsb", [128, D], F32)
    GT = sb("GT", [128, 8, 9], F32)
    bias9 = sb("bias9", [128, 9], F32)
    expbc = sb("expbc", [128, 8], F32)
    sig = sb("sig", [128, 2, 9], F32)
    tmp8 = sb("tmp8", [128, 8], F32)
    mx1 = sb("mx1", [128, 1], F32)
    oh = sb("oh", [128, 2, 8], F32)
    ohT = sb("ohT", [8, 2, 128], BF16)
    t1sc = sb("t1sc", [128, 1], F32)
    wsh = sb("wsh_sb", [128, 2], F32)
    wex = sb("wex", [128, 2], F32)
    identf = sb("identf", [128, 128], F32)
    identb = sb("identb", [128, 128], BF16)
    ew1T = [sb(f"ew1T{i}", [128, 8, 512], BF16) for i in range(2)]
    ew2T = [sb(f"ew2T{i}", [128, 8, 512], BF16) for i in range(2)]
    ew3T = [sb(f"ew3T{i}", [128, 4, D], BF16) for i in range(2)]
    eb1bc = sb("eb1bc", [128, 8, 512], BF16)
    eb2bc = sb("eb2bc", [128, 8, 512], BF16)
    eb3sb = sb("eb3sb", [8, D], BF16)
    selgbc = sb("selgbc", [128, 512], F32)
    selbbc = sb("selbbc", [128, 512], F32)
    sc1 = sb("sc1", [128, 2, 512], F32)
    sc2 = sb("sc2", [128, 2, 512], F32)
    gbuf = sb("gbuf", [128, 2, 512], F32)
    zm = sb("zm", [128, 2, 512], BF16)
    zmfm = sb("zmfm", [128, 2, 4, 128], BF16)
    epart = sb("epart_sb", [128, D], F32)
    scrd = sb("scrd", [128, 2, 512], BF16)
    stats = {}
    for nmm in ("sum_b", "ssq_b", "rstd_b", "bias_b", "t0", "t1"):
        stats[nmm] = sb("st_" + nmm, [128, 16, 2] if nmm.startswith("sum") else [128, 16], F32)
    p.bnscr = sb("bnscr", [128, 12], F32)
    sum_b, ssq_b = stats["sum_b"], stats["ssq_b"]
    rstd_b, bias_b = stats["rstd_b"], stats["bias_b"]
    t0s, t1s = stats["t0"], stats["t1"]

    pT = [ps("pT0", [128, 128], F32), ps("pT1", [128, 128], BF16)]
    ph = [ps(f"ph{i}", [128, 512], F32) for i in range(2)]
    pout = [ps(f"pout{i}", [128, 512], F32) for i in range(4)]
    wtmpb = sb("wtmpb", [128, D], BF16)

    B = {}
    for _n in ('ibs its flat xm xmfm xmbf gsb GT bias9 expbc sig tmp8 mx1 oh ohT t1sc wsh wex identf identb ew1T ew2T ew3T eb1bc eb2bc eb3sb selgbc selbbc sc1 sc2 gbuf zm zmfm epart scrd sum_b ssq_b rstd_b bias_b t0s t1s pT ph pout wtmpb').split():
        B[_n] = locals()[_n]
    dr = {'y4_d': y4_d, 'ib_d': ib_d, 'it_d': it_d, 'sgw_d': sgw_d, 'sgb_d': sgb_d,
          'egw_d': egw_d, 'egb_d': egb_d, 'expb_d': expb_d, 'ew1_d': ew1_d, 'eb1_d': eb1_d,
          'ew2_d': ew2_d, 'eb2_d': eb2_d, 'ew3_d': ew3_d, 'eb3_d': eb3_d, 'selg_d': selg_d,
          'selb_d': selb_d, 'ep_out': ep_out, 'wsh_out': wsh_out, 'ew1scr': ew1scr,
          'ew2scr': ew2scr, 'ew3scr': ew3scr}
    emit_block4b(p, nc, B, dr, fused=False)

    p.build(stack)
    return nc, stack


def emit_block4c(p, nc, B, dr, fused=False):
    X = B['X']
    ibs = B['ibs']
    its = B['its']
    flat = B['flat']
    flatf = B['flatf']
    own = B['own']
    tmpi = B['tmpi']
    basebc = B['basebc']
    wshs = B['wshs']
    fg = B['fg']
    ep = B['ep']
    comb = B['comb']
    ftile = B['ftile']
    scr = B['scr']
    lnf = B['lnf']
    sum_b = B['sum_b']
    ssq_b = B['ssq_b']
    rstd_b = B['rstd_b']
    bias_b = B['bias_b']
    t0s = B['t0s']
    t1s = B['t1s']
    x_d = dr.get('x_d')
    f_d = dr.get('f_d')
    ib_d = dr.get('ib_d')
    it_d = dr.get('it_d')
    ep_d = dr.get('ep_d')
    wsh_d = dr.get('wsh_d')
    base_d = dr.get('base_d')
    out_d = dr.get('out_d')
    ffix = dr.get('ffix')
    if not fused:
        for t in range(NTILE):
            p.op("pool", lambda e, t=t: e.dma_start(out=X[:, t, :], in_=x_d[t, :, :]),
                 writes=(("X", t),), inc=16)
    if not dr.get("skip_ffix_copy"):
        for t in range(NTILE):
            p.op("sp", lambda e, t=t: e.dma_start(out=ffix[t * 128:(t + 1) * 128, :],
                                                  in_=f_d[t * 128:(t + 1) * 128, :]),
                 writes=(("ffix", t),), inc=16)
    for k in range(16):
        p.op("sp", lambda e, k=k: e.dma_start(out=ibs[:, k], in_=ib_d[k * 128:(k + 1) * 128]),
             writes=(("idx",),), inc=16)
        p.op("sp", lambda e, k=k: e.dma_start(out=its[:, k], in_=it_d[k * 128:(k + 1) * 128]),
             writes=(("idx",),), inc=16)
        p.op("sp", lambda e, k=k: e.dma_start(out=wshs[:, k], in_=wsh_d[k * 128:(k + 1) * 128]),
             writes=(("wshs",),), inc=16)
    p.op("sp", lambda e: e.dma_start(out=basebc[:, :], in_=base_d[None, :].broadcast_to([128, 1])),
         writes=(("base",),), inc=16)
    p.op("dve", lambda e: e.tensor_scalar(flat[:], ibs[:], T, None, ALU.mult),
         reads=(("idx",),), writes=(("flat",),))
    p.op("dve", lambda e: e.tensor_tensor(flat[:], flat[:], its[:], ALU.add),
         reads=(("flat",), ("idx",)), writes=(("flat",),))
    p.op("dve", lambda e: e.tensor_copy(flatf[:], flat[:]),
         reads=(("flat",),), writes=(("flatf",),))
    p.op("dve", lambda e: e.tensor_scalar(flatf[:], flatf[:], basebc[:, 0:1], None, ALU.subtract),
         reads=(("flatf",), ("base",)), writes=(("flatf",),))
    p.op("dve", lambda e: e.tensor_scalar(own[:], flatf[:], 0.0, None, ALU.is_ge),
         reads=(("flatf",),), writes=(("own",),))
    p.op("dve", lambda e: e.tensor_scalar(tmpi[:], flatf[:], float(ROWS), None, ALU.is_lt),
         reads=(("flatf",),), writes=(("tmpi",),))
    p.op("dve", lambda e: e.tensor_tensor(own[:], own[:], tmpi[:], ALU.mult),
         reads=(("own",), ("tmpi",)), writes=(("own",),))
    p.op("dve", lambda e: e.tensor_tensor(flatf[:], flatf[:], own[:], ALU.mult),
         reads=(("flatf",), ("own",)), writes=(("flatf",),))
    p.op("dve", lambda e: e.tensor_scalar(tmpi[:], own[:], -float(ROWS), float(ROWS), ALU.mult, ALU.add),
         reads=(("own",),), writes=(("tmpi",),))
    p.op("dve", lambda e: e.tensor_tensor(flatf[:], flatf[:], tmpi[:], ALU.add),
         reads=(("flatf",), ("tmpi",)), writes=(("flatf",),))
    p.op("dve", lambda e: e.tensor_copy(flat[:], flatf[:]),
         reads=(("flatf",),), writes=(("flat",),))
    fkeys = tuple(("ffix", t) for t in range(NTILE))
    for k in range(16):
        sl = k % 2
        fgv = fg[:, sl, :]
        epv = ep[:, sl, :]
        cbv = comb[:, sl, :]
        p.op("pool",
             lambda e, k=k, fgv=fgv: e.indirect_dma_start(
                 out=fgv, out_offset=None,
                 in_=f_d[:, :], in_offset=IndirectOffsetOnAxis(flat[:, k:k + 1], 0)),
             reads=(("flat",), ("fg", sl)), writes=(("fg", sl),), inc=16)
        p.op("sp", lambda e, k=k, epv=epv: e.dma_start(out=epv, in_=ep_d[k * 128:(k + 1) * 128, :]),
             reads=(("ep", sl),), writes=(("ep", sl),), inc=16)
        p.op("dve",
             lambda e, k=k, fgv=fgv, epv=epv, cbv=cbv: e.scalar_tensor_tensor(
                 cbv, fgv, wshs[:, k:k + 1], epv, ALU.mult, ALU.add),
             reads=(("fg", sl), ("ep", sl), ("wshs",), ("comb", sl)), writes=(("comb", sl),))
        p.op("pool",
             lambda e, k=k, cbv=cbv: e.indirect_dma_start(
                 out=ffix[:, :], out_offset=IndirectOffsetOnAxis(flat[:, k:k + 1], 0),
                 in_=cbv, in_offset=None),
             reads=(("flat",), ("comb", sl)) + fkeys, writes=(("ffixs", k),), inc=16)
    skeys = tuple(("ffixs", k) for k in range(16))
    for pair in range(NTILE // 2):
        for pp in range(2):
            tile = pair * 2 + pp
            sl = tile % 2
            ftv = ftile[:, sl, :]
            p.op("sp", lambda e, tile=tile, ftv=ftv: e.dma_start(
                out=ftv, in_=ffix[tile * 128:(tile + 1) * 128, :]),
                 reads=skeys + (("ftile", sl),), writes=(("ftile", sl),), inc=16)
            emit_ln_stats(p, ftv, tile, sum_b, None, tile, None, (("ftile", sl),))
        emit_ln_finalize2(p, sum_b, ssq_b, rstd_b, bias_b, t0s, t1s,
                          (pair * 2, pair * 2 + 2), D)
        for pp in range(2):
            tile = pair * 2 + pp
            sl = tile % 2
            ftv = ftile[:, sl, :]
            emit_ln_norm(p, ftv, (("ftile", sl),), lnf[:, sl, :], (("lnf", sl),),
                         rstd_b, bias_b, tile)
            p.op("pool",
                 lambda e, tile=tile, sl=sl: e.tensor_tensor(
                     X[:, tile, :], X[:, tile, :], lnf[:, sl, :], ALU.add),
                 reads=(("lnf", sl), ("X", tile)), writes=(("X", tile),))
            p.op("pool", lambda e, tile=tile: e.dma_start(out=out_d[tile, :, :], in_=X[:, tile, :]),
                 reads=(("X", tile),), writes=(), inc=16)


def build_b4c_program():
    nc = bass.Bass("TRN2", target_bir_lowering=False)
    stack = contextlib.ExitStack()
    M = 2048
    x_d = nc.dram_tensor("x", [NTILE, 128, D], F32, kind="ExternalInput").ap()
    f_d = nc.dram_tensor("fpad", [ROWS + 128, D], F32, kind="ExternalInput").ap()
    ib_d = nc.dram_tensor("ib", [M], I32, kind="ExternalInput").ap()
    it_d = nc.dram_tensor("it", [M], I32, kind="ExternalInput").ap()
    ep_d = nc.dram_tensor("epart", [M, D], F32, kind="ExternalInput").ap()
    wsh_d = nc.dram_tensor("wshall", [M], F32, kind="ExternalInput").ap()
    base_d = nc.dram_tensor("base", [1], F32, kind="ExternalInput").ap()
    out_d = nc.dram_tensor("out", [NTILE, 128, D], F32, kind="ExternalOutput").ap()
    ffix = nc.dram_tensor("ffix", [ROWS + 128, D], F32, kind="Internal").ap()

    p = Prog(nc)
    sb = lambda name, shape, dt: stack.enter_context(nc.sbuf_tensor(name, shape, dt))
    ps = lambda name, shape, dt: stack.enter_context(nc.psum_tensor(name, shape, dt))

    X = sb("X", [128, NTILE, D], BF16)
    ibs = sb("ibs", [128, 16], I32)
    its = sb("its", [128, 16], I32)
    flat = sb("flat", [128, 16], I32)
    flatf = sb("flatf", [128, 16], F32)
    own = sb("own", [128, 16], F32)
    tmpi = sb("tmpi", [128, 16], F32)
    basebc = sb("basebc", [128, 1], F32)
    wshs = sb("wshs", [128, 16], F32)
    fg = sb("fg", [128, 2, D], F32)
    ep = sb("ep", [128, 2, D], F32)
    comb = sb("comb", [128, 2, D], F32)
    ftile = sb("ftile", [128, 2, D], F32)
    scr = sb("scr", [128, 2, D], BF16)
    lnf = sb("lnf", [128, 2, D], BF16)
    stats = {}
    for nmm in ("sum_b", "ssq_b", "rstd_b", "bias_b", "t0", "t1"):
        stats[nmm] = sb("st_" + nmm, [128, NTILE, 2] if nmm.startswith("sum") else [128, NTILE], F32)
    p.bnscr = sb("bnscr", [128, 12], F32)
    sum_b, ssq_b = stats["sum_b"], stats["ssq_b"]
    rstd_b, bias_b = stats["rstd_b"], stats["bias_b"]
    t0s, t1s = stats["t0"], stats["t1"]

    B = {}
    for _n in ('X ibs its flat flatf own tmpi basebc wshs fg ep comb ftile scr lnf sum_b ssq_b rstd_b bias_b t0s t1s').split():
        B[_n] = locals()[_n]
    dr = {'x_d': x_d, 'f_d': f_d, 'ib_d': ib_d, 'it_d': it_d, 'ep_d': ep_d,
          'wsh_d': wsh_d, 'base_d': base_d, 'out_d': out_d, 'ffix': ffix}
    emit_block4c(p, nc, B, dr, fused=False)

    p.build(stack)
    return nc, stack


def build_b2_program():
    nc = bass.Bass("TRN2", target_bir_lowering=False)
    stack = contextlib.ExitStack()
    x_d = nc.dram_tensor("x", [NTILE, 128, D], F32, kind="ExternalInput").ap()
    wr_d = nc.dram_tensor("wr", [D, D], F32, kind="ExternalInput").ap()
    wk_d = nc.dram_tensor("wk", [D, D], F32, kind="ExternalInput").ap()
    wv_d = nc.dram_tensor("wv", [D, D], F32, kind="ExternalInput").ap()
    mur_d = nc.dram_tensor("mur", [D], F32, kind="ExternalInput").ap()
    muk_d = nc.dram_tensor("muk", [D], F32, kind="ExternalInput").ap()
    out_d = nc.dram_tensor("out", [NTILE, 128, D], F32, kind="ExternalOutput").ap()
    scr_w = {n: nc.dram_tensor(f"wscr_{n}", [D, D], BF16, kind="Internal").ap()
             for n in ("wr", "wk", "wv")}

    p = Prog(nc)
    sb = lambda name, shape, dt: stack.enter_context(nc.sbuf_tensor(name, shape, dt))
    ps = lambda name, shape, dt: stack.enter_context(nc.psum_tensor(name, shape, dt))

    bufs = {}
    bufs["X"] = sb("X", [128, NTILE, D], BF16)
    bufs["wrT"] = sb("wrT", [128, 8, D], BF16)
    bufs["wkT"] = sb("wkT", [128, 8, D], BF16)
    bufs["wvT"] = sb("wvT", [128, 8, D], BF16)
    bufs["yfm"] = sb("yfm", [128, 8, 513], BF16)
    bufs["xr"] = sb("xr", [128, 8, 512], BF16)
    bufs["xk"] = sb("xk", [128, 8, 512], BF16)
    bufs["kact"] = sb("kact", [128, 8, 512], BF16)
    bufs["scr"] = sb("scr", [128, D], F32)
    bufs["lnf"] = sb("lnf", [128, 2, D], F32)
    bufs["ytm"] = [sb(f"ytm{i}", [128, D], BF16) for i in range(2)]
    bufs["krelu"] = [sb(f"krelu{i}", [128, 512], F32) for i in range(2)]
    bufs["rgate"] = sb("rgate", [128, 2, D], F32)
    bufs["fblk"] = sb("fblk", [128, 4, D], F32)
    bufs["dif"] = [sb(f"dif{i}", [128, 512], BF16) for i in range(2)]
    bufs["ident"] = sb("ident", [128, 128], BF16)
    bufs["mur"] = sb("mur_sb", [128, 8], F32)
    bufs["muk"] = sb("muk_sb", [128, 8], F32)
    bufs["psum"] = [ps(f"pb{i}", [128, 512], F32) for i in range(7)]
    bufs["psumT"] = ps("psT", [128, 128], BF16)
    stats = {}
    for nmm in ("sum_a", "ssq_a", "rstd_a", "bias_a", "sum_b", "ssq_b", "rstd_b", "bias_b", "t0", "t1"):
        stats[nmm] = sb("st_" + nmm, [128, NTILE, 2] if nmm.startswith("sum") else [128, NTILE], F32)
    p.bnscr = sb("bnscr", [128, 12], F32)
    bufs["stats"] = stats
    p.op("pool", lambda e: e.memset(bufs["ident"][:], 1.0), writes=(("ident",),))
    p.op("pool", lambda e: e.affine_select(bufs["ident"][:], bufs["ident"][:],
                                           [[-1, 128]], ALU.is_equal, 0.0,
                                           base=0, channel_multiplier=1),
         reads=(("ident",),), writes=(("ident",),))
    for j in range(8):
        p.op("sp", lambda e, j=j: e.dma_start(out=bufs["mur"][:, j],
                                              in_=mur_d[j * 128:(j + 1) * 128]),
             writes=(("mu",),), inc=16)
        p.op("sp", lambda e, j=j: e.dma_start(out=bufs["muk"][:, j],
                                              in_=muk_d[j * 128:(j + 1) * 128]),
             writes=(("mu",),), inc=16)
    wtmps = [bufs["ytm"][0], bufs["ytm"][1]]
    wtkeys = [("ytm", 0), ("ytm", 1)]
    emit_weight_load_T(p, wr_d, scr_w["wr"], bufs["wrT"], "wr", D, D, wtmps, wtkeys)
    emit_weight_load_T(p, wk_d, scr_w["wk"], bufs["wkT"], "wk", D, D, wtmps, wtkeys)
    emit_weight_load_T(p, wv_d, scr_w["wv"], bufs["wvT"], "wv", D, D, wtmps, wtkeys)
    for t in range(NTILE):
        p.op("pool", lambda e, t=t: e.dma_start(out=bufs["X"][:, t, :], in_=x_d[t, :, :]),
             writes=(("X", t),), inc=16)

    emit_block2(p, nc, bufs, {})

    for t in range(NTILE):
        p.op("pool", lambda e, t=t: e.dma_start(out=out_d[t, :, :], in_=bufs["X"][:, t, :]),
             reads=(("X", t),), writes=(), inc=16)

    p.build(stack)
    return nc, stack


def _shard_rows(a):
    B = a.shape[0]
    rest = a.shape[2:]
    a = np.ascontiguousarray(a.reshape(8, B_LOC * T, *rest))
    return [np.ascontiguousarray(a[i].reshape(NTILE, 128, *rest)) for i in range(8)]





def build_fused_program():
    nc = bass.Bass("TRN2", target_bir_lowering=False)
    outer = contextlib.ExitStack()
    di = lambda n, sh: nc.dram_tensor(n, sh, F32, kind="ExternalInput").ap()
    x_d = di("x", [NTILE, 128, D])
    wq1, wk1, wv1, wo1 = di("wq1", [D, D]), di("wk1", [D, D]), di("wv1", [D, D]), di("wo1", [D, D])
    bo1 = di("bo1", [D])
    wr2, wk2b, wv2b = di("wr2", [D, D]), di("wk2b", [D, D]), di("wv2b", [D, D])
    mur2, muk2 = di("mur2", [D]), di("muk2", [D])
    wq3, wk3, wv3 = di("wq3", [D, D]), di("wk3", [D, D]), di("wv3", [D, D])
    bq3, bk3, bv3, beta3 = di("bq3", [D]), di("bk3", [D]), di("bv3", [D]), di("beta3", [8])
    sw1_d, sw2_d = di("sw1", [512, D]), di("sw2", [512, D])
    sb1_d, sb2_d = di("sb1", [512]), di("sb2", [512])
    sw3_d, sb3_d = di("sw3", [D, 512]), di("sb3", [D])
    slg_d, slb_d = di("slg", [512]), di("slb", [512])
    sgw_d, sgb_d = di("sgw", [1, D]), di("sgb", [1])
    egw_d, egb_d, expb_d = di("egw", [8, D]), di("egb", [8]), di("expb", [8])
    ew1_d, eb1_d = di("ew1", [8 * 512, D]), di("eb1", [8, 512])
    ew2_d, eb2_d = di("ew2", [8 * 512, D]), di("eb2", [8, 512])
    ew3_d, eb3_d = di("ew3", [8 * D, 512]), di("eb3", [8, D])
    selg_d, selb_d = di("selg", [512]), di("selb", [512])
    ib_d = nc.dram_tensor("ib", [2048], I32, kind="ExternalInput").ap()
    it_d = nc.dram_tensor("it", [2048], I32, kind="ExternalInput").ap()
    base_d = di("basef", [1])
    out_d = nc.dram_tensor("out", [ROWS + 128, D], F32, kind="ExternalOutput").ap()
    out_view = out_d[0:ROWS, :].rearrange("(t p) c -> t p c", p=128)

    scrw = {}
    for n, sh in (("wq1", (D, D)), ("wk1", (D, D)), ("wv1", (D, D)), ("wo1", (D, D)),
                  ("wr2", (D, D)), ("wk2b", (D, D)), ("wv2b", (D, D)),
                  ("sw1", (512, D)), ("sw2", (512, D)), ("sw3", (D, 512))):
        scrw[n] = nc.dram_tensor("scr_" + n, list(sh), BF16, kind="Internal").ap()
    ew1scr = nc.dram_tensor("ew1scr", [8 * 512, D], BF16, kind="Internal").ap()
    ew2scr = nc.dram_tensor("ew2scr", [8 * 512, D], BF16, kind="Internal").ap()
    ew3scr = nc.dram_tensor("ew3scr", [8 * D, 512], BF16, kind="Internal").ap()
    y4loc = nc.dram_tensor("y4loc", [ROWS + 128, D], F32, kind="Internal").ap()
    y4bf = nc.dram_tensor("y4bf", [ROWS, D], BF16, kind="Internal").ap()
    x3loc = nc.dram_tensor("x3loc", [ROWS + 128, D], BF16, kind="Internal").ap()
    floc = nc.dram_tensor("floc", [ROWS + 128, D], F32, kind="Internal").ap()
    ffix = nc.dram_tensor("ffix", [ROWS + 128, D], F32, kind="Internal").ap()
    cc_in = nc.dram_tensor("cc_in", [2048, D], F32, kind="Internal").ap()
    cc_rs = nc.dram_tensor("cc_rs", [256, D], F32, kind="Internal").ap()
    agp_in = nc.dram_tensor("agp_in", [256, D], F32, kind="Internal").ap()
    agp_out = nc.dram_tensor("agp_out", [2048, D], F32, kind="Internal").ap()
    agw_in = nc.dram_tensor("agw_in", [256, 1], F32, kind="Internal").ap()
    agw_out = nc.dram_tensor("agw_out", [2048, 1], F32, kind="Internal").ap()
    RG = [list(range(8))]

    p = Prog(nc)
    osb = lambda name, shape, dt: outer.enter_context(nc.sbuf_tensor(name, shape, dt))

    X = osb("X", [128, NTILE, D], BF16)
    w0 = osb("w0", [128, 8, D], BF16)
    w1 = osb("w1", [128, 8, D], BF16)
    w2 = osb("w2", [128, 8, D], BF16)
    ident = osb("ident", [128, 128], BF16)
    identf = osb("identf", [128, 128], F32)
    stats = {}
    for nmm in ("sum_a", "ssq_a", "rstd_a", "bias_a", "sum_b", "ssq_b", "rstd_b",
                "bias_b", "t0", "t1"):
        stats[nmm] = osb("st_" + nmm, [128, NTILE, 2] if nmm.startswith("sum") else [128, NTILE], F32)
    p.bnscr = osb("bnscr", [128, 12], F32)
    flatperm = osb("flatperm", [128, 16], I32)

    p.alloc_sems(outer)
    block = outer.enter_context(nc.Block())

    def phase_done():
        p.barrier()
        p.flush(block)

    p.op("pool", lambda e: e.memset(ident[:], 1.0), writes=(("ident",),))
    p.op("pool", lambda e: e.affine_select(ident[:], ident[:], [[-1, 128]],
                                           ALU.is_equal, 0.0, base=0, channel_multiplier=1),
         reads=(("ident",),), writes=(("ident",),))
    p.op("pool", lambda e: e.memset(identf[:], 1.0), writes=(("identf",),))
    p.op("pool", lambda e: e.affine_select(identf[:], identf[:], [[-1, 128]],
                                           ALU.is_equal, 0.0, base=0, channel_multiplier=1),
         reads=(("identf",),), writes=(("identf",),))
    for n, wd in (("wq", wq1), ("wk", wk1), ("wv", wv1), ("wo", wo1)):
        emit_weight_cast(p, wd, scrw[n + "1"], n)
    for t in range(NTILE):
        p.op("pool", lambda e, t=t: e.dma_start(out=X[:, t, :], in_=x_d[t, :, :]),
             writes=(("X", t),), inc=16)

    with contextlib.ExitStack() as ph:
        sb = lambda name, shape, dt: ph.enter_context(nc.sbuf_tensor(name, shape, dt))
        ps = lambda name, shape, dt: ph.enter_context(nc.psum_tensor(name, shape, dt))
        bufs = {"X": X, "ident": ident, "stats": stats,
                "wqT": w0, "wkT": w1, "wvT": w2}
        bufs["woT"] = sb("woT", [128, 8, D], BF16)
        bufs["b1_yfm"] = sb("b1_yfm", [128, 8, 512], BF16)
        bufs["qfm"] = sb("qfm", [128, 8, 512], BF16)
        bufs["kev"] = sb("kev", [128, 8, 640], BF16)
        bufs["kod"] = sb("kod", [128, 8, 640], BF16)
        bufs["vpad"] = sb("vpad", [128, 5, 16, 65], BF16)
        bufs["expS"] = [sb(f"expS{i}", [128, 512], BF16) for i in range(2)]
        bufs["otm"] = sb("otm", [128, 2, D], BF16)
        bufs["ofm"] = sb("ofm", [128, 2, 8, 128], BF16)
        bufs["fblk"] = sb("fblk", [128, 2, D], BF16)
        bufs["scr"] = sb("scr", [128, D], BF16)
        bufs["lnf"] = sb("lnf", [128, 2, D], BF16)
        bufs["ytm"] = [sb(f"ytm{i}", [128, D], BF16) for i in range(2)]
        bufs["bo_bc"] = sb("bo_bc", [128, D], BF16)
        bufs["maskA"] = sb("maskA", [128, 512], BF16)
        bufs["rcol"] = sb("rcol", [128, 16], F32)
        bufs["psum"] = [ps(f"pb{i}", [128, 512], F32) for i in range(4)]
        bufs["psumT"] = [ps(f"psT{i}", [128, 128], BF16) for i in range(2)]
        bufs["psum_av"] = [ps(f"pav{i}", [128, 65], F32) for i in range(2)]
        emit_b1_setup(p, nc, bufs, bo1)
        wtmps = [bufs["ytm"][0], bufs["ytm"][1]]
        wtkeys = [("ytm", 0), ("ytm", 1)]
        for n, wd in (("wq", wq1), ("wk", wk1), ("wv", wv1), ("wo", wo1)):
            emit_weight_load_T(p, wd, scrw[n + "1"], bufs[n + "T"], n, D, D, cast=False)
        emit_weight_cast(p, wr2, scrw["wr2"], "wr")
        emit_weight_cast(p, wk2b, scrw["wk2b"], "wk2")
        emit_weight_cast(p, wv2b, scrw["wv2b"], "wv2")
        emit_weight_cast(p, sw1_d, scrw["sw1"], "sw1")
        emit_weight_cast(p, sw2_d, scrw["sw2"], "sw2")
        emit_weight_cast(p, sw3_d, scrw["sw3"], "sw3")
        emit_block1(p, nc, bufs)
        phase_done()

    with contextlib.ExitStack() as ph:
        sb = lambda name, shape, dt: ph.enter_context(nc.sbuf_tensor(name, shape, dt))
        ps = lambda name, shape, dt: ph.enter_context(nc.psum_tensor(name, shape, dt))
        bufs = {"X": X, "ident": ident, "stats": stats,
                "wrT": w0, "wkT": w1, "wvT": w2}
        bufs["yfm"] = sb("yfm2", [128, 8, 513], BF16)
        bufs["xr"] = sb("xr", [128, 8, 512], BF16)
        bufs["xk"] = sb("xk", [128, 8, 512], BF16)
        bufs["kact"] = sb("kact", [128, 8, 512], BF16)
        bufs["scr"] = sb("scr2", [128, D], F32)
        bufs["lnf"] = sb("lnf2", [128, 2, D], F32)
        bufs["ytm"] = [sb(f"ytm2{i}", [128, D], BF16) for i in range(2)]
        bufs["krelu"] = [sb(f"krelu{i}", [128, 512], F32) for i in range(2)]
        bufs["rgate"] = sb("rgate", [128, 2, D], F32)
        bufs["fblk"] = sb("fblk2", [128, 4, D], F32)
        bufs["dif"] = [sb(f"dif{i}", [128, 512], BF16) for i in range(2)]
        bufs["mur"] = sb("mur_sb", [128, 8], F32)
        bufs["muk"] = sb("muk_sb", [128, 8], F32)
        bufs["psum"] = [ps(f"pb2{i}", [128, 512], F32) for i in range(7)]
        bufs["psumT"] = ps("psT2", [128, 128], BF16)
        for j in range(8):
            p.op("sp", lambda e, j=j: e.dma_start(out=bufs["mur"][:, j],
                                                  in_=mur2[j * 128:(j + 1) * 128]),
                 writes=(("mu",),), inc=16)
            p.op("sp", lambda e, j=j: e.dma_start(out=bufs["muk"][:, j],
                                                  in_=muk2[j * 128:(j + 1) * 128]),
                 writes=(("mu",),), inc=16)
        wtmps = [bufs["ytm"][0], bufs["ytm"][1]]
        wtkeys = [("ytm", 0), ("ytm", 1)]
        for src_d, scr_ in ((ew1_d, ew1scr), (ew2_d, ew2scr), (ew3_d, ew3scr)):
            for ei in range(8):
                n_ = scr_.shape[0] // 8
                p.op("pool", lambda e, ei=ei, src_d=src_d, scr_=scr_, n_=n_: e.dma_start(
                    out=scr_[ei * n_:(ei + 1) * n_, :], in_=src_d[ei * n_:(ei + 1) * n_, :]),
                    reads=(), writes=(("wscrb", id(scr_), ei),), inc=16)
        emit_weight_load_T(p, wr2, scrw["wr2"], bufs["wrT"], "wr", D, D, cast=False)
        emit_weight_load_T(p, wk2b, scrw["wk2b"], bufs["wkT"], "wk", D, D, cast=False, cast_name="wk2")
        emit_weight_load_T(p, wv2b, scrw["wv2b"], bufs["wvT"], "wv", D, D, cast=False, cast_name="wv2")
        emit_block2(p, nc, bufs, {})
        phase_done()

    with contextlib.ExitStack() as ph:
        sb = lambda name, shape, dt: ph.enter_context(nc.sbuf_tensor(name, shape, dt))
        ps = lambda name, shape, dt: ph.enter_context(nc.psum_tensor(name, shape, dt))
        bufs = {"X": X, "ident": ident, "stats": stats,
                "wqN": w0, "wkN": w1, "wvN": w2}
        bufs["b3_yfm"] = sb("b3_yfm", [128, 8, 512], BF16)
        bufs["qfm3"] = sb("qfm3", [128, 8, 512], BF16)
        bufs["kfm3"] = sb("kfm3", [128, 8, 512], BF16)
        bufs["sqf"] = sb("sqf", [128, 8, 512], BF16)
        bufs["skf"] = sb("skf", [128, 8, 512], BF16)
        bufs["vpad3"] = sb("vpad3", [128, 4, 8, 129], BF16)
        bufs["esb"] = sb("esb", [128, 4, 512], BF16)
        bufs["ablk"] = sb("ablk", [128, 4, D], BF16)
        bufs["Mz"] = sb("Mz", [128, 8, 129], F32)
        bufs["Mzbf"] = sb("Mzbf", [128, 8, 129], BF16)
        bufs["vdp"] = [sb(f"vdp{i}", [128, 129], BF16) for i in range(2)]
        bufs["sktm"] = [sb(f"sktm{i}", [128, 128], BF16) for i in range(2)]
        bufs["atmp"] = sb("atmp", [128, 2, 128], F32)
        bufs["es1"] = sb("es1", [128, 2, 512], F32)
        bufs["rq_t"] = sb("rq_t", [128, 4], F32)
        bufs["rd_t"] = sb("rd_t", [128, 4], F32)
        bufs["rk_t"] = sb("rk_t", [128, 4], F32)
        bufs["scr"] = sb("scr3", [128, D], BF16)
        bufs["lnf"] = sb("lnf3", [128, 2, D], BF16)
        bufs["ytm"] = [sb(f"ytm3{i}", [128, D], BF16) for i in range(2)]
        bufs["cmask"] = sb("cmask", [128, 4, 512], BF16)
        bufs["gbc"] = sb("gbc", [128, 8], F32)
        bufs["ombc"] = sb("ombc", [128, 8], F32)
        bufs["bqc"] = sb("bqc", [128, 8], F32)
        bufs["bkc"] = sb("bkc", [128, 8], F32)
        bufs["bvbc"] = sb("bvbc", [128, 8, 128], BF16)
        bufs["pj"] = [ps(f"pj3{i}", [128, 512], F32) for i in range(2)]
        bufs["pdot"] = [ps(f"pdot{i}", [128, 129], F32) for i in range(2)]
        bufs["pmem"] = [ps("pmem0", [128, 258], F32)] * 2
        bufs["pMz"] = ps("pMz", [128, 129], F32)
        bufs["psumT"] = [ps(f"psT3{i}", [128, 128], BF16) for i in range(2)]
        y4t = sb("y4t3", [128, 1, D], F32)
        y4view = y4loc[0:ROWS, :].rearrange("(t p) c -> t p c", p=128)
        stats_ = bufs["stats"]
        sum_a, ssq_a = stats_["sum_a"], stats_["ssq_a"]
        rstd_a, bias_a = stats_["rstd_a"], stats_["bias_a"]
        t0s, t1s = stats_["t0"], stats_["t1"]
        scr3 = bufs["scr"]

        def b3_tail(blk):
            for tt in range(4):
                tile = blk * 4 + tt
                emit_ln_stats(p, X[:, tile, :], tile, sum_a, ssq_a, tile, scr3, ("X", tile))
            emit_ln_finalize2(p, sum_a, ssq_a, rstd_a, bias_a, t0s, t1s,
                              (blk * 4, blk * 4 + 4), D)
            for tt in range(4):
                tile = blk * 4 + tt
                y4v = y4t[:, 0, :]
                emit_ln_norm(p, X[:, tile, :], (("X", tile),), y4v, (("y4t", 0),),
                             rstd_a, bias_a, tile, tag=f"y4 {tile}")
                p.op("pool", lambda e, tile=tile, y4v=y4v: e.dma_start(out=y4view[tile, :, :], in_=y4v),
                     reads=(("y4t", 0),), writes=(("y4w", tile),), inc=16)
                p.op("act", lambda e, tile=tile: e.dma_start(
                    out=x3loc[tile * 128:(tile + 1) * 128, :], in_=X[:, tile, :]),
                     reads=(("X", tile),), writes=(("x3w", tile),), inc=16)

        bufs["tail_hook"] = b3_tail
        emit_b3_setup(p, nc, bufs, bq3, bk3, bv3, beta3)
        emit_weight_load_nat(p, wq3, bufs["wqN"], "wq", D, D)
        emit_weight_load_nat(p, wk3, bufs["wkN"], "wk", D, D)
        emit_weight_load_nat(p, wv3, bufs["wvN"], "wv", D, D)
        emit_block3(p, nc, bufs)
        p.op("dve", lambda e: e.memset(y4t[:, 0, :], 0.0),
             reads=(("y4t", 0),), writes=(("y4t", 0),))
        p.op("sp", lambda e: e.dma_start(out=y4loc[ROWS:ROWS + 128, :], in_=y4t[:, 0, :]),
             reads=(("y4t", 0),), writes=(("y4guard",),), inc=16)
        ibs = sb("ibs5", [128, 16], I32)
        its = sb("its5", [128, 16], I32)
        flat = flatperm
        flatf = sb("flatf5", [128, 16], F32)
        own = sb("own5", [128, 16], F32)
        tmpi = sb("tmpi5", [128, 16], F32)
        basebc = sb("basebc5", [128, 1], F32)
        for k in range(16):
            p.op("sp", lambda e, k=k: e.dma_start(out=ibs[:, k], in_=ib_d[k * 128:(k + 1) * 128]),
                 writes=(("idx5",),), inc=16)
            p.op("sp", lambda e, k=k: e.dma_start(out=its[:, k], in_=it_d[k * 128:(k + 1) * 128]),
                 writes=(("idx5",),), inc=16)
        p.op("sp", lambda e: e.dma_start(out=basebc[:, :], in_=base_d[None, :].broadcast_to([128, 1])),
             writes=(("base5",),), inc=16)
        p.op("dve", lambda e: e.tensor_scalar(flat[:], ibs[:], T, None, ALU.mult),
             reads=(("idx5",),), writes=(("flat5",),))
        p.op("dve", lambda e: e.tensor_tensor(flat[:], flat[:], its[:], ALU.add),
             reads=(("flat5",), ("idx5",)), writes=(("flat5",),))
        p.op("dve", lambda e: e.tensor_copy(flatf[:], flat[:]),
             reads=(("flat5",),), writes=(("flatf5",),))
        p.op("dve", lambda e: e.tensor_scalar(flatf[:], flatf[:], basebc[:, 0:1], None, ALU.subtract),
             reads=(("flatf5",), ("base5",)), writes=(("flatf5",),))
        p.op("dve", lambda e: e.tensor_scalar(own[:], flatf[:], 0.0, None, ALU.is_ge),
             reads=(("flatf5",),), writes=(("own5",),))
        p.op("dve", lambda e: e.tensor_scalar(tmpi[:], flatf[:], float(ROWS), None, ALU.is_lt),
             reads=(("flatf5",),), writes=(("tmpi5",),))
        p.op("dve", lambda e: e.tensor_tensor(own[:], own[:], tmpi[:], ALU.mult),
             reads=(("own5",), ("tmpi5",)), writes=(("own5",),))
        p.op("dve", lambda e: e.tensor_tensor(flatf[:], flatf[:], own[:], ALU.mult),
             reads=(("flatf5",), ("own5",)), writes=(("flatf5",),))
        p.op("dve", lambda e: e.tensor_scalar(tmpi[:], own[:], -float(ROWS), float(ROWS), ALU.mult, ALU.add),
             reads=(("own5",),), writes=(("tmpi5",),))
        p.op("dve", lambda e: e.tensor_tensor(flatf[:], flatf[:], tmpi[:], ALU.add),
             reads=(("flatf5",), ("tmpi5",)), writes=(("flatf5",),))
        p.op("dve", lambda e: e.tensor_copy(flat[:], flatf[:]),
             reads=(("flatf5",),), writes=(("flat5",),))
        phase_done()

    with contextlib.ExitStack() as ph:
        sb = lambda name, shape, dt: ph.enter_context(nc.sbuf_tensor(name, shape, dt))
        ps = lambda name, shape, dt: ph.enter_context(nc.psum_tensor(name, shape, dt))
        bufs = {"X": X, "ident": ident, "stats": stats,
                "sw1T": w0[:, :, 0:512], "sw2T": w0[:, :, 512:1024],
                "sw3T": w1[:, 0:4, :]}
        bufs["b4_yfm"] = sb("b4_yfm", [128, 8, 512], BF16)
        bufs["shfm"] = sb("shfm", [128, 4, 4, 128], BF16)
        bufs["sc1"] = sb("sc1", [128, 4, 512], F32)
        bufs["sc2"] = sb("sc2", [128, 4, 512], F32)
        bufs["gbuf"] = sb("gbuf", [128, 4, 512], F32)
        bufs["shbf"] = sb("shbf", [128, 4, 512], BF16)
        bufs["ftile"] = sb("ftile4", [128, 4, D], F32)
        bufs["y4t"] = sb("y4t", [128, 2, D], F32)
        bufs["scr"] = sb("scr4", [128, 2, D], BF16)
        bufs["ytm"] = [sb(f"ytm4{i}", [128, D], BF16) for i in range(2)]
        bufs["y4bf"] = y4bf
        bufs["y4flat"] = y4loc
        bufs["sb1bc"] = sb("sb1bc", [128, 512], F32)
        bufs["sb2bc"] = sb("sb2bc", [128, 512], F32)
        bufs["sb3bc"] = sb("sb3bc", [128, D], F32)
        bufs["slgbc"] = sb("slgbc", [128, 512], F32)
        bufs["slbbc"] = sb("slbbc", [128, 512], F32)
        bufs["pj"] = [ps(f"pj4{i}", [128, 512], F32) for i in range(6)]
        bufs["psumT"] = [ps(f"psT4{i}", [128, 128], BF16) for i in range(2)]
        for bc, dd in ((bufs["sb1bc"], sb1_d), (bufs["sb2bc"], sb2_d), (bufs["sb3bc"], sb3_d),
                       (bufs["slgbc"], slg_d), (bufs["slbbc"], slb_d)):
            p.op("pool", lambda e, bc=bc, dd=dd: e.dma_start(
                out=bc[:, :], in_=dd[None, :].broadcast_to([128, dd.shape[0]])),
                writes=(("b4c",),), inc=16)
        wtmps = [bufs["ytm"][0], bufs["ytm"][1]]
        wtkeys = [("ytm", 0), ("ytm", 1)]
        emit_weight_load_T(p, sw1_d, scrw["sw1"], bufs["sw1T"], "sw1", 512, D, cast=False)
        emit_weight_load_T(p, sw2_d, scrw["sw2"], bufs["sw2T"], "sw2", 512, D, cast=False)
        emit_weight_load_T(p, sw3_d, scrw["sw3"], bufs["sw3T"], "sw3", D, 512, cast=False)
        fgz = [sb(f"fgz{i}", [128, D], F32) for i in range(2)]
        y4keys = tuple(("y4w", t) for t in range(NTILE)) + (("y4guard",),)
        for k in range(16):
            sl_ = k % 2
            fz = fgz[sl_]
            p.op("pool",
                 lambda e, k=k, fz=fz: e.indirect_dma_start(
                     out=fz[:, :], out_offset=None,
                     in_=y4loc[:, :], in_offset=IndirectOffsetOnAxis(flatperm[:, k:k + 1], 0)),
                 reads=(("flat5",), ("fgz", sl_)) + y4keys, writes=(("fgz", sl_),), inc=16)
            p.op("sp", lambda e, k=k, fz=fz: e.dma_start(out=cc_in[k * 128:(k + 1) * 128, :], in_=fz[:, :]),
                 reads=(("fgz", sl_),), writes=(("ccin", k),), inc=16)
        p.op("pool", lambda e: e.collective_compute(
            "ReduceScatter", ALU.add, RG, [cc_in.opt()], [cc_rs.opt()]),
            reads=tuple(("ccin", k) for k in range(16)), writes=(("ccrs",),), cc=True)
        for j in range(8):
            p.op("sp", lambda e, j=j: e.dma_start_transpose(
                w2[:, j, 0:512],
                ew1scr[0:512, j * 128:(j + 1) * 128]),
                reads=(("wscrb", id(ew1scr), 0),), writes=(("ew1T", 0),), inc=16)
        bufs["split_y4"] = True
        bufs["skip_y4_pass"] = True
        emit_block4_shared(p, nc, bufs,
                           y4loc[0:ROWS, :].rearrange("(t p) c -> t p c", p=128),
                           floc[0:ROWS, :].rearrange("(t p) c -> t p c", p=128))
        phase_done()

    with contextlib.ExitStack() as ph:
        sb = lambda name, shape, dt: ph.enter_context(nc.sbuf_tensor(name, shape, dt))
        ps = lambda name, shape, dt: ph.enter_context(nc.psum_tensor(name, shape, dt))
        ftile6 = sb("ftile6", [128, 4, D], F32)
        scr6 = sb("scr6", [128, 1, D], BF16)
        lnf6 = sb("lnf6", [128, 2, D], BF16)
        sum_b6 = sb("sum_p", [128, NTILE, 2], F32)
        ssq_b6 = None
        rstd_b6 = sb("rstd_p", [128, NTILE], F32)
        bias_b6 = sb("bias_p", [128, NTILE], F32)
        t06 = sb("t0p", [128, NTILE], F32)
        t16 = sb("t1p", [128, NTILE], F32)
        flview = floc[0:ROWS, :].rearrange("(t p) c -> t p c", p=128)

        def emit_pass_pair(pair):
            for pp in range(2):
                tile = pair * 2 + pp
                sl = tile % 4
                ftv = ftile6[:, sl, :]
                p.op("act", lambda e, tile=tile, ftv=ftv: e.dma_start(
                    out=ftv, in_=flview[tile, :, :]),
                     reads=(("ftile6", sl), ("fw", tile)), writes=(("ftile6", sl),), inc=16)
                emit_ln_stats(p, ftv, tile, sum_b6, None, tile, None, (("ftile6", sl),))
            emit_ln_finalize2(p, sum_b6, ssq_b6, rstd_b6, bias_b6, t06, t16,
                              (pair * 2, pair * 2 + 2), D)
            for pp in range(2):
                tile = pair * 2 + pp
                sl = tile % 4
                ftv = ftile6[:, sl, :]
                emit_ln_norm(p, ftv, (("ftile6", sl),), lnf6[:, sl % 2, :], (("lnf6", sl % 2),),
                             rstd_b6, bias_b6, tile)
                p.op("pool",
                     lambda e, tile=tile, sl=sl: e.tensor_tensor(
                         X[:, tile, :], X[:, tile, :], lnf6[:, sl % 2, :], ALU.add),
                     reads=(("lnf6", sl % 2), ("X", tile), ("x3w", tile)), writes=(("X", tile),))
                p.op("pool", lambda e, tile=tile: e.dma_start(out=out_view[tile, :, :], in_=X[:, tile, :]),
                     reads=(("X", tile),), writes=(("outw", tile),), inc=16)

        def pass_hook(ei):
            emit_pass_pair(2 * ei)
            emit_pass_pair(2 * ei + 1)
        B = {}
        B["ibs"] = B["its"] = B["flat"] = None
        B["xm"] = sb("xm", [128, 2, D], F32)
        B["xmfm"] = sb("xmfm", [128, 8, 256], F32)
        B["xmbf"] = sb("xmbf", [128, 8, 256], BF16)
        B["gsb"] = sb("gsb", [128, D], F32)
        B["GT"] = sb("GT", [128, 8, 9], F32)
        B["bias9"] = sb("bias9", [128, 9], F32)
        B["expbc"] = sb("expbc", [128, 8], F32)
        B["sig"] = sb("sig", [128, 2, 9], F32)
        B["tmp8"] = sb("tmp8", [128, 8], F32)
        B["mx1"] = sb("mx1", [128, 1], F32)
        B["oh"] = sb("oh", [128, 2, 8], F32)
        B["ohT"] = sb("ohT", [8, 2, 128], BF16)
        B["t1sc"] = sb("t1sc", [128, 1], F32)
        B["wsh"] = sb("wsh_sb", [128, 2], F32)
        B["wex"] = sb("wex", [128, 2], F32)
        B["identf"] = identf
        B["identb"] = ident
        B["ew1T"] = [w2[:, :, 0:512], w2[:, :, 512:1024]]
        B["ew2T"] = [w0[:, :, 0:512], w0[:, :, 512:1024]]
        B["ew3T"] = [w1[:, 0:4, :], w1[:, 4:8, :]]
        B["eb1bc"] = sb("eb1bc", [128, 8, 512], BF16)
        B["eb2bc"] = sb("eb2bc", [128, 8, 512], BF16)
        B["eb3sb"] = sb("eb3sb", [8, D], BF16)
        B["selgbc"] = sb("selgbc", [128, 512], F32)
        B["selbbc"] = sb("selbbc", [128, 512], F32)
        B["sc1"] = sb("sc15", [128, 2, 512], F32)
        B["sc2"] = sb("sc25", [128, 2, 512], F32)
        B["gbuf"] = sb("gbuf5", [128, 2, 512], F32)
        B["zm"] = sb("zm", [128, 2, 512], BF16)
        B["zmfm"] = sb("zmfm", [128, 2, 4, 128], BF16)
        B["epart"] = sb("epart_sb", [128, D], F32)
        B["scrd"] = sb("scrd", [128, 1, 512], BF16)
        B["sum_b"] = stats["sum_b"]; B["ssq_b"] = stats["ssq_b"]
        B["rstd_b"] = stats["rstd_b"]; B["bias_b"] = stats["bias_b"]
        B["t0s"] = stats["t0"]; B["t1s"] = stats["t1"]
        B["pT"] = [ps("pT5a", [128, 128], F32), ps("pT5b", [128, 128], BF16)]
        B["ph"] = [ps(f"ph5{i}", [128, 512], F32) for i in range(2)]
        B["pout"] = [ps(f"pout5{i}", [128, 512], F32) for i in range(4)]
        B["wtmpb"] = sb("wtmpb5", [128, D], BF16)
        dr = {"ccrs": cc_rs, "sgw_d": sgw_d, "sgb_d": sgb_d, "egw_d": egw_d,
              "egb_d": egb_d, "expb_d": expb_d, "ew1_d": ew1_d, "eb1_d": eb1_d,
              "ew2_d": ew2_d, "eb2_d": eb2_d, "ew3_d": ew3_d, "eb3_d": eb3_d,
              "selg_d": selg_d, "selb_d": selb_d,
              "ep_out": agp_in.rearrange("(a p) c -> a p c", p=128),
              "wsh_out": agw_in.rearrange("(a p) x -> a (p x)", p=128),
              "ew1scr": ew1scr, "ew2scr": ew2scr, "ew3scr": ew3scr}
        emit_block4b(p, nc, B, dr, fused=True, hook=pass_hook)
        p.op("pool", lambda e: e.collective_compute(
            "AllGather", ALU.bypass, RG, [agp_in.opt()], [agp_out.opt()]),
            reads=(("epart", 0), ("epart", 1)), writes=(("agp",),), cc=True)
        p.op("pool", lambda e: e.collective_compute(
            "AllGather", ALU.bypass, RG, [agw_in.opt()], [agw_out.opt()]),
            reads=(("wsh",),), writes=(("agw",),), cc=True)
        phase_done()

    with contextlib.ExitStack() as ph:
        sb = lambda name, shape, dt: ph.enter_context(nc.sbuf_tensor(name, shape, dt))
        wshs = sb("wshs", [128, 16], F32)
        fg = sb("fg", [128, 2, D], F32)
        ep = sb("ep", [128, 2, D], F32)
        comb = sb("comb", [128, 2, D], F32)
        xg = sb("xg", [128, 2, D], BF16)
        orow = sb("orow", [128, 2, D], F32)
        scrm = sb("scrm", [128, 2, D], BF16)
        sum_m, ssq_m = stats["sum_a"], stats["ssq_a"]
        rstd_m, bias_m = stats["rstd_a"], stats["bias_a"]
        t0m, t1m = stats["t0"], stats["t1"]
        for k in range(16):
            p.op("sp", lambda e, k=k: e.dma_start(out=wshs[:, k], in_=agw_out[k * 128:(k + 1) * 128, 0]),
                 reads=(("agw",),), writes=(("wshs",),), inc=16)
        outk = tuple(("outw", t) for t in range(NTILE))
        x3k = tuple(("x3w", t) for t in range(NTILE))
        for k in range(16):
            sl = k % 2
            fgv, epv, cbv, xgv, orv = fg[:, sl, :], ep[:, sl, :], comb[:, sl, :], xg[:, sl, :], orow[:, sl, :]
            p.op("pool",
                 lambda e, k=k, fgv=fgv: e.indirect_dma_start(
                     out=fgv, out_offset=None,
                     in_=floc[:, :], in_offset=IndirectOffsetOnAxis(flatperm[:, k:k + 1], 0)),
                 reads=(("flat5",), ("fg", sl)), writes=(("fg", sl),), inc=16)
            p.op("pool",
                 lambda e, k=k, xgv=xgv: e.indirect_dma_start(
                     out=xgv, out_offset=None,
                     in_=x3loc[:, :], in_offset=IndirectOffsetOnAxis(flatperm[:, k:k + 1], 0)),
                 reads=(("flat5",), ("xg", sl)) + x3k, writes=(("xg", sl),), inc=16)
            p.op("sp", lambda e, k=k, epv=epv: e.dma_start(out=epv, in_=agp_out[k * 128:(k + 1) * 128, :]),
                 reads=(("ep", sl), ("agp",)), writes=(("ep", sl),), inc=16)
            p.op("dve",
                 lambda e, k=k, fgv=fgv, epv=epv, cbv=cbv: e.scalar_tensor_tensor(
                     cbv, fgv, wshs[:, k:k + 1], epv, ALU.mult, ALU.add),
                 reads=(("fg", sl), ("ep", sl), ("wshs",), ("comb", sl)), writes=(("comb", sl),))
            emit_ln_stats(p, cbv, k, sum_m, None, k, None, (("comb", sl),))
            emit_ln_finalize2(p, sum_m, ssq_m, rstd_m, bias_m, t0m, t1m, (k, k + 1), D)
            emit_ln_norm(p, cbv, (("comb", sl),), orv, (("orow", sl),), rstd_m, bias_m, k)
            p.op("dve",
                 lambda e, orv=orv, xgv=xgv: e.tensor_tensor(orv, orv, xgv, ALU.add),
                 reads=(("orow", sl), ("xg", sl)), writes=(("orow", sl),))
            p.op("pool",
                 lambda e, k=k, orv=orv: e.indirect_dma_start(
                     out=out_d[:, :], out_offset=IndirectOffsetOnAxis(flatperm[:, k:k + 1], 0),
                     in_=orv, in_offset=None),
                 reads=(("flat5",), ("orow", sl)) + outk, writes=(("outs", k),), inc=16)
        phase_done()

    p.finish(block)
    return nc, outer


_PROG_CACHE = {}


def _get_prog(name, builder):
    if name not in _PROG_CACHE:
        _PROG_CACHE[name] = builder()
    return _PROG_CACHE[name][0]


USE_FUSED = True


def kernel(**inputs):
    if USE_FUSED:
        return kernel_fused(**inputs)
    return kernel_unfused(**inputs)


def kernel_fused(**inputs):
    ncores = 8
    shards = _shard_rows(np.asarray(inputs["L0_attn_query"], np.float32))
    w = {k: np.ascontiguousarray(np.asarray(inputs[k], np.float32))
         for k in inputs if k != "L0_attn_query" and "index" not in k}
    idx_b = np.ascontiguousarray(np.asarray(inputs["L1_ffn_index_b"], np.int32))
    idx_t = np.ascontiguousarray(np.asarray(inputs["L1_ffn_index_t"], np.int32))
    nc = _get_prog("fused", build_fused_program)
    common = {
        "wq1": w["L0_attn_Wq"], "wk1": w["L0_attn_Wk"], "wv1": w["L0_attn_Wv"],
        "wo1": w["L0_attn_Wo"], "bo1": w["L0_attn_bo"],
        "wr2": w["L0_ffn_W_r"], "wk2b": w["L0_ffn_W_k"], "wv2b": w["L0_ffn_W_v"],
        "mur2": w["L0_ffn_mu_r"], "muk2": w["L0_ffn_mu_k"],
        "wq3": w["L1_attn_Wq"], "bq3": w["L1_attn_bq"], "wk3": w["L1_attn_Wk"],
        "bk3": w["L1_attn_bk"], "wv3": w["L1_attn_Wv"], "bv3": w["L1_attn_bv"],
        "beta3": w["L1_attn_beta"],
        "sw1": w["L1_ffn_sw1"], "sb1": w["L1_ffn_sb1"],
        "sw2": w["L1_ffn_sw2"], "sb2": w["L1_ffn_sb2"],
        "sw3": w["L1_ffn_sw3"], "sb3": w["L1_ffn_sb3"],
        "slg": w["L1_ffn_s_ln_g"], "slb": w["L1_ffn_s_ln_b"],
        "sgw": w["L1_ffn_sg_w"], "sgb": w["L1_ffn_sg_b"],
        "egw": w["L1_ffn_eg_w"], "egb": w["L1_ffn_eg_b"],
        "expb": w["L1_ffn_exp_bias"],
        "ew1": w["L1_ffn_ew1"].reshape(8 * 512, D), "eb1": w["L1_ffn_eb1"],
        "ew2": w["L1_ffn_ew2"].reshape(8 * 512, D), "eb2": w["L1_ffn_eb2"],
        "ew3": w["L1_ffn_ew3"].reshape(8 * D, 512), "eb3": w["L1_ffn_eb3"],
        "selg": w["L1_ffn_sel_ln_g"], "selb": w["L1_ffn_sel_ln_b"],
        "ib": idx_b, "it": idx_t,
    }
    in_maps = [dict(common, x=shards[c], basef=np.array([ROWS * c], np.float32))
               for c in range(ncores)]
    res = run_bass_kernel_spmd(nc, in_maps, core_ids=list(range(ncores)))
    out = np.concatenate([np.asarray(r["out"])[:ROWS].reshape(B_LOC, T, D)
                          for r in res.results], axis=0)
    return out


def kernel_unfused(**inputs):
    ncores = 8
    cores = list(range(ncores))

    def run(name, builder, in_maps):
        nc = _get_prog(name, builder)
        return run_bass_kernel_spmd(nc, in_maps, core_ids=cores).results

    shards = _shard_rows(np.asarray(inputs["L0_attn_query"], np.float32))
    w = {k: np.ascontiguousarray(np.asarray(inputs[k], np.float32))
         for k in inputs if k != "L0_attn_query" and "index" not in k}
    idx_b = np.ascontiguousarray(np.asarray(inputs["L1_ffn_index_b"], np.int32))
    idx_t = np.ascontiguousarray(np.asarray(inputs["L1_ffn_index_t"], np.int32))

    r1 = run("b1", build_b1_program,
             [{"x": s, "wq": w["L0_attn_Wq"], "wk": w["L0_attn_Wk"],
               "wv": w["L0_attn_Wv"], "wo": w["L0_attn_Wo"], "bo": w["L0_attn_bo"]}
              for s in shards])
    x1 = [np.asarray(r["out"]) for r in r1]
    r2 = run("b2", build_b2_program,
             [{"x": x1[c], "wr": w["L0_ffn_W_r"], "wk": w["L0_ffn_W_k"],
               "wv": w["L0_ffn_W_v"], "mur": w["L0_ffn_mu_r"], "muk": w["L0_ffn_mu_k"]}
              for c in cores])
    x2 = [np.asarray(r["out"]) for r in r2]
    r3 = run("b3", build_b3_program,
             [{"x": x2[c], "wq": w["L1_attn_Wq"], "bq": w["L1_attn_bq"],
               "wk": w["L1_attn_Wk"], "bk": w["L1_attn_bk"], "wv": w["L1_attn_Wv"],
               "bv": w["L1_attn_bv"], "beta": w["L1_attn_beta"]} for c in cores])
    x3 = [np.asarray(r["out"]) for r in r3]
    r4a = run("b4a", build_b4a_program,
              [{"x": x3[c], "sw1": w["L1_ffn_sw1"], "sb1": w["L1_ffn_sb1"],
                "sw2": w["L1_ffn_sw2"], "sb2": w["L1_ffn_sb2"],
                "sw3": w["L1_ffn_sw3"], "sb3": w["L1_ffn_sb3"],
                "slg": w["L1_ffn_s_ln_g"], "slb": w["L1_ffn_s_ln_b"]} for c in cores])
    y4full = np.ascontiguousarray(
        np.concatenate([np.asarray(r["y4"]).reshape(ROWS, D) for r in r4a], axis=0))
    f_slices = [np.asarray(r["f"]).reshape(ROWS, D) for r in r4a]
    MSL = 2048 // ncores
    ew1 = w["L1_ffn_ew1"].reshape(8 * 512, D)
    ew2 = w["L1_ffn_ew2"].reshape(8 * 512, D)
    ew3 = w["L1_ffn_ew3"].reshape(8 * D, 512)
    r4b = run("b4b", build_b4b_program,
              [{"y4full": y4full, "ib": idx_b[c * MSL:(c + 1) * MSL],
                "it": idx_t[c * MSL:(c + 1) * MSL],
                "sgw": w["L1_ffn_sg_w"], "sgb": w["L1_ffn_sg_b"],
                "egw": w["L1_ffn_eg_w"], "egb": w["L1_ffn_eg_b"],
                "expb": w["L1_ffn_exp_bias"],
                "ew1": ew1, "eb1": w["L1_ffn_eb1"],
                "ew2": ew2, "eb2": w["L1_ffn_eb2"],
                "ew3": ew3, "eb3": w["L1_ffn_eb3"],
                "selg": w["L1_ffn_sel_ln_g"], "selb": w["L1_ffn_sel_ln_b"]}
               for c in cores])
    epart = np.ascontiguousarray(
        np.concatenate([np.asarray(r["epart"]).reshape(MSL, D) for r in r4b], axis=0))
    wshall = np.ascontiguousarray(
        np.concatenate([np.asarray(r["wsh"]).reshape(MSL) for r in r4b], axis=0))
    zrow = np.zeros((128, D), np.float32)
    r4c = run("b4c", build_b4c_program,
              [{"x": x3[c],
                "fpad": np.ascontiguousarray(np.concatenate([f_slices[c], zrow], 0)),
                "ib": idx_b, "it": idx_t, "epart": epart, "wshall": wshall,
                "base": np.array([ROWS * c], np.float32)} for c in cores])
    out = np.concatenate([np.asarray(r["out"]).reshape(B_LOC, T, D) for r in r4c], axis=0)
    return out
